# version 17
# baseline (speedup 1.0000x reference)
import math
from contextlib import ExitStack

import numpy as np
import ml_dtypes

import concourse.bass as bass
import concourse.mybir as mybir
from concourse.bass_utils import run_bass_kernel_spmd
from concourse.aot_env import allow_ici_all_to_all

F32 = mybir.dt.float32
BF16 = mybir.dt.bfloat16
I32 = mybir.dt.int32
ACT = mybir.ActivationFunctionType
ALU = mybir.AluOpType

N_CORES = 8
B, N, D = 4, 2048, 2048
HID = 5632
NKC = D // 128
NHC = HID // 128
TOK = 1024
NT = TOK // 128
EPS = 1e-5

ENGINES = ("sync", "gpsimd", "tensor", "vector", "scalar")


class Buf:
    __slots__ = ("name", "w", "readers")

    def __init__(self, name):
        self.name = name
        self.w = None
        self.readers = []


class Tracker:
    def __init__(self, nc, es):
        self.nc = nc
        self.es = es
        self.ops = {e: [] for e in ENGINES}
        self.sems = {}
        self.count = {}
        self.waited = {e: {} for e in ENGINES}
        self.eng_of_sem = {}
        for e in ENGINES:
            self._mk("p_" + e, e)

    def _mk(self, name, eng=None):
        if name not in self.sems:
            self.sems[name] = self.es.enter_context(self.nc.semaphore(name))
            self.count[name] = 0
            self.eng_of_sem[name] = eng
        return name

    def wait(self, eng, tag):
        if tag is None:
            return
        sem, val = tag
        if val <= 0:
            return
        if self.eng_of_sem.get(sem) == eng and eng == "tensor":
            return
        if self.waited[eng].get(sem, 0) >= val:
            return
        self.waited[eng][sem] = val
        h = self.sems[sem]
        self.ops[eng].append(lambda e, h=h, v=val: e.wait_ge(h, v))

    def _pre(self, eng, reads, writes):
        for b in reads:
            self.wait(eng, b.w)
        for b in writes:
            self.wait(eng, b.w)
            for t in b.readers:
                self.wait(eng, t)

    def op(self, eng, fns, reads=(), writes=(), self_sync=False):
        if not isinstance(fns, (list, tuple)):
            fns = [fns]
        self._pre(eng, reads, writes)
        sem = "p_" + eng
        self.count[sem] += 1
        v = self.count[sem]
        h = self.sems[sem]
        last = fns[-1]
        fns = list(fns[:-1]) + [lambda e, f=last, h=h: f(e).then_inc(h, 1)]
        t = (sem, v)
        for f in fns:
            self.ops[eng].append(f)
        for b in writes:
            b.w = t
            b.readers = []
        for b in reads:
            b.readers.append(t)
        return t

    def dma(self, eng, slot, fn, reads=(), writes=()):
        sem = self._mk("d_" + slot)
        self._pre(eng, reads, writes)
        self.count[sem] += 16
        v = self.count[sem]
        h = self.sems[sem]
        self.ops[eng].append(lambda e, f=fn, h=h: f(e).then_inc(h, 16))
        t = (sem, v)
        for b in writes:
            b.w = t
            b.readers = []
        for b in reads:
            b.readers.append(t)
        return t

    def barrier(self, bufs):
        tags = []
        for b in bufs:
            if b.w:
                tags.append(b.w)
            tags.extend(b.readers)
        for e in ENGINES:
            for t in tags:
                self.wait(e, t)


def _slopes16():
    start = 2.0 ** (-(2.0 ** (-(math.log2(16) - 3))))
    return np.array([start ** (i + 1) for i in range(16)], dtype=np.float64)


def _hi_lo(x):
    x = np.asarray(x, dtype=np.float32)
    hi = x.astype(ml_dtypes.bfloat16)
    lo = (x - hi.astype(np.float32)).astype(ml_dtypes.bfloat16)
    return np.ascontiguousarray(hi), np.ascontiguousarray(lo)


def build_nc():
    nc = bass.Bass("TRN2", target_bir_lowering=False)

    dt_in = lambda n, s, d: nc.dram_tensor(n, s, d, kind="ExternalInput").ap()
    x_my = dt_in("x_my", [TOK, D], F32)
    wqkvT = dt_in("wqkvT", [D, 768], F32)
    bqk = dt_in("bqk", [128, 4], F32)
    bv_h = dt_in("bv_h", [1, 256], BF16)
    bv_l = dt_in("bv_l", [1, 256], BF16)
    woutT = dt_in("woutT", [D, D], F32)
    bout_h = dt_in("bout_h", [1, D], BF16)
    bout_l = dt_in("bout_l", [1, D], BF16)
    wgT_t = dt_in("wgT_t", [NHC, 128, NKC, 128], I32)
    wuT_t = dt_in("wuT_t", [NHC, 128, NKC, 128], I32)
    wdT = dt_in("wdT", [HID, D], I32)
    sg_pp = dt_in("sg_pp", [128, NHC], F32)
    su_pp = dt_in("su_pp", [128, NHC], F32)
    sdb = dt_in("sdb", [128, D], F32)
    attc = dt_in("attc", [128, 6 * 128], F32)
    lamc = dt_in("lamc", [128, 2], F32)
    dk_pp = dt_in("dk_pp", [128, 2], F32)
    identf = dt_in("identf", [128, 128], F32)
    identb = dt_in("identb", [128, 128], BF16)
    ones_row = dt_in("ones_row", [1, 512], BF16)
    n128_row = dt_in("n128_row", [1, 512], BF16)
    ones_colb = dt_in("ones_colb", [128, 1], BF16)
    n128_col = dt_in("n128_col", [128, 1], F32)
    out_my = nc.dram_tensor("out_my", [TOK, D], F32, kind="ExternalOutput").ap()

    ag_in = [nc.dram_tensor(f"ag_in{i}", [128, 2, NKC, 256], BF16).ap() for i in range(4)]
    ag_out = [nc.dram_tensor(f"ag_out{i}", [8, 128, 2, NKC, 256], BF16, addr_space="Shared").ap() for i in range(4)]
    cc_in = [nc.dram_tensor(f"cc_in{i}", [8, 128, 2, 256], F32).ap() for i in range(4)]
    cc_out = [nc.dram_tensor(f"cc_out{i}", [8, 128, 2, 256], F32).ap() for i in range(4)]
    y_scr = nc.dram_tensor("y_scr", [TOK, D], F32).ap()
    x2_scr = nc.dram_tensor("x2_scr", [TOK, D], F32).ap()
    x2fm_scr = nc.dram_tensor("x2fm_scr", [2, 128, NKC, TOK], BF16).ap()

    es = ExitStack()
    T = Tracker(nc, es)

    def sb(name, shape, dtype, stack):
        t = stack.enter_context(nc.sbuf_tensor(name, shape, dtype))
        return t, Buf(name)

    def psum(name, shape, dtype, stack):
        t = stack.enter_context(nc.psum_tensor(name, shape, dtype))
        return t, Buf(name)

    c_attc, b_attc = sb("c_attc", [128, 6 * 128], F32, es)
    c_lamc, b_lamc = sb("c_lamc", [128, 2], F32, es)
    c_dk, b_dk = sb("c_dk", [128, 2], F32, es)
    c_bqk, b_bqk = sb("c_bqk", [128, 4], F32, es)
    c_bvh, b_bvh = sb("c_bvh", [1, 256], BF16, es)
    c_bvl, b_bvl = sb("c_bvl", [1, 256], BF16, es)
    c_bouth, b_bouth = sb("c_bouth", [1, D], BF16, es)
    c_boutl, b_boutl = sb("c_boutl", [1, D], BF16, es)
    c_idf, b_idf = sb("c_idf", [128, 128], F32, es)
    c_idb, b_idb = sb("c_idb", [128, 128], BF16, es)
    c_ones, b_ones = sb("c_ones", [1, 512], BF16, es)
    c_n128c, b_n128c = sb("c_n128c", [128, 1], F32, es)
    c_sg, b_sg = sb("c_sg", [128, NHC], F32, es)
    c_su, b_su = sb("c_su", [128, NHC], F32, es)
    c_sdb, b_sdb = sb("c_sdb", [128, D], F32, es)

    const_list = [
        (c_idf, b_idf, identf), (c_idb, b_idb, identb),
        (c_attc, b_attc, attc), (c_lamc, b_lamc, lamc), (c_dk, b_dk, dk_pp), (c_bqk, b_bqk, bqk),
        (c_bvh, b_bvh, bv_h), (c_bvl, b_bvl, bv_l),
        (c_bouth, b_bouth, bout_h), (c_boutl, b_boutl, bout_l),
        (c_ones, b_ones, ones_row), (c_n128c, b_n128c, n128_col),
        (c_sg, b_sg, sg_pp), (c_su, b_su, su_pp), (c_sdb, b_sdb, sdb),
    ]
    for (t, bf, src) in const_list:
        T.dma("sync", "const", lambda e, t=t, s=src: e.dma_start(out=t[:], in_=s[:, :]), writes=(bf,))
    tot = ("d_const", T.count["d_const"])
    for (_, bf, _) in const_list:
        bf.w = tot

    wqsc = ExitStack()
    wqh, b_wqh = sb("wqh", [128, NKC * 768], BF16, wqsc)
    wql, b_wql = sb("wql", [128, NKC * 768], BF16, wqsc)
    wqtsc = ExitStack()
    wqt = [sb(f"wqt{i}", [128, 768], F32, wqtsc) for i in range(NKC)]
    for kc in range(NKC):
        wtt, wtb = wqt[kc]
        T.dma("sync", f"wqf{kc % 2}", lambda e, k=kc, d=wtt: e.dma_start(out=d[:], in_=wqkvT[k * 128:(k + 1) * 128, :]), writes=(wtb,))
    for kc in range(NKC):
        wqt[kc][1].w = ("d_wqf" + str(kc % 2), T.count["d_wqf" + str(kc % 2)])

    a0 = ExitStack()
    xf = [sb(f"xf{i}", [128, D], F32, a0) for i in range(2)]
    stgh = [sb(f"stgh{i}", [128, 512], BF16, a0) for i in range(4)]
    stgl = [sb(f"stgl{i}", [128, 512], BF16, a0) for i in range(4)]
    ps_tf = [psum(f"ps_tf{i}", [128, 512], F32, a0) for i in range(4)]
    b_agin = [Buf(f"ag_in{i}") for i in range(4)]
    b_agout = [Buf(f"ag_out{i}") for i in range(4)]
    ag_slot = [[Buf(f"agslot{q}_{i}") for i in range(8)] for q in range(4)]

    def emit_ag(hf):
        T.op("gpsimd", lambda e, h=hf: e.collective_compute(
            "AllGather", ALU.bypass, replica_groups=[list(range(N_CORES))],
            ins=[ag_in[h].tensor.ap().opt()], outs=[ag_out[h].tensor.ap().opt()]),
            reads=tuple(ag_slot[hf]), writes=(b_agout[hf],))

    for ti in range(NT):
        bi = ti % 2
        xft, xfb = xf[bi]
        T.dma("sync", f"xf{bi}", lambda e, t=ti, d=xft: e.dma_start(out=d[:], in_=x_my[t * 128:(t + 1) * 128, :]), writes=(xfb,))
        hf = ti // 2
        for q in range(4):
            sl = (ti * 4 + q) % 4
            pst, psb = ps_tf[sl]
            fns = []
            for k in range(4):
                kc = q * 4 + k
                fns.append(lambda e, s=xft, k2=kc, p=pst, kk=k:
                           e.transpose(p[:, kk * 128:(kk + 1) * 128], s[:, k2 * 128:(k2 + 1) * 128], c_idf[:]))
            T.op("tensor", fns, reads=(xfb, b_idf), writes=(psb,))
            ht, hb = stgh[sl]
            lt, lb = stgl[sl]
            T.op("scalar", lambda e, p=pst, d=ht: e.copy(out=d[:], in_=p[:]), reads=(psb,), writes=(hb,))
            T.op("vector", lambda e, p=pst, d1=ht, d2=lt: e.tensor_tensor(out=d2[:], in0=p[:], in1=d1[:], op=ALU.subtract),
                 reads=(psb, hb), writes=(lb,))
            T.dma("scalar", f"agwh{sl}", lambda e, s=ht, q2=q, t=ti % 2, f=hf:
                  e.dma_start(out=ag_in[f][:, 0, q2 * 4:(q2 + 1) * 4, t * 128:(t + 1) * 128],
                              in_=s[:].rearrange("p (k t) -> p k t", k=4)),
                  reads=(hb,), writes=(ag_slot[hf][sl],))
            T.dma("scalar", f"agwl{sl}", lambda e, s=lt, q2=q, t=ti % 2, f=hf:
                  e.dma_start(out=ag_in[f][:, 1, q2 * 4:(q2 + 1) * 4, t * 128:(t + 1) * 128],
                              in_=s[:].rearrange("p (k t) -> p k t", k=4)),
                  reads=(lb,), writes=(ag_slot[hf][4 + sl],))
        if ti % 2 == 1:
            emit_ag(ti // 2)

    T.barrier([x[1] for x in xf + stgh + stgl + ps_tf])
    a0.close()

    for kc in range(NKC):
        wtt, wtb = wqt[kc]
        T.op("scalar", lambda e, k=kc, s=wtt: e.copy(out=wqh[:, k * 768:(k + 1) * 768], in_=s[:]),
             reads=(wtb,), writes=(b_wqh,))
        T.op("vector", lambda e, k=kc, s=wtt: e.tensor_tensor(out=wql[:, k * 768:(k + 1) * 768], in0=s[:],
                                                              in1=wqh[:, k * 768:(k + 1) * 768], op=ALU.subtract),
             reads=(wtb, b_wqh), writes=(b_wql,))
    T.barrier([x[1] for x in wqt])
    wqtsc.close()
    am = ExitStack()

    xg = [sb(f"xg{i}", [128, 2 * NKC * 512], BF16, am) for i in range(2)]
    qkv_q, b_q = sb("qkv_q", [128, 2048], BF16, am)
    qkv_k, b_k = sb("qkv_k", [128, 2048], BF16, am)
    qkv_v, b_v = sb("qkv_v", [128, 2048], BF16, am)
    qk32 = [sb(f"qk32_{i}", [128, 512], F32, am) for i in range(2)]
    sbS, b_S = sb("sbS", [128, 256], F32, am)
    sbSh, b_Sh = sb("sbSh", [128, 256], BF16, am)
    sbSl, b_Sl = sb("sbSl", [128, 256], BF16, am)
    smf = [sb(f"smf{i}", [128, 256], F32, am) for i in range(2)]
    smh = [sb(f"smh{i}", [128, 256], BF16, am) for i in range(2)]
    sml = [sb(f"sml{i}", [128, 256], BF16, am) for i in range(2)]
    ktf = [sb(f"ktf{i}", [128, 256], F32, am) for i in range(2)]
    ktg = [sb(f"ktg{i}", [128, 256], F32, am) for i in range(2)]
    ktdh = [sb(f"ktdh{i}", [128, 256], BF16, am) for i in range(2)]
    ktdl = [sb(f"ktdl{i}", [128, 256], BF16, am) for i in range(2)]
    qdf = [sb(f"qdf{i}", [128, 256], F32, am) for i in range(2)]
    qdh = [sb(f"qdh{i}", [128, 256], BF16, am) for i in range(2)]
    qdl = [sb(f"qdl{i}", [128, 256], BF16, am) for i in range(2)]
    astg = [sb(f"astg{i}", [128, 256], F32, am) for i in range(2)]
    ps = [psum(f"psA{i}", [128, 512], F32, am) for i in range(6)]
    pkt = [psum(f"pkt{i}", [128, 512], BF16, am) for i in range(2)]
    b_ccin = [Buf(f"cc_in{i}") for i in range(4)]

    DQ0, DK0, DM0 = 0, 256, 512

    def chunk_views(c):
        pp = c % 2
        def qs(h, lo):
            o = 1024 * lo + h * 512 + c * 128
            return qkv_q[:, o:o + 128]
        def ks(h, lo):
            o = 1024 * lo + h * 512 + c * 128
            return qkv_k[:, o:o + 128]
        def vs(h, lo):
            o = 1024 * lo + c * 256 + h * 128
            return qkv_v[:, o:o + 128]
        return pp, qs, ks, vs

    def emit_chunk_front(b, g, c):
        pp, qs, ks, vs = chunk_views(c)
        psS_t, psS_b = ps[pp]
        pkt_t, pkt_b = pkt[pp]
        ktft, ktfb = ktf[pp]
        qdft, qdfb = qdf[pp]
        qdht, qdhb = qdh[pp]
        qdlt, qdlb = qdl[pp]
        smft, smfb = smf[pp]
        smht, smhb = smh[pp]
        smlt, smlb = sml[pp]
        ktdht, ktdhb = ktdh[pp]
        ktdlt, ktdlb = ktdl[pp]

        fns = []
        for h in range(2):
            fns += [
                lambda e, h2=h: e.matmul(psS_t[:, h2 * 128:(h2 + 1) * 128], ks(h2, 0), qs(h2, 0), start=True, stop=False),
                lambda e, h2=h: e.matmul(psS_t[:, h2 * 128:(h2 + 1) * 128], ks(h2, 0), qs(h2, 1), start=False, stop=False),
                lambda e, h2=h: e.matmul(psS_t[:, h2 * 128:(h2 + 1) * 128], ks(h2, 1), qs(h2, 0), start=False, stop=True),
            ]
        T.op("tensor", fns, reads=(b_k, b_q), writes=(psS_b,))
        T.op("tensor", [
            lambda e: e.transpose(pkt_t[:, 0:128], ks(0, 0), c_idb[:]),
            lambda e: e.transpose(pkt_t[:, 128:256], ks(1, 0), c_idb[:]),
            lambda e: e.transpose(pkt_t[:, 256:384], ks(0, 1), c_idb[:]),
            lambda e: e.transpose(pkt_t[:, 384:512], ks(1, 1), c_idb[:]),
        ], reads=(b_k, b_idb), writes=(pkt_b,))

        T.op("vector", [
            lambda e: e.tensor_tensor(out=qdft[:, 0:128], in0=qs(0, 0), in1=qs(0, 1), op=ALU.add),
            lambda e: e.tensor_tensor(out=qdft[:, 128:256], in0=qs(1, 0), in1=qs(1, 1), op=ALU.add),
        ], reads=(b_q,), writes=(qdfb,))
        T.op("vector", lambda e: e.tensor_tensor(out=qdft[:], in0=qdft[:], in1=c_attc[:, DQ0:DQ0 + 256], op=ALU.mult),
             reads=(qdfb, b_attc), writes=(qdfb,))
        T.op("vector", lambda e: e.tensor_copy(out=qdht[:], in_=qdft[:]), reads=(qdfb,), writes=(qdhb,))
        T.op("vector", lambda e: e.tensor_tensor(out=qdlt[:], in0=qdft[:], in1=qdht[:], op=ALU.subtract),
             reads=(qdfb, qdhb), writes=(qdlb,))

        T.op("vector", lambda e: e.tensor_tensor(out=smft[:], in0=psS_t[:, 0:256], in1=c_attc[:, DM0:DM0 + 256], op=ALU.mult),
             reads=(psS_b, b_attc), writes=(smfb,))
        T.op("vector", lambda e: e.tensor_copy(out=smht[:], in_=smft[:]), reads=(smfb,), writes=(smhb,))
        T.op("vector", lambda e: e.tensor_tensor(out=smlt[:], in0=smft[:], in1=smht[:], op=ALU.subtract),
             reads=(smfb, smhb), writes=(smlb,))

        ktgt, ktgb = ktg[pp]
        T.op("vector", [
            lambda e: e.tensor_scalar_mul(ktft[:, 0:128], pkt_t[:, 0:128], c_dk[:, 0:1]),
            lambda e: e.tensor_scalar_mul(ktft[:, 128:256], pkt_t[:, 128:256], c_dk[:, 1:2]),
        ], reads=(pkt_b, b_dk), writes=(ktfb,))
        T.op("vector", [
            lambda e: e.scalar_tensor_tensor(out=ktgt[:, 0:128], in0=pkt_t[:, 256:384], scalar=c_dk[:, 0:1], in1=ktft[:, 0:128], op0=ALU.mult, op1=ALU.add),
            lambda e: e.scalar_tensor_tensor(out=ktgt[:, 128:256], in0=pkt_t[:, 384:512], scalar=c_dk[:, 1:2], in1=ktft[:, 128:256], op0=ALU.mult, op1=ALU.add),
        ], reads=(pkt_b, b_dk, ktfb), writes=(ktgb,))
        T.op("vector", lambda e: e.tensor_copy(out=ktdht[:], in_=ktgt[:]), reads=(ktgb,), writes=(ktdhb,))
        T.op("vector", lambda e: e.tensor_tensor(out=ktdlt[:], in0=ktgt[:], in1=ktdht[:], op=ALU.subtract),
             reads=(ktgb, ktdhb), writes=(ktdlb,))

    def emit_chunk_back(b, g, c):
        gc = g * 4 + c
        pp, qs, ks, vs = chunk_views(c)
        psU_t, psU_b = ps[2 + pp]
        psV_t, psV_b = ps[4 + pp]
        qdht, qdhb = qdh[pp]
        qdlt, qdlb = qdl[pp]
        smht, smhb = smh[pp]
        smlt, smlb = sml[pp]
        ktdht, ktdhb = ktdh[pp]
        ktdlt, ktdlb = ktdl[pp]

        fns = []
        for h in range(2):
            fns += [
                lambda e, h2=h: e.matmul(psV_t[:, h2 * 128:(h2 + 1) * 128], vs(h2, 0), smht[:, h2 * 128:(h2 + 1) * 128], start=True, stop=False),
                lambda e, h2=h: e.matmul(psV_t[:, h2 * 128:(h2 + 1) * 128], vs(h2, 1), smht[:, h2 * 128:(h2 + 1) * 128], start=False, stop=False),
                lambda e, h2=h: e.matmul(psV_t[:, h2 * 128:(h2 + 1) * 128], vs(h2, 0), smlt[:, h2 * 128:(h2 + 1) * 128], start=False, stop=False),
                lambda e, h2=h: e.matmul(psV_t[:, h2 * 128:(h2 + 1) * 128], sbSh[:, h2 * 128:(h2 + 1) * 128], qdht[:, h2 * 128:(h2 + 1) * 128], start=False, stop=False),
                lambda e, h2=h: e.matmul(psV_t[:, h2 * 128:(h2 + 1) * 128], sbSl[:, h2 * 128:(h2 + 1) * 128], qdht[:, h2 * 128:(h2 + 1) * 128], start=False, stop=False),
                lambda e, h2=h: e.matmul(psV_t[:, h2 * 128:(h2 + 1) * 128], sbSh[:, h2 * 128:(h2 + 1) * 128], qdlt[:, h2 * 128:(h2 + 1) * 128], start=False, stop=True),
            ]
        T.op("tensor", fns, reads=(b_v, smhb, smlb, b_Sh, b_Sl, qdhb, qdlb), writes=(psV_b,))
        fns = []
        for h in range(2):
            fns += [
                lambda e, h2=h: e.matmul(psU_t[:, h2 * 128:(h2 + 1) * 128], ktdht[:, h2 * 128:(h2 + 1) * 128], vs(h2, 0), start=True, stop=False),
                lambda e, h2=h: e.matmul(psU_t[:, h2 * 128:(h2 + 1) * 128], ktdlt[:, h2 * 128:(h2 + 1) * 128], vs(h2, 0), start=False, stop=False),
                lambda e, h2=h: e.matmul(psU_t[:, h2 * 128:(h2 + 1) * 128], ktdht[:, h2 * 128:(h2 + 1) * 128], vs(h2, 1), start=False, stop=True),
            ]
        T.op("tensor", fns, reads=(ktdhb, ktdlb, b_v), writes=(psU_b,))

        T.op("vector", [
            lambda e: e.scalar_tensor_tensor(out=sbS[:, 0:128], in0=sbS[:, 0:128], scalar=c_lamc[:, 0:1], in1=psU_t[:, 0:128], op0=ALU.mult, op1=ALU.add),
            lambda e: e.scalar_tensor_tensor(out=sbS[:, 128:256], in0=sbS[:, 128:256], scalar=c_lamc[:, 1:2], in1=psU_t[:, 128:256], op0=ALU.mult, op1=ALU.add),
        ], reads=(psU_b, b_lamc), writes=(b_S,))
        T.op("vector", lambda e: e.tensor_copy(out=sbSh[:], in_=sbS[:]), reads=(b_S,), writes=(b_Sh,))
        T.op("vector", lambda e: e.tensor_tensor(out=sbSl[:], in0=sbS[:], in1=sbSh[:], op=ALU.subtract),
             reads=(b_S, b_Sh), writes=(b_Sl,))

        stt, stb = astg[pp]
        T.op("scalar", lambda e: e.copy(out=stt[:], in_=psV_t[:, 0:256]), reads=(psV_b,), writes=(stb,))
        j, toff = gc // 2, (gc % 2) * 128
        T.dma("gpsimd", f"ccw{pp}", lambda e, j2=j, t=toff, s=stt:
              e.dma_start(out=cc_in[b][j2, :, :, t:t + 128], in_=s[:].rearrange("p (h t) -> p h t", h=2)),
              reads=(stb,), writes=(b_ccin[b],))

    def emit_group(b, g):
        gi = b * 4 + g
        bi = gi % 2
        xgt, xgb = xg[bi]
        for hl in range(2):
            for si in range(2):
                seg = 2 * g + si
                T.dma("sync", f"xg{bi}_{hl}", lambda e, h=hl, sg=seg, s2=si, d=xgt, f=b:
                      e.dma_start(out=d[:, h * NKC * 512:(h + 1) * NKC * 512]
                                  .rearrange("p (k t) -> p k t", k=NKC)[:, :, s2 * 256:s2 * 256 + 256],
                                  in_=ag_out[f][sg, :, h, :, :]),
                      reads=(b_agout[b],), writes=(xgb,))

        for fc in range(4):
            pst, psb = ps[fc]
            fns = []
            for kc in range(NKC):
                for p in range(3):
                    wt = wqh if p in (0, 2) else wql
                    xs = 0 if p in (0, 1) else 1
                    fns.append(lambda e, w=wt, k=kc, f=fc, x=xs, d=xgt, bk=pst, st=(kc == 0 and p == 0), sp=(kc == NKC - 1 and p == 2):
                               e.matmul(bk[:], w[:, k * 768 + f * 128:k * 768 + (f + 1) * 128],
                                        d[:, x * NKC * 512 + k * 512:x * NKC * 512 + (k + 1) * 512], start=st, stop=sp))
            T.op("tensor", fns, reads=(xgb, b_wqh, b_wql), writes=(psb,))
        for sl in range(4):
            bank, half = 4 + sl // 2, (sl % 2) * 256
            pst, psb = ps[bank]
            fns = []
            for kc in range(NKC):
                for p in range(3):
                    wt = wqh if p in (0, 2) else wql
                    xs = 0 if p in (0, 1) else 1
                    fns.append(lambda e, w=wt, k=kc, s=sl, bk=pst, hf=half, x=xs, d=xgt, st=(kc == 0 and p == 0):
                               e.matmul(bk[:, hf:hf + 256], d[:, x * NKC * 512 + k * 512 + s * 128:x * NKC * 512 + k * 512 + (s + 1) * 128],
                                        w[:, k * 768 + 512:(k + 1) * 768], start=st, stop=False))
            fns.append(lambda e, bk=pst, hf=half: e.matmul(bk[:, hf:hf + 256], c_ones[0:1, 0:128], c_bvh[0:1, :], start=False, stop=False))
            fns.append(lambda e, bk=pst, hf=half: e.matmul(bk[:, hf:hf + 256], c_ones[0:1, 0:128], c_bvl[0:1, :], start=False, stop=True))
            T.op("tensor", fns, reads=(xgb, b_wqh, b_wql, b_ones, b_bvh, b_bvl), writes=(psb,))

        for fc in range(4):
            dstt, dstb = (qkv_q, b_q) if fc < 2 else (qkv_k, b_k)
            h = fc % 2
            pst, psb = ps[fc]
            s32t, s32b = qk32[fc % 2]
            T.op("vector", lambda e, p=pst, d=s32t, f=fc: e.tensor_scalar_add(d[:], p[:], c_bqk[:, f:f + 1]),
                 reads=(psb, b_bqk), writes=(s32b,))
            T.op("scalar", lambda e, s=s32t, d=dstt, h2=h: e.copy(out=d[:, h2 * 512:(h2 + 1) * 512], in_=s[:]),
                 reads=(s32b,), writes=(dstb,))
            T.op("vector", lambda e, s=s32t, d=dstt, h2=h: e.tensor_tensor(out=d[:, 1024 + h2 * 512:1024 + (h2 + 1) * 512], in0=s[:],
                                                                           in1=d[:, h2 * 512:(h2 + 1) * 512], op=ALU.subtract),
                 reads=(s32b, dstb), writes=(dstb,))
        for sl in range(4):
            bank, half = 4 + sl // 2, (sl % 2) * 256
            pst, psb = ps[bank]
            T.op("scalar", lambda e, p=pst, hf=half, s=sl: e.copy(out=qkv_v[:, s * 256:(s + 1) * 256], in_=p[:, hf:hf + 256]),
                 reads=(psb,), writes=(b_v,))
            T.op("vector", lambda e, p=pst, hf=half, s=sl: e.tensor_tensor(out=qkv_v[:, 1024 + s * 256:1024 + (s + 1) * 256],
                                                                           in0=p[:, hf:hf + 256], in1=qkv_v[:, s * 256:(s + 1) * 256], op=ALU.subtract),
                 reads=(psb,), writes=(b_v,))

        emit_chunk_front(b, g, 0)
        emit_chunk_front(b, g, 1)
        emit_chunk_back(b, g, 0)
        emit_chunk_front(b, g, 2)
        emit_chunk_back(b, g, 1)
        emit_chunk_front(b, g, 3)
        emit_chunk_back(b, g, 2)
        emit_chunk_back(b, g, 3)

    b_ccout = [Buf(f"cc_out{i}") for i in range(4)]

    def _a2a(b2):
        def f(e):
            with allow_ici_all_to_all():
                return e.collective_compute("AllToAll", ALU.bypass, replica_groups=[list(range(N_CORES))],
                                            ins=[cc_in[b2].tensor.ap().opt()], outs=[cc_out[b2].tensor.ap().opt()])
        return f

    for b in range(B):
        T.op("vector", lambda e: e.memset(sbS[:], 0.0), writes=(b_S,))
        T.op("vector", lambda e: e.memset(sbSh[:], 0.0), writes=(b_Sh,))
        T.op("vector", lambda e: e.memset(sbSl[:], 0.0), writes=(b_Sl,))
        for g in range(4):
            emit_group(b, g)
        T.op("gpsimd", _a2a(b), reads=(b_ccin[b],), writes=(b_ccout[b],))

    T.barrier([b_wqh, b_wql, b_q, b_k, b_v, b_S, b_Sh, b_Sl] +
              [x[1] for x in wqt + xg + qk32 + smf + smh + sml + ktf + ktg + ktdh + ktdl + qdf + qdh + qdl + astg + ps + pkt])
    am.close()
    wqsc.close()

    pb_ = ExitStack()
    wo_h, b_woh = sb("wo_h", [128, NKC * 1024], BF16, pb_)
    wo_l, b_wol = sb("wo_l", [128, NKC * 1024], BF16, pb_)
    wo_t = [sb(f"wo_t{i}", [128, 1024], F32, pb_) for i in range(3)]
    at32 = [sb(f"at32_{i}", [128, 2048], F32, pb_) for i in range(2)]
    b_a32p = [[Buf(f"a32_{i}_{p}") for p in range(2)] for i in range(2)]
    ath = [sb(f"ath{i}", [128, 2048], BF16, pb_) for i in range(2)]
    atl = [sb(f"atl{i}", [128, 2048], BF16, pb_) for i in range(2)]
    xmt = [sb(f"xmt{i}", [128, 1024], F32, pb_) for i in range(2)]
    ystg = [sb(f"ystg{i}", [128, 1024], F32, pb_) for i in range(2)]
    psB = [psum(f"psB{i}", [128, 1024], F32, pb_) for i in range(2)]
    b_yscr = Buf("y_scr")

    stat, b_stat = sb("stat", [128, 16], F32, pb_)
    yt = [sb(f"yt{i}", [128, 2048], F32, pb_) for i in range(2)]
    x2t = [sb(f"x2t{i}", [128, 2048], F32, pb_) for i in range(2)]
    x2h = [sb(f"x2h{i}", [128, 2048], BF16, pb_) for i in range(3)]
    x2l = [sb(f"x2l{i}", [128, 2048], BF16, pb_) for i in range(3)]
    x2stg = [sb(f"x2stg{i}", [128, 1024], BF16, pb_) for i in range(2)]
    ps_x2 = [psum(f"ps_x2{i}", [128, 1024], BF16, pb_) for i in range(2)]
    b_x2scr = Buf("x2_scr")
    b_x2fm = Buf("x2fm_scr")

    def emit_ln(engV, y_src, y_b, st_t, st_b, out_t, out_b):
        T.op("scalar", lambda e: e.activation(out=out_t[:], in_=y_src[:], func=ACT.Square, accum_out=st_t[:, 1:2]),
             reads=(y_b,), writes=(out_b, st_b))
        T.op("scalar", lambda e: e.activation(out=out_t[:], in_=y_src[:], func=ACT.Copy, accum_out=st_t[:, 0:1]),
             reads=(y_b,), writes=(out_b, st_b))
        T.op("vector", lambda e: e.tensor_scalar_mul(st_t[:, 2:3], st_t[:, 0:1], 1.0 / D), reads=(st_b,), writes=(st_b,), self_sync=True)
        T.op("vector", lambda e: e.tensor_scalar_mul(st_t[:, 3:4], st_t[:, 1:2], 1.0 / D), reads=(st_b,), writes=(st_b,), self_sync=True)
        T.op("vector", lambda e: e.tensor_tensor(out=st_t[:, 4:5], in0=st_t[:, 2:3], in1=st_t[:, 2:3], op=ALU.mult),
             reads=(st_b,), writes=(st_b,), self_sync=True)
        T.op("vector", lambda e: e.tensor_tensor(out=st_t[:, 5:6], in0=st_t[:, 3:4], in1=st_t[:, 4:5], op=ALU.subtract),
             reads=(st_b,), writes=(st_b,), self_sync=True)
        T.op("vector", lambda e: e.tensor_scalar_add(st_t[:, 6:7], st_t[:, 5:6], EPS), reads=(st_b,), writes=(st_b,), self_sync=True)
        T.op("scalar", lambda e: e.activation(out=st_t[:, 7:8], in_=st_t[:, 6:7], func=ACT.Sqrt), reads=(st_b,), writes=(st_b,))
        T.op("vector", lambda e: e.reciprocal(st_t[:, 8:9], st_t[:, 7:8]), reads=(st_b,), writes=(st_b,), self_sync=True)
        T.op("vector", lambda e: e.tensor_tensor(out=st_t[:, 9:10], in0=st_t[:, 2:3], in1=st_t[:, 8:9], op=ALU.mult),
             reads=(st_b,), writes=(st_b,), self_sync=True)
        T.op("vector", lambda e: e.tensor_scalar_mul(st_t[:, 10:11], st_t[:, 9:10], -1.0), reads=(st_b,), writes=(st_b,), self_sync=True)
        T.op("vector", lambda e: e.tensor_scalar(out=out_t[:], in0=y_src[:], scalar1=st_t[:, 8:9], scalar2=st_t[:, 10:11],
                                                 op0=ALU.mult, op1=ALU.add),
             reads=(y_b, st_b), writes=(out_b,))

    def emit_ln1_stats(ti):
        bi = ti % 2
        ytt, ytb = yt[bi]
        x2tt, x2tb = x2t[bi]
        emit_ln("vector", ytt, ytb, stat, b_stat, x2tt, x2tb)
        T.dma("gpsimd", f"x2w{bi}", lambda e, t=ti, s=x2tt: e.dma_start(out=x2_scr[t * 128:(t + 1) * 128, :], in_=s[:]),
              reads=(x2tb,), writes=(b_x2scr,))
        x2ht, x2hb = x2h[ti % 3]
        x2lt, x2lb = x2l[ti % 3]
        T.op("scalar", lambda e, s=x2tt, d=x2ht: e.copy(out=d[:], in_=s[:]), reads=(x2tb,), writes=(x2hb,))
        T.op("vector", lambda e, s=x2tt, d1=x2ht, d2=x2lt: e.tensor_tensor(out=d2[:], in0=s[:], in1=d1[:], op=ALU.subtract),
             reads=(x2tb, x2hb), writes=(x2lb,))

    def emit_ln1_transposes(ti):
        x2ht, x2hb = x2h[ti % 3]
        x2lt, x2lb = x2l[ti % 3]
        for grp in range(4):
            hl, oct_ = grp // 2, grp % 2
            pbk = (ti * 4 + grp) % 2
            pst, psb = ps_x2[pbk]
            srct, srcb = (x2ht, x2hb) if hl == 0 else (x2lt, x2lb)
            fns = []
            for k in range(8):
                kc = oct_ * 8 + k
                fns.append(lambda e, s=srct, k2=kc, p=pst, kk=k:
                           e.transpose(p[:, kk * 128:(kk + 1) * 128], s[:, k2 * 128:(k2 + 1) * 128], c_idb[:]))
            T.op("tensor", fns, reads=(srcb, b_idb), writes=(psb,))
            stt, stb = x2stg[pbk]
            T.op("scalar", lambda e, p=pst, d=stt: e.copy(out=d[:], in_=p[:]), reads=(psb,), writes=(stb,))
            T.dma("gpsimd", f"x2fw{pbk}", lambda e, s=stt, h=hl, o=oct_, t=ti:
                  e.dma_start(out=x2fm_scr[h, :, o * 8:(o + 1) * 8, t * 128:(t + 1) * 128],
                              in_=s[:].rearrange("p (k t) -> p k t", k=8)),
                  reads=(stb,), writes=(b_x2fm,))

    for half in range(2):
        for kc in range(NKC):
            wtt, wtb = wo_t[kc % 3]
            T.dma("sync", f"wof{kc % 3}", lambda e, k=kc, h=half, d=wtt:
                  e.dma_start(out=d[:], in_=woutT[k * 128:(k + 1) * 128, h * 1024:(h + 1) * 1024]), writes=(wtb,))
            T.op("scalar", lambda e, k=kc, s=wtt: e.copy(out=wo_h[:, k * 1024:(k + 1) * 1024], in_=s[:]),
                 reads=(wtb,), writes=(b_woh,))
            T.op("vector", lambda e, k=kc, s=wtt: e.tensor_tensor(out=wo_l[:, k * 1024:(k + 1) * 1024], in0=s[:],
                                                                  in1=wo_h[:, k * 1024:(k + 1) * 1024], op=ALU.subtract),
                 reads=(wtb, b_woh), writes=(b_wol,))
        def emit_at_prep(ti):
            bi = ti % 2
            a32t, a32b = at32[bi]
            atht, athb = ath[bi]
            atlt, atlb = atl[bi]
            for j in range(8):
                T.dma("sync", f"at{bi}_{j % 2}", lambda e, j2=j, b2=ti // 2, t2=(ti % 2) * 128, d=a32t:
                      e.dma_start(out=d[:, j2 * 256:(j2 + 1) * 256].rearrange("p (h t) -> p h t", h=2),
                                  in_=cc_out[b2][j2, :, :, t2:t2 + 128]),
                      reads=(b_ccout[ti // 2],), writes=(b_a32p[bi][j % 2],))
            a32rd = (b_a32p[bi][0], b_a32p[bi][1])
            T.op("scalar", lambda e, s=a32t, d=atht: e.copy(out=d[:], in_=s[:]), reads=a32rd, writes=(athb,))
            T.op("vector", lambda e, s=a32t, d1=atht, d2=atlt: e.tensor_tensor(out=d2[:], in0=s[:], in1=d1[:], op=ALU.subtract),
                 reads=a32rd + (athb,), writes=(atlb,))
            xmtt, xmtb = xmt[bi]
            T.dma("sync", f"xm{bi}", lambda e, t=ti, h=half, d=xmtt:
                  e.dma_start(out=d[:], in_=x_my[t * 128:(t + 1) * 128, h * 1024:(h + 1) * 1024]), writes=(xmtb,))

        emit_at_prep(0)
        for ti in range(NT):
            bi = ti % 2
            atht, athb = ath[bi]
            atlt, atlb = atl[bi]
            xmtt, xmtb = xmt[bi]
            if half == 1:
                ytt0, ytb0 = yt[bi]
                T.dma("sync", f"yld{bi}", lambda e, t=ti, d=ytt0: e.dma_start(out=d[:, 0:1024], in_=y_scr[t * 128:(t + 1) * 128, 0:1024]),
                      reads=(b_yscr,), writes=(ytb0,))
            pst, psb = psB[bi]
            fns = []
            for ncc in range(2):
                for kc in range(NKC):
                    for p in range(3):
                        wt = wo_h if p in (0, 2) else wo_l
                        asrc = atht if p in (0, 1) else atlt
                        fns.append(lambda e, w=wt, a=asrc, k=kc, n=ncc, bk=pst, st=(kc == 0 and p == 0):
                                   e.matmul(bk[:, n * 512:(n + 1) * 512], a[:, k * 128:(k + 1) * 128],
                                            w[:, k * 1024 + n * 512:k * 1024 + (n + 1) * 512], start=st, stop=False))
                fns.append(lambda e, n=ncc, h=half, bk=pst: e.matmul(bk[:, n * 512:(n + 1) * 512], c_ones[0:1, 0:128],
                                                                     c_bouth[0:1, h * 1024 + n * 512:h * 1024 + (n + 1) * 512], start=False, stop=False))
                fns.append(lambda e, n=ncc, h=half, bk=pst: e.matmul(bk[:, n * 512:(n + 1) * 512], c_ones[0:1, 0:128],
                                                                     c_boutl[0:1, h * 1024 + n * 512:h * 1024 + (n + 1) * 512], start=False, stop=True))
            T.op("tensor", fns, reads=(athb, atlb, b_woh, b_wol, b_ones, b_bouth, b_boutl), writes=(psb,))
            if ti + 1 < NT:
                emit_at_prep(ti + 1)
            if half == 0:
                yst, ysb = ystg[bi]
                T.op("vector", lambda e, p=pst, x=xmtt, d=yst: e.scalar_tensor_tensor(out=d[:], in0=p[:], scalar=1.0, in1=x[:], op0=ALU.mult, op1=ALU.add),
                     reads=(psb, xmtb), writes=(ysb,))
                T.dma("gpsimd", f"yw{bi}", lambda e, t=ti, s=yst:
                      e.dma_start(out=y_scr[t * 128:(t + 1) * 128, 0:1024], in_=s[:]), reads=(ysb,), writes=(b_yscr,))
            else:
                ytt0, ytb0 = yt[bi]
                T.op("vector", lambda e, p=pst, x=xmtt, d=ytt0: e.scalar_tensor_tensor(out=d[:, 1024:2048], in0=p[:], scalar=1.0, in1=x[:], op0=ALU.mult, op1=ALU.add),
                     reads=(psb, xmtb, ytb0), writes=(ytb0,))
                emit_ln1_stats(ti)
                if ti >= 2:
                    emit_ln1_transposes(ti - 2)
                if ti == NT - 1:
                    emit_ln1_transposes(ti - 1)
                    emit_ln1_transposes(ti)

    T.barrier([b_woh, b_wol, b_stat] +
              [x[1] for x in wo_t + at32 + ath + atl + xmt + ystg + psB + yt + x2t + x2h + x2l + x2stg + ps_x2])
    pb_.close()

    pc = ExitStack()
    x2f, b_x2f = sb("x2f", [128, 2 * NKC * 512], BF16, pc)
    hbuf, b_h = sb("hbuf", [128, 2 * NHC * 512], BF16, pc)
    wg = [sb(f"wg{i}", [128, NKC * 128], BF16, pc) for i in range(3)]
    wu = [sb(f"wu{i}", [128, NKC * 128], BF16, pc) for i in range(3)]
    wd = [sb(f"wd{i}", [128, D], BF16, pc) for i in range(3)]
    gsil, b_gsil = sb("gsil", [128, 512], F32, pc)
    utmp, b_utmp = sb("utmp", [128, 512], F32, pc)
    h32, b_h32 = gsil, b_gsil
    x2r, b_x2r = sb("x2r", [128, 2048], F32, pc)
    y2 = [sb(f"y2t{i}", [128, 2048], F32, pc) for i in range(2)]
    otst, b_ot = x2r, b_x2r
    st2, b_st2 = sb("st2", [128, 16], F32, pc)
    psC = [psum(f"psC{i}", [128, 512], F32, pc) for i in range(8)]
    b_out = Buf("out_my")

    wg_preloaded = set()

    def emit_x2f_loads(th):
        for hl in range(2):
            T.dma("sync", f"x2f{hl}", lambda e, h=hl, t=th:
                  e.dma_start(out=x2f[:, h * NKC * 512:(h + 1) * NKC * 512].rearrange("p (k t) -> p k t", k=NKC),
                              in_=x2fm_scr[h, :, :, t * 512:(t + 1) * 512]),
                  reads=(b_x2fm,), writes=(b_x2f,))

    emit_x2f_loads(0)
    for th in range(2):
        for hc in range(NHC):
            bi = hc % 3
            wgt, wgb = wg[bi]
            wut, wub = wu[bi]
            if (th, hc) not in wg_preloaded:
                T.dma("gpsimd", f"wg{bi}", lambda e, h=hc, d=wgt: e.dma_start(out=d[:].rearrange("p (k t) -> p k t", k=NKC), in_=wgT_t[h, :, :, :]),
                      writes=(wgb,))
                T.op("scalar", lambda e, d=wgt: e.add(out=d[:], in_=d[:], add=c_n128c[:, 0:1]), reads=(wgb,), writes=(wgb,))
                T.dma("gpsimd", f"wu{bi}", lambda e, h=hc, d=wut: e.dma_start(out=d[:].rearrange("p (k t) -> p k t", k=NKC), in_=wuT_t[h, :, :, :]),
                      writes=(wub,))
                T.op("scalar", lambda e, d=wut: e.add(out=d[:], in_=d[:], add=c_n128c[:, 0:1]), reads=(wub,), writes=(wub,))
            pg_t, pg_b = psC[2 + hc % 2]
            pu_t, pu_b = psC[4 + hc % 2]
            for (wt, wb_, pt, pb2) in ((wgt, wgb, pg_t, pg_b), (wut, wub, pu_t, pu_b)):
                fns = []
                for kc in range(NKC):
                    for hl in range(2):
                        fns.append(lambda e, w=wt, k=kc, h=hl, bk=pt, st=(kc == 0 and hl == 0), sp=(kc == NKC - 1 and hl == 1):
                                   e.matmul(bk[:], w[:, k * 128:(k + 1) * 128],
                                            x2f[:, h * NKC * 512 + k * 512:h * NKC * 512 + (k + 1) * 512], start=st, stop=sp))
                T.op("tensor", fns, reads=(wb_, b_x2f), writes=(pb2,))
            T.op("scalar", lambda e, p=pg_t, h=hc: e.activation(out=gsil[:], in_=p[:], func=ACT.Silu, scale=c_sg[:, h:h + 1]),
                 reads=(pg_b, b_sg), writes=(b_gsil,))
            T.op("vector", lambda e, p=pu_t, h=hc: e.tensor_scalar_mul(utmp[:], p[:], c_su[:, h:h + 1]),
                 reads=(pu_b, b_su), writes=(b_utmp,))
            T.op("vector", lambda e: e.tensor_tensor(out=h32[:], in0=gsil[:], in1=utmp[:], op=ALU.mult),
                 reads=(b_gsil, b_utmp), writes=(b_h32, b_gsil))
            T.op("scalar", lambda e, h=hc: e.copy(out=hbuf[:, h * 512:(h + 1) * 512], in_=h32[:]),
                 reads=(b_h32,), writes=(b_h,))
            T.op("vector", lambda e, h=hc: e.tensor_tensor(out=hbuf[:, NHC * 512 + h * 512:NHC * 512 + (h + 1) * 512],
                                                           in0=h32[:], in1=hbuf[:, h * 512:(h + 1) * 512], op=ALU.subtract),
                 reads=(b_h32,), writes=(b_h,))

        if th == 0:
            emit_x2f_loads(1)
            for hc2 in (0, 1):
                bi2 = hc2 % 3
                wgt2, wgb2 = wg[bi2]
                wut2, wub2 = wu[bi2]
                T.dma("gpsimd", f"wg{bi2}", lambda e, h=hc2, d=wgt2: e.dma_start(out=d[:].rearrange("p (k t) -> p k t", k=NKC), in_=wgT_t[h, :, :, :]),
                      writes=(wgb2,))
                T.op("scalar", lambda e, d=wgt2: e.add(out=d[:], in_=d[:], add=c_n128c[:, 0:1]), reads=(wgb2,), writes=(wgb2,))
                T.dma("gpsimd", f"wu{bi2}", lambda e, h=hc2, d=wut2: e.dma_start(out=d[:].rearrange("p (k t) -> p k t", k=NKC), in_=wuT_t[h, :, :, :]),
                      writes=(wub2,))
                T.op("scalar", lambda e, d=wut2: e.add(out=d[:], in_=d[:], add=c_n128c[:, 0:1]), reads=(wub2,), writes=(wub2,))
                wg_preloaded.add((1, hc2))

        for tp in range(2):
            for kc in range(NHC):
                bi = kc % 3
                wdt, wdb = wd[bi]
                T.dma("gpsimd", f"wd{bi}", lambda e, k=kc, d=wdt: e.dma_start(out=d[:], in_=wdT[k * 128:(k + 1) * 128, :]), writes=(wdb,))
                T.op("scalar", lambda e, d=wdt: e.add(out=d[:], in_=d[:], add=c_n128c[:, 0:1]), reads=(wdb,), writes=(wdb,))
                fns = []
                for ti_in in range(2):
                    tloc = (tp * 2 + ti_in) * 128
                    for hl in range(2):
                        for ncc in range(4):
                            bk = ti_in * 4 + ncc
                            fns.append(lambda e, w=wdt, k2=kc, h=hl, t=tloc, n=ncc, p=psC[bk][0],
                                       st=(kc == 0 and hl == 0), sp=(kc == NHC - 1 and hl == 1):
                                       e.matmul(p[:], hbuf[:, h * NHC * 512 + k2 * 512 + t:h * NHC * 512 + k2 * 512 + t + 128],
                                                w[:, n * 512:(n + 1) * 512], start=st, stop=sp))
                T.op("tensor", fns, reads=(wdb, b_h), writes=tuple(psC[i][1] for i in range(8)))

            for ti_in in range(2):
                ti = th * 4 + tp * 2 + ti_in
                y2t, b_y2t = y2[ti_in]
                T.dma("sync", "x2r", lambda e, t=ti, d=x2r: e.dma_start(out=d[:], in_=x2_scr[t * 128:(t + 1) * 128, :]),
                      reads=(b_x2scr,), writes=(b_x2r,))
                for ncc in range(4):
                    bk = ti_in * 4 + ncc
                    pt, pb2 = psC[bk]
                    T.op("vector", lambda e, p=pt, n=ncc, d=y2t: e.tensor_tensor(out=d[:, n * 512:(n + 1) * 512], in0=p[:],
                                                                                 in1=c_sdb[:, n * 512:(n + 1) * 512], op=ALU.mult),
                         reads=(pb2, b_sdb), writes=(b_y2t,))
                T.op("vector", lambda e, d=y2t: e.tensor_tensor(out=d[:], in0=d[:], in1=x2r[:], op=ALU.add),
                     reads=(b_y2t, b_x2r), writes=(b_y2t,))
            for ti_in in range(2):
                ti = th * 4 + tp * 2 + ti_in
                y2t, b_y2t = y2[ti_in]
                emit_ln("vector", y2t, b_y2t, st2, b_st2, otst, b_ot)
                T.dma("sync", "outw", lambda e, t=ti, s=otst: e.dma_start(out=out_my[t * 128:(t + 1) * 128, :], in_=s[:]),
                      reads=(b_ot,), writes=(b_out,))

    h = T.sems["d_outw"]
    v = T.count["d_outw"]
    T.ops["sync"].append(lambda e, h=h, v=v: e.wait_ge(h, v))

    with nc.Block() as block:
        @block.sync
        def _(e):
            for f in T.ops["sync"]:
                f(e)

        @block.gpsimd
        def _(e):
            for f in T.ops["gpsimd"]:
                f(e)

        @block.tensor
        def _(e):
            for f in T.ops["tensor"]:
                f(e)

        @block.vector
        def _(e):
            for f in T.ops["vector"]:
                f(e)

        @block.scalar
        def _(e):
            for f in T.ops["scalar"]:
                f(e)

    pc.close()
    es.close()
    return nc


_NC_CACHE = []


def kernel(L0_attn_x, L0_attn_w_qkv, L0_attn_b_qkv, L0_attn_w_out, L0_attn_b_out,
           L0_ffn_wq_gate, L0_ffn_s_gate, L0_ffn_wq_up, L0_ffn_s_up,
           L0_ffn_wq_down, L0_ffn_s_down):
    x = np.ascontiguousarray(np.asarray(L0_attn_x, dtype=np.float32))
    w_qkv = np.asarray(L0_attn_w_qkv, dtype=np.float32)
    b_qkv = np.asarray(L0_attn_b_qkv, dtype=np.float32)
    w_out = np.asarray(L0_attn_w_out, dtype=np.float32)
    b_out = np.asarray(L0_attn_b_out, dtype=np.float32)
    wq_gate = np.asarray(L0_ffn_wq_gate, dtype=np.int32)
    wq_up = np.asarray(L0_ffn_wq_up, dtype=np.int32)
    wq_down = np.asarray(L0_ffn_wq_down, dtype=np.int32)
    s_gate = np.asarray(L0_ffn_s_gate, dtype=np.float32)
    s_up = np.asarray(L0_ffn_s_up, dtype=np.float32)
    s_down = np.asarray(L0_ffn_s_down, dtype=np.float32)

    woutT = np.ascontiguousarray(w_out.T)
    bo_h, bo_l = _hi_lo(b_out)
    wgT_t = np.ascontiguousarray(wq_gate.reshape(NHC, 128, NKC, 128).transpose(0, 3, 2, 1))
    wuT_t = np.ascontiguousarray(wq_up.reshape(NHC, 128, NKC, 128).transpose(0, 3, 2, 1))
    wdT = np.ascontiguousarray(wq_down.T)
    sg_pp = np.ascontiguousarray(s_gate.reshape(NHC, 128).T)
    su_pp = np.ascontiguousarray(s_up.reshape(NHC, 128).T)
    sdb = np.ascontiguousarray(np.broadcast_to(s_down, (128, D)))
    identf = np.eye(128, dtype=np.float32)
    identb = np.eye(128, dtype=ml_dtypes.bfloat16)
    ones_row = np.ones((1, 512), dtype=ml_dtypes.bfloat16)
    n128_row = np.full((1, 512), -128.0, dtype=ml_dtypes.bfloat16)
    ones_colb = np.ones((128, 1), dtype=ml_dtypes.bfloat16)

    slopes = _slopes16()
    lam = np.exp(-slopes)
    pos = np.arange(128, dtype=np.float64)
    diff = pos[:, None] - pos[None, :]

    in_maps = []
    for r in range(N_CORES):
        x_my = np.ascontiguousarray(x[:, r * 256:(r + 1) * 256, :].reshape(TOK, D))
        h0, h1 = 2 * r, 2 * r + 1
        rows = np.concatenate([np.arange(h0 * 128, (h1 + 1) * 128),
                               2048 + np.arange(h0 * 128, (h1 + 1) * 128),
                               4096 + np.arange(h0 * 128, (h1 + 1) * 128)])
        wqkvT = np.ascontiguousarray(w_qkv[rows, :].T)
        bq = b_qkv[rows]
        bqk_ = np.stack([bq[0:128], bq[128:256], bq[256:384], bq[384:512]], axis=1).astype(np.float32)
        bvh, bvl = _hi_lo(bq[512:768])
        attc = np.zeros((128, 6 * 128), dtype=np.float32)
        lamc = np.zeros((128, 2), dtype=np.float32)
        dkpp = np.zeros((128, 2), dtype=np.float32)
        for hl, hh in enumerate((h0, h1)):
            lh = lam[hh]
            attc[:, hl * 128:(hl + 1) * 128] = (lh ** (pos + 1.0))[None, :]
            attc[:, 256 + hl * 128:256 + (hl + 1) * 128] = (lh ** (127.0 - pos))[None, :]
            dmT = np.where(diff >= 0, lh ** np.maximum(diff, 0.0), 0.0).T
            attc[:, 512 + hl * 128:512 + (hl + 1) * 128] = dmT.astype(np.float32)
            lamc[:, hl] = lh ** 128.0
            dkpp[:, hl] = (lh ** (127.0 - pos)).astype(np.float32)
        in_maps.append(dict(
            x_my=x_my, wqkvT=wqkvT, bqk=bqk_,
            bv_h=bvh.reshape(1, 256), bv_l=bvl.reshape(1, 256),
            woutT=woutT, bout_h=bo_h.reshape(1, D), bout_l=bo_l.reshape(1, D),
            wgT_t=wgT_t, wuT_t=wuT_t, wdT=wdT,
            sg_pp=sg_pp, su_pp=su_pp, sdb=sdb,
            attc=attc, lamc=lamc, dk_pp=dkpp,
            identf=identf, identb=identb,
            ones_row=ones_row, n128_row=n128_row, ones_colb=ones_colb,
            n128_col=np.full((128, 1), -128.0, dtype=np.float32),
        ))

    if not _NC_CACHE:
        _NC_CACHE.append(build_nc())
    nc = _NC_CACHE[0]
    res = run_bass_kernel_spmd(nc, in_maps, core_ids=list(range(N_CORES)))
    out = np.empty((B, N, D), dtype=np.float32)
    for r in range(N_CORES):
        out[:, r * 256:(r + 1) * 256, :] = res.results[r]["out_my"].reshape(B, 256, D)
    return out


# version 18
# speedup vs baseline: 1.0895x; 1.0895x over previous
import math
from contextlib import ExitStack

import numpy as np
import ml_dtypes

import concourse.bass as bass
import concourse.mybir as mybir
from concourse.bass_utils import run_bass_kernel_spmd
from concourse.aot_env import allow_ici_all_to_all

F32 = mybir.dt.float32
BF16 = mybir.dt.bfloat16
I32 = mybir.dt.int32
ACT = mybir.ActivationFunctionType
ALU = mybir.AluOpType

N_CORES = 8
B, N, D = 4, 2048, 2048
HID = 5632
NKC = D // 128
NHC = HID // 128
TOK = 1024
NT = TOK // 128
EPS = 1e-5

ENGINES = ("sync", "gpsimd", "tensor", "vector", "scalar")


class Buf:
    __slots__ = ("name", "w", "readers")

    def __init__(self, name):
        self.name = name
        self.w = None
        self.readers = []


class Tracker:
    def __init__(self, nc, es):
        self.nc = nc
        self.es = es
        self.ops = {e: [] for e in ENGINES}
        self.sems = {}
        self.count = {}
        self.waited = {e: {} for e in ENGINES}
        self.eng_of_sem = {}
        for e in ENGINES:
            self._mk("p_" + e, e)

    def _mk(self, name, eng=None):
        if name not in self.sems:
            self.sems[name] = self.es.enter_context(self.nc.semaphore(name))
            self.count[name] = 0
            self.eng_of_sem[name] = eng
        return name

    def wait(self, eng, tag):
        if tag is None:
            return
        sem, val = tag
        if val <= 0:
            return
        if self.eng_of_sem.get(sem) == eng and eng == "tensor":
            return
        if self.waited[eng].get(sem, 0) >= val:
            return
        self.waited[eng][sem] = val
        h = self.sems[sem]
        self.ops[eng].append(lambda e, h=h, v=val: e.wait_ge(h, v))

    def _pre(self, eng, reads, writes):
        for b in reads:
            self.wait(eng, b.w)
        for b in writes:
            self.wait(eng, b.w)
            for t in b.readers:
                self.wait(eng, t)

    def op(self, eng, fns, reads=(), writes=(), self_sync=False):
        if not isinstance(fns, (list, tuple)):
            fns = [fns]
        self._pre(eng, reads, writes)
        sem = "p_" + eng
        self.count[sem] += 1
        v = self.count[sem]
        h = self.sems[sem]
        last = fns[-1]
        fns = list(fns[:-1]) + [lambda e, f=last, h=h: f(e).then_inc(h, 1)]
        t = (sem, v)
        for f in fns:
            self.ops[eng].append(f)
        for b in writes:
            b.w = t
            b.readers = []
        for b in reads:
            b.readers.append(t)
        return t

    def dma(self, eng, slot, fn, reads=(), writes=()):
        sem = self._mk("d_" + slot)
        self._pre(eng, reads, writes)
        self.count[sem] += 16
        v = self.count[sem]
        h = self.sems[sem]
        self.ops[eng].append(lambda e, f=fn, h=h: f(e).then_inc(h, 16))
        t = (sem, v)
        for b in writes:
            b.w = t
            b.readers = []
        for b in reads:
            b.readers.append(t)
        return t

    def barrier(self, bufs):
        tags = []
        for b in bufs:
            if b.w:
                tags.append(b.w)
            tags.extend(b.readers)
        for e in ENGINES:
            for t in tags:
                self.wait(e, t)


def _slopes16():
    start = 2.0 ** (-(2.0 ** (-(math.log2(16) - 3))))
    return np.array([start ** (i + 1) for i in range(16)], dtype=np.float64)


def _hi_lo(x):
    x = np.asarray(x, dtype=np.float32)
    hi = x.astype(ml_dtypes.bfloat16)
    lo = (x - hi.astype(np.float32)).astype(ml_dtypes.bfloat16)
    return np.ascontiguousarray(hi), np.ascontiguousarray(lo)


def build_nc():
    nc = bass.Bass("TRN2", target_bir_lowering=False)

    dt_in = lambda n, s, d: nc.dram_tensor(n, s, d, kind="ExternalInput").ap()
    x_my = dt_in("x_my", [TOK, D], F32)
    wqkvT = dt_in("wqkvT", [D, 768], F32)
    bqk = dt_in("bqk", [128, 4], F32)
    bv_h = dt_in("bv_h", [1, 256], BF16)
    bv_l = dt_in("bv_l", [1, 256], BF16)
    woutT = dt_in("woutT", [D, D], F32)
    bout_h = dt_in("bout_h", [1, D], BF16)
    bout_l = dt_in("bout_l", [1, D], BF16)
    wgT_t = dt_in("wgT_t", [NHC, 128, NKC, 128], I32)
    wuT_t = dt_in("wuT_t", [NHC, 128, NKC, 128], I32)
    wdT = dt_in("wdT", [HID, D], I32)
    sg_pp = dt_in("sg_pp", [128, NHC], F32)
    su_pp = dt_in("su_pp", [128, NHC], F32)
    sdb = dt_in("sdb", [128, D], F32)
    attc = dt_in("attc", [128, 6 * 128], F32)
    lamc = dt_in("lamc", [128, 2], F32)
    dk_pp = dt_in("dk_pp", [128, 2], F32)
    identf = dt_in("identf", [128, 128], F32)
    identb = dt_in("identb", [128, 128], BF16)
    ones_row = dt_in("ones_row", [1, 512], BF16)
    n128_row = dt_in("n128_row", [1, 512], BF16)
    ones_colb = dt_in("ones_colb", [128, 1], BF16)
    out_my = nc.dram_tensor("out_my", [TOK, D], F32, kind="ExternalOutput").ap()

    ag_in = [nc.dram_tensor(f"ag_in{i}", [128, 2, NKC, 256], BF16).ap() for i in range(4)]
    ag_out = [nc.dram_tensor(f"ag_out{i}", [8, 128, 2, NKC, 256], BF16, addr_space="Shared").ap() for i in range(4)]
    cc_in = [nc.dram_tensor(f"cc_in{i}", [8, 128, 2, 256], F32).ap() for i in range(4)]
    cc_out = [nc.dram_tensor(f"cc_out{i}", [8, 128, 2, 256], F32).ap() for i in range(4)]
    y_scr = nc.dram_tensor("y_scr", [TOK, D], F32).ap()
    x2_scr = nc.dram_tensor("x2_scr", [TOK, D], F32).ap()
    x2fm_scr = nc.dram_tensor("x2fm_scr", [2, 128, NKC, TOK], BF16).ap()

    es = ExitStack()
    T = Tracker(nc, es)

    def sb(name, shape, dtype, stack):
        t = stack.enter_context(nc.sbuf_tensor(name, shape, dtype))
        return t, Buf(name)

    def psum(name, shape, dtype, stack):
        t = stack.enter_context(nc.psum_tensor(name, shape, dtype))
        return t, Buf(name)

    c_attc, b_attc = sb("c_attc", [128, 6 * 128], F32, es)
    c_lamc, b_lamc = sb("c_lamc", [128, 2], F32, es)
    c_dk, b_dk = sb("c_dk", [128, 2], F32, es)
    c_bqk, b_bqk = sb("c_bqk", [128, 4], F32, es)
    c_bvh, b_bvh = sb("c_bvh", [1, 256], BF16, es)
    c_bvl, b_bvl = sb("c_bvl", [1, 256], BF16, es)
    c_bouth, b_bouth = sb("c_bouth", [1, D], BF16, es)
    c_boutl, b_boutl = sb("c_boutl", [1, D], BF16, es)
    c_idf, b_idf = sb("c_idf", [128, 128], F32, es)
    c_idb, b_idb = sb("c_idb", [128, 128], BF16, es)
    c_ones, b_ones = sb("c_ones", [1, 512], BF16, es)
    c_n128, b_n128 = sb("c_n128", [1, 512], BF16, es)
    c_onec, b_onec = sb("c_onec", [128, 1], BF16, es)
    c_sg, b_sg = sb("c_sg", [128, NHC], F32, es)
    c_su, b_su = sb("c_su", [128, NHC], F32, es)
    c_sdb, b_sdb = sb("c_sdb", [128, D], F32, es)

    const_list = [
        (c_attc, b_attc, attc), (c_lamc, b_lamc, lamc), (c_dk, b_dk, dk_pp), (c_bqk, b_bqk, bqk),
        (c_bvh, b_bvh, bv_h), (c_bvl, b_bvl, bv_l),
        (c_bouth, b_bouth, bout_h), (c_boutl, b_boutl, bout_l),
        (c_idf, b_idf, identf), (c_idb, b_idb, identb),
        (c_ones, b_ones, ones_row), (c_n128, b_n128, n128_row),
        (c_onec, b_onec, ones_colb),
        (c_sg, b_sg, sg_pp), (c_su, b_su, su_pp), (c_sdb, b_sdb, sdb),
    ]
    for (t, bf, src) in const_list:
        T.dma("sync", "const", lambda e, t=t, s=src: e.dma_start(out=t[:], in_=s[:, :]), writes=(bf,))
    tot = ("d_const", T.count["d_const"])
    for (_, bf, _) in const_list:
        bf.w = tot

    wqsc = ExitStack()
    wqh, b_wqh = sb("wqh", [128, NKC * 768], BF16, wqsc)
    wql, b_wql = sb("wql", [128, NKC * 768], BF16, wqsc)
    wqtsc = ExitStack()
    wqt = [sb(f"wqt{i}", [128, 768], F32, wqtsc) for i in range(NKC)]
    for kc in range(NKC):
        wtt, wtb = wqt[kc]
        T.dma("sync", f"wqf{kc % 2}", lambda e, k=kc, d=wtt: e.dma_start(out=d[:], in_=wqkvT[k * 128:(k + 1) * 128, :]), writes=(wtb,))
    for kc in range(NKC):
        wqt[kc][1].w = ("d_wqf" + str(kc % 2), T.count["d_wqf" + str(kc % 2)])

    a0 = ExitStack()
    xf = [sb(f"xf{i}", [128, D], F32, a0) for i in range(2)]
    stgh = [sb(f"stgh{i}", [128, 512], BF16, a0) for i in range(4)]
    stgl = [sb(f"stgl{i}", [128, 512], BF16, a0) for i in range(4)]
    ps_tf = [psum(f"ps_tf{i}", [128, 512], F32, a0) for i in range(4)]
    b_agin = [Buf(f"ag_in{i}") for i in range(4)]
    b_agout = [Buf(f"ag_out{i}") for i in range(4)]
    ag_slot = [[Buf(f"agslot{q}_{i}") for i in range(8)] for q in range(4)]

    def emit_ag(hf):
        T.op("gpsimd", lambda e, h=hf: e.collective_compute(
            "AllGather", ALU.bypass, replica_groups=[list(range(N_CORES))],
            ins=[ag_in[h].tensor.ap().opt()], outs=[ag_out[h].tensor.ap().opt()]),
            reads=tuple(ag_slot[hf]), writes=(b_agout[hf],))

    for ti in range(NT):
        bi = ti % 2
        xft, xfb = xf[bi]
        T.dma("sync", f"xf{bi}", lambda e, t=ti, d=xft: e.dma_start(out=d[:], in_=x_my[t * 128:(t + 1) * 128, :]), writes=(xfb,))
        hf = ti // 2
        for q in range(4):
            sl = (ti * 4 + q) % 4
            pst, psb = ps_tf[sl]
            fns = []
            for k in range(4):
                kc = q * 4 + k
                fns.append(lambda e, s=xft, k2=kc, p=pst, kk=k:
                           e.transpose(p[:, kk * 128:(kk + 1) * 128], s[:, k2 * 128:(k2 + 1) * 128], c_idf[:]))
            T.op("tensor", fns, reads=(xfb, b_idf), writes=(psb,))
            ht, hb = stgh[sl]
            lt, lb = stgl[sl]
            T.op("scalar", lambda e, p=pst, d=ht: e.copy(out=d[:], in_=p[:]), reads=(psb,), writes=(hb,))
            T.op("vector", lambda e, p=pst, d1=ht, d2=lt: e.tensor_tensor(out=d2[:], in0=p[:], in1=d1[:], op=ALU.subtract),
                 reads=(psb, hb), writes=(lb,))
            T.dma("scalar", f"agwh{sl}", lambda e, s=ht, q2=q, t=ti % 2, f=hf:
                  e.dma_start(out=ag_in[f][:, 0, q2 * 4:(q2 + 1) * 4, t * 128:(t + 1) * 128],
                              in_=s[:].rearrange("p (k t) -> p k t", k=4)),
                  reads=(hb,), writes=(ag_slot[hf][sl],))
            T.dma("scalar", f"agwl{sl}", lambda e, s=lt, q2=q, t=ti % 2, f=hf:
                  e.dma_start(out=ag_in[f][:, 1, q2 * 4:(q2 + 1) * 4, t * 128:(t + 1) * 128],
                              in_=s[:].rearrange("p (k t) -> p k t", k=4)),
                  reads=(lb,), writes=(ag_slot[hf][4 + sl],))
        if ti % 2 == 1:
            emit_ag(ti // 2)

    T.barrier([x[1] for x in xf + stgh + stgl + ps_tf])
    a0.close()

    for kc in range(NKC):
        wtt, wtb = wqt[kc]
        T.op("scalar", lambda e, k=kc, s=wtt: e.copy(out=wqh[:, k * 768:(k + 1) * 768], in_=s[:]),
             reads=(wtb,), writes=(b_wqh,))
        T.op("vector", lambda e, k=kc, s=wtt: e.tensor_tensor(out=wql[:, k * 768:(k + 1) * 768], in0=s[:],
                                                              in1=wqh[:, k * 768:(k + 1) * 768], op=ALU.subtract),
             reads=(wtb, b_wqh), writes=(b_wql,))
    T.barrier([x[1] for x in wqt])
    wqtsc.close()
    am = ExitStack()

    xg = [sb(f"xg{i}", [128, 2 * NKC * 512], BF16, am) for i in range(2)]
    qkv_q, b_q = sb("qkv_q", [128, 2048], BF16, am)
    qkv_k, b_k = sb("qkv_k", [128, 2048], BF16, am)
    qkv_v, b_v = sb("qkv_v", [128, 2048], BF16, am)
    qk32 = [sb(f"qk32_{i}", [128, 512], F32, am) for i in range(2)]
    sbS, b_S = sb("sbS", [128, 256], F32, am)
    sbSh, b_Sh = sb("sbSh", [128, 256], BF16, am)
    sbSl, b_Sl = sb("sbSl", [128, 256], BF16, am)
    smf = [sb(f"smf{i}", [128, 256], F32, am) for i in range(2)]
    smh = [sb(f"smh{i}", [128, 256], BF16, am) for i in range(2)]
    sml = [sb(f"sml{i}", [128, 256], BF16, am) for i in range(2)]
    ktf = [sb(f"ktf{i}", [128, 256], F32, am) for i in range(2)]
    ktg = [sb(f"ktg{i}", [128, 256], F32, am) for i in range(2)]
    ktdh = [sb(f"ktdh{i}", [128, 256], BF16, am) for i in range(2)]
    ktdl = [sb(f"ktdl{i}", [128, 256], BF16, am) for i in range(2)]
    qdf = [sb(f"qdf{i}", [128, 256], F32, am) for i in range(2)]
    qdh = [sb(f"qdh{i}", [128, 256], BF16, am) for i in range(2)]
    qdl = [sb(f"qdl{i}", [128, 256], BF16, am) for i in range(2)]
    astg = [sb(f"astg{i}", [128, 256], F32, am) for i in range(2)]
    ps = [psum(f"psA{i}", [128, 512], F32, am) for i in range(6)]
    pkt = [psum(f"pkt{i}", [128, 512], BF16, am) for i in range(2)]
    b_ccin = [Buf(f"cc_in{i}") for i in range(4)]

    DQ0, DK0, DM0 = 0, 256, 512

    def chunk_views(c):
        pp = c % 2
        def qs(h, lo):
            o = 1024 * lo + h * 512 + c * 128
            return qkv_q[:, o:o + 128]
        def ks(h, lo):
            o = 1024 * lo + h * 512 + c * 128
            return qkv_k[:, o:o + 128]
        def vs(h, lo):
            o = 1024 * lo + c * 256 + h * 128
            return qkv_v[:, o:o + 128]
        return pp, qs, ks, vs

    def emit_chunk_front(b, g, c):
        pp, qs, ks, vs = chunk_views(c)
        psS_t, psS_b = ps[pp]
        pkt_t, pkt_b = pkt[pp]
        ktft, ktfb = ktf[pp]
        qdft, qdfb = qdf[pp]
        qdht, qdhb = qdh[pp]
        qdlt, qdlb = qdl[pp]
        smft, smfb = smf[pp]
        smht, smhb = smh[pp]
        smlt, smlb = sml[pp]
        ktdht, ktdhb = ktdh[pp]
        ktdlt, ktdlb = ktdl[pp]

        fns = []
        for h in range(2):
            fns += [
                lambda e, h2=h: e.matmul(psS_t[:, h2 * 128:(h2 + 1) * 128], ks(h2, 0), qs(h2, 0), start=True, stop=False),
                lambda e, h2=h: e.matmul(psS_t[:, h2 * 128:(h2 + 1) * 128], ks(h2, 0), qs(h2, 1), start=False, stop=False),
                lambda e, h2=h: e.matmul(psS_t[:, h2 * 128:(h2 + 1) * 128], ks(h2, 1), qs(h2, 0), start=False, stop=True),
            ]
        T.op("tensor", fns, reads=(b_k, b_q), writes=(psS_b,))
        T.op("tensor", [
            lambda e: e.transpose(pkt_t[:, 0:128], ks(0, 0), c_idb[:]),
            lambda e: e.transpose(pkt_t[:, 128:256], ks(1, 0), c_idb[:]),
            lambda e: e.transpose(pkt_t[:, 256:384], ks(0, 1), c_idb[:]),
            lambda e: e.transpose(pkt_t[:, 384:512], ks(1, 1), c_idb[:]),
        ], reads=(b_k, b_idb), writes=(pkt_b,))

        T.op("vector", [
            lambda e: e.tensor_tensor(out=qdft[:, 0:128], in0=qs(0, 0), in1=qs(0, 1), op=ALU.add),
            lambda e: e.tensor_tensor(out=qdft[:, 128:256], in0=qs(1, 0), in1=qs(1, 1), op=ALU.add),
        ], reads=(b_q,), writes=(qdfb,))
        T.op("vector", lambda e: e.tensor_tensor(out=qdft[:], in0=qdft[:], in1=c_attc[:, DQ0:DQ0 + 256], op=ALU.mult),
             reads=(qdfb, b_attc), writes=(qdfb,))
        T.op("vector", lambda e: e.tensor_copy(out=qdht[:], in_=qdft[:]), reads=(qdfb,), writes=(qdhb,))
        T.op("vector", lambda e: e.tensor_tensor(out=qdlt[:], in0=qdft[:], in1=qdht[:], op=ALU.subtract),
             reads=(qdfb, qdhb), writes=(qdlb,))

        T.op("vector", lambda e: e.tensor_tensor(out=smft[:], in0=psS_t[:, 0:256], in1=c_attc[:, DM0:DM0 + 256], op=ALU.mult),
             reads=(psS_b, b_attc), writes=(smfb,))
        T.op("vector", lambda e: e.tensor_copy(out=smht[:], in_=smft[:]), reads=(smfb,), writes=(smhb,))
        T.op("vector", lambda e: e.tensor_tensor(out=smlt[:], in0=smft[:], in1=smht[:], op=ALU.subtract),
             reads=(smfb, smhb), writes=(smlb,))

        ktgt, ktgb = ktg[pp]
        T.op("vector", [
            lambda e: e.tensor_scalar_mul(ktft[:, 0:128], pkt_t[:, 0:128], c_dk[:, 0:1]),
            lambda e: e.tensor_scalar_mul(ktft[:, 128:256], pkt_t[:, 128:256], c_dk[:, 1:2]),
        ], reads=(pkt_b, b_dk), writes=(ktfb,))
        T.op("vector", [
            lambda e: e.scalar_tensor_tensor(out=ktgt[:, 0:128], in0=pkt_t[:, 256:384], scalar=c_dk[:, 0:1], in1=ktft[:, 0:128], op0=ALU.mult, op1=ALU.add),
            lambda e: e.scalar_tensor_tensor(out=ktgt[:, 128:256], in0=pkt_t[:, 384:512], scalar=c_dk[:, 1:2], in1=ktft[:, 128:256], op0=ALU.mult, op1=ALU.add),
        ], reads=(pkt_b, b_dk, ktfb), writes=(ktgb,))
        T.op("vector", lambda e: e.tensor_copy(out=ktdht[:], in_=ktgt[:]), reads=(ktgb,), writes=(ktdhb,))
        T.op("vector", lambda e: e.tensor_tensor(out=ktdlt[:], in0=ktgt[:], in1=ktdht[:], op=ALU.subtract),
             reads=(ktgb, ktdhb), writes=(ktdlb,))

    def emit_chunk_back(b, g, c):
        gc = g * 4 + c
        pp, qs, ks, vs = chunk_views(c)
        psU_t, psU_b = ps[2 + pp]
        psV_t, psV_b = ps[4 + pp]
        qdht, qdhb = qdh[pp]
        qdlt, qdlb = qdl[pp]
        smht, smhb = smh[pp]
        smlt, smlb = sml[pp]
        ktdht, ktdhb = ktdh[pp]
        ktdlt, ktdlb = ktdl[pp]

        fns = []
        for h in range(2):
            fns += [
                lambda e, h2=h: e.matmul(psV_t[:, h2 * 128:(h2 + 1) * 128], vs(h2, 0), smht[:, h2 * 128:(h2 + 1) * 128], start=True, stop=False),
                lambda e, h2=h: e.matmul(psV_t[:, h2 * 128:(h2 + 1) * 128], vs(h2, 1), smht[:, h2 * 128:(h2 + 1) * 128], start=False, stop=False),
                lambda e, h2=h: e.matmul(psV_t[:, h2 * 128:(h2 + 1) * 128], vs(h2, 0), smlt[:, h2 * 128:(h2 + 1) * 128], start=False, stop=False),
                lambda e, h2=h: e.matmul(psV_t[:, h2 * 128:(h2 + 1) * 128], sbSh[:, h2 * 128:(h2 + 1) * 128], qdht[:, h2 * 128:(h2 + 1) * 128], start=False, stop=False),
                lambda e, h2=h: e.matmul(psV_t[:, h2 * 128:(h2 + 1) * 128], sbSl[:, h2 * 128:(h2 + 1) * 128], qdht[:, h2 * 128:(h2 + 1) * 128], start=False, stop=False),
                lambda e, h2=h: e.matmul(psV_t[:, h2 * 128:(h2 + 1) * 128], sbSh[:, h2 * 128:(h2 + 1) * 128], qdlt[:, h2 * 128:(h2 + 1) * 128], start=False, stop=True),
            ]
        T.op("tensor", fns, reads=(b_v, smhb, smlb, b_Sh, b_Sl, qdhb, qdlb), writes=(psV_b,))
        fns = []
        for h in range(2):
            fns += [
                lambda e, h2=h: e.matmul(psU_t[:, h2 * 128:(h2 + 1) * 128], ktdht[:, h2 * 128:(h2 + 1) * 128], vs(h2, 0), start=True, stop=False),
                lambda e, h2=h: e.matmul(psU_t[:, h2 * 128:(h2 + 1) * 128], ktdlt[:, h2 * 128:(h2 + 1) * 128], vs(h2, 0), start=False, stop=False),
                lambda e, h2=h: e.matmul(psU_t[:, h2 * 128:(h2 + 1) * 128], ktdht[:, h2 * 128:(h2 + 1) * 128], vs(h2, 1), start=False, stop=True),
            ]
        T.op("tensor", fns, reads=(ktdhb, ktdlb, b_v), writes=(psU_b,))

        T.op("vector", [
            lambda e: e.scalar_tensor_tensor(out=sbS[:, 0:128], in0=sbS[:, 0:128], scalar=c_lamc[:, 0:1], in1=psU_t[:, 0:128], op0=ALU.mult, op1=ALU.add),
            lambda e: e.scalar_tensor_tensor(out=sbS[:, 128:256], in0=sbS[:, 128:256], scalar=c_lamc[:, 1:2], in1=psU_t[:, 128:256], op0=ALU.mult, op1=ALU.add),
        ], reads=(psU_b, b_lamc), writes=(b_S,))
        T.op("vector", lambda e: e.tensor_copy(out=sbSh[:], in_=sbS[:]), reads=(b_S,), writes=(b_Sh,))
        T.op("vector", lambda e: e.tensor_tensor(out=sbSl[:], in0=sbS[:], in1=sbSh[:], op=ALU.subtract),
             reads=(b_S, b_Sh), writes=(b_Sl,))

        stt, stb = astg[pp]
        T.op("scalar", lambda e: e.copy(out=stt[:], in_=psV_t[:, 0:256]), reads=(psV_b,), writes=(stb,))
        j, toff = gc // 2, (gc % 2) * 128
        T.dma("gpsimd", f"ccw{pp}", lambda e, j2=j, t=toff, s=stt:
              e.dma_start(out=cc_in[b][j2, :, :, t:t + 128], in_=s[:].rearrange("p (h t) -> p h t", h=2)),
              reads=(stb,), writes=(b_ccin[b],))

    def emit_group(b, g):
        gi = b * 4 + g
        bi = gi % 2
        xgt, xgb = xg[bi]
        for hl in range(2):
            for si in range(2):
                seg = 2 * g + si
                T.dma("sync", f"xg{bi}_{hl}", lambda e, h=hl, sg=seg, s2=si, d=xgt, f=b:
                      e.dma_start(out=d[:, h * NKC * 512:(h + 1) * NKC * 512]
                                  .rearrange("p (k t) -> p k t", k=NKC)[:, :, s2 * 256:s2 * 256 + 256],
                                  in_=ag_out[f][sg, :, h, :, :]),
                      reads=(b_agout[b],), writes=(xgb,))

        for fc in range(4):
            pst, psb = ps[fc]
            fns = []
            for kc in range(NKC):
                for p in range(3):
                    wt = wqh if p in (0, 2) else wql
                    xs = 0 if p in (0, 1) else 1
                    fns.append(lambda e, w=wt, k=kc, f=fc, x=xs, d=xgt, bk=pst, st=(kc == 0 and p == 0), sp=(kc == NKC - 1 and p == 2):
                               e.matmul(bk[:], w[:, k * 768 + f * 128:k * 768 + (f + 1) * 128],
                                        d[:, x * NKC * 512 + k * 512:x * NKC * 512 + (k + 1) * 512], start=st, stop=sp))
            T.op("tensor", fns, reads=(xgb, b_wqh, b_wql), writes=(psb,))
        for sl in range(4):
            bank, half = 4 + sl // 2, (sl % 2) * 256
            pst, psb = ps[bank]
            fns = []
            for kc in range(NKC):
                for p in range(3):
                    wt = wqh if p in (0, 2) else wql
                    xs = 0 if p in (0, 1) else 1
                    fns.append(lambda e, w=wt, k=kc, s=sl, bk=pst, hf=half, x=xs, d=xgt, st=(kc == 0 and p == 0):
                               e.matmul(bk[:, hf:hf + 256], d[:, x * NKC * 512 + k * 512 + s * 128:x * NKC * 512 + k * 512 + (s + 1) * 128],
                                        w[:, k * 768 + 512:(k + 1) * 768], start=st, stop=False))
            fns.append(lambda e, bk=pst, hf=half: e.matmul(bk[:, hf:hf + 256], c_ones[0:1, 0:128], c_bvh[0:1, :], start=False, stop=False))
            fns.append(lambda e, bk=pst, hf=half: e.matmul(bk[:, hf:hf + 256], c_ones[0:1, 0:128], c_bvl[0:1, :], start=False, stop=True))
            T.op("tensor", fns, reads=(xgb, b_wqh, b_wql, b_ones, b_bvh, b_bvl), writes=(psb,))

        for fc in range(4):
            dstt, dstb = (qkv_q, b_q) if fc < 2 else (qkv_k, b_k)
            h = fc % 2
            pst, psb = ps[fc]
            s32t, s32b = qk32[fc % 2]
            T.op("vector", lambda e, p=pst, d=s32t, f=fc: e.tensor_scalar_add(d[:], p[:], c_bqk[:, f:f + 1]),
                 reads=(psb, b_bqk), writes=(s32b,))
            T.op("scalar", lambda e, s=s32t, d=dstt, h2=h: e.copy(out=d[:, h2 * 512:(h2 + 1) * 512], in_=s[:]),
                 reads=(s32b,), writes=(dstb,))
            T.op("vector", lambda e, s=s32t, d=dstt, h2=h: e.tensor_tensor(out=d[:, 1024 + h2 * 512:1024 + (h2 + 1) * 512], in0=s[:],
                                                                           in1=d[:, h2 * 512:(h2 + 1) * 512], op=ALU.subtract),
                 reads=(s32b, dstb), writes=(dstb,))
        for sl in range(4):
            bank, half = 4 + sl // 2, (sl % 2) * 256
            pst, psb = ps[bank]
            T.op("scalar", lambda e, p=pst, hf=half, s=sl: e.copy(out=qkv_v[:, s * 256:(s + 1) * 256], in_=p[:, hf:hf + 256]),
                 reads=(psb,), writes=(b_v,))
            T.op("vector", lambda e, p=pst, hf=half, s=sl: e.tensor_tensor(out=qkv_v[:, 1024 + s * 256:1024 + (s + 1) * 256],
                                                                           in0=p[:, hf:hf + 256], in1=qkv_v[:, s * 256:(s + 1) * 256], op=ALU.subtract),
                 reads=(psb,), writes=(b_v,))

        emit_chunk_front(b, g, 0)
        emit_chunk_front(b, g, 1)
        emit_chunk_back(b, g, 0)
        emit_chunk_front(b, g, 2)
        emit_chunk_back(b, g, 1)
        emit_chunk_front(b, g, 3)
        emit_chunk_back(b, g, 2)
        emit_chunk_back(b, g, 3)

    b_ccout = [Buf(f"cc_out{i}") for i in range(4)]

    def _a2a(b2):
        def f(e):
            with allow_ici_all_to_all():
                return e.collective_compute("AllToAll", ALU.bypass, replica_groups=[list(range(N_CORES))],
                                            ins=[cc_in[b2].tensor.ap().opt()], outs=[cc_out[b2].tensor.ap().opt()])
        return f

    for b in range(B):
        T.op("vector", lambda e: e.memset(sbS[:], 0.0), writes=(b_S,))
        T.op("vector", lambda e: e.memset(sbSh[:], 0.0), writes=(b_Sh,))
        T.op("vector", lambda e: e.memset(sbSl[:], 0.0), writes=(b_Sl,))
        for g in range(4):
            emit_group(b, g)
        T.op("gpsimd", _a2a(b), reads=(b_ccin[b],), writes=(b_ccout[b],))

    T.barrier([b_wqh, b_wql, b_q, b_k, b_v, b_S, b_Sh, b_Sl] +
              [x[1] for x in wqt + xg + qk32 + smf + smh + sml + ktf + ktg + ktdh + ktdl + qdf + qdh + qdl + astg + ps + pkt])
    am.close()
    wqsc.close()

    pb_ = ExitStack()
    wo_h, b_woh = sb("wo_h", [128, NKC * 1024], BF16, pb_)
    wo_l, b_wol = sb("wo_l", [128, NKC * 1024], BF16, pb_)
    wo_t = [sb(f"wo_t{i}", [128, 1024], F32, pb_) for i in range(3)]
    at32 = [sb(f"at32_{i}", [128, 2048], F32, pb_) for i in range(2)]
    b_a32p = [[Buf(f"a32_{i}_{p}") for p in range(2)] for i in range(2)]
    ath = [sb(f"ath{i}", [128, 2048], BF16, pb_) for i in range(2)]
    atl = [sb(f"atl{i}", [128, 2048], BF16, pb_) for i in range(2)]
    xmt = [sb(f"xmt{i}", [128, 1024], F32, pb_) for i in range(2)]
    ystg = [sb(f"ystg{i}", [128, 1024], F32, pb_) for i in range(2)]
    psB = [psum(f"psB{i}", [128, 1024], F32, pb_) for i in range(3)]
    b_yscr = Buf("y_scr")

    stat, b_stat = sb("stat", [128, 16], F32, pb_)
    yt = [sb(f"yt{i}", [128, 2048], F32, pb_) for i in range(2)]
    x2t = [sb(f"x2t{i}", [128, 2048], F32, pb_) for i in range(2)]
    x2h = [sb(f"x2h{i}", [128, 2048], BF16, pb_) for i in range(3)]
    x2l = [sb(f"x2l{i}", [128, 2048], BF16, pb_) for i in range(3)]
    x2stg = [sb(f"x2stg{i}", [128, 1024], BF16, pb_) for i in range(2)]
    ps_x2 = [psum(f"ps_x2{i}", [128, 1024], BF16, pb_) for i in range(2)]
    b_x2scr = Buf("x2_scr")
    b_x2fm = Buf("x2fm_scr")

    def emit_ln(engV, y_src, y_b, st_t, st_b, out_t, out_b):
        T.op("scalar", lambda e: e.activation(out=out_t[:], in_=y_src[:], func=ACT.Square, accum_out=st_t[:, 1:2]),
             reads=(y_b,), writes=(out_b, st_b))
        T.op("scalar", lambda e: e.activation(out=out_t[:], in_=y_src[:], func=ACT.Copy, accum_out=st_t[:, 0:1]),
             reads=(y_b,), writes=(out_b, st_b))
        T.op("vector", lambda e: e.tensor_scalar_mul(st_t[:, 2:3], st_t[:, 0:1], 1.0 / D), reads=(st_b,), writes=(st_b,), self_sync=True)
        T.op("vector", lambda e: e.tensor_scalar_mul(st_t[:, 3:4], st_t[:, 1:2], 1.0 / D), reads=(st_b,), writes=(st_b,), self_sync=True)
        T.op("vector", lambda e: e.tensor_tensor(out=st_t[:, 4:5], in0=st_t[:, 2:3], in1=st_t[:, 2:3], op=ALU.mult),
             reads=(st_b,), writes=(st_b,), self_sync=True)
        T.op("vector", lambda e: e.tensor_tensor(out=st_t[:, 5:6], in0=st_t[:, 3:4], in1=st_t[:, 4:5], op=ALU.subtract),
             reads=(st_b,), writes=(st_b,), self_sync=True)
        T.op("vector", lambda e: e.tensor_scalar_add(st_t[:, 6:7], st_t[:, 5:6], EPS), reads=(st_b,), writes=(st_b,), self_sync=True)
        T.op("scalar", lambda e: e.activation(out=st_t[:, 7:8], in_=st_t[:, 6:7], func=ACT.Sqrt), reads=(st_b,), writes=(st_b,))
        T.op("vector", lambda e: e.reciprocal(st_t[:, 8:9], st_t[:, 7:8]), reads=(st_b,), writes=(st_b,), self_sync=True)
        T.op("vector", lambda e: e.tensor_tensor(out=st_t[:, 9:10], in0=st_t[:, 2:3], in1=st_t[:, 8:9], op=ALU.mult),
             reads=(st_b,), writes=(st_b,), self_sync=True)
        T.op("vector", lambda e: e.tensor_scalar_mul(st_t[:, 10:11], st_t[:, 9:10], -1.0), reads=(st_b,), writes=(st_b,), self_sync=True)
        T.op("vector", lambda e: e.tensor_scalar(out=out_t[:], in0=y_src[:], scalar1=st_t[:, 8:9], scalar2=st_t[:, 10:11],
                                                 op0=ALU.mult, op1=ALU.add),
             reads=(y_b, st_b), writes=(out_b,))

    def emit_ln1_stats(ti):
        bi = ti % 2
        ytt, ytb = yt[bi]
        x2tt, x2tb = x2t[bi]
        emit_ln("vector", ytt, ytb, stat, b_stat, x2tt, x2tb)
        T.dma("gpsimd", f"x2w{bi}", lambda e, t=ti, s=x2tt: e.dma_start(out=x2_scr[t * 128:(t + 1) * 128, :], in_=s[:]),
              reads=(x2tb,), writes=(b_x2scr,))
        x2ht, x2hb = x2h[ti % 3]
        x2lt, x2lb = x2l[ti % 3]
        T.op("scalar", lambda e, s=x2tt, d=x2ht: e.copy(out=d[:], in_=s[:]), reads=(x2tb,), writes=(x2hb,))
        T.op("vector", lambda e, s=x2tt, d1=x2ht, d2=x2lt: e.tensor_tensor(out=d2[:], in0=s[:], in1=d1[:], op=ALU.subtract),
             reads=(x2tb, x2hb), writes=(x2lb,))

    def emit_ln1_transposes(ti):
        x2ht, x2hb = x2h[ti % 3]
        x2lt, x2lb = x2l[ti % 3]
        for grp in range(4):
            hl, oct_ = grp // 2, grp % 2
            pbk = (ti * 4 + grp) % 2
            pst, psb = ps_x2[pbk]
            srct, srcb = (x2ht, x2hb) if hl == 0 else (x2lt, x2lb)
            fns = []
            for k in range(8):
                kc = oct_ * 8 + k
                fns.append(lambda e, s=srct, k2=kc, p=pst, kk=k:
                           e.transpose(p[:, kk * 128:(kk + 1) * 128], s[:, k2 * 128:(k2 + 1) * 128], c_idb[:]))
            T.op("tensor", fns, reads=(srcb, b_idb), writes=(psb,))
            stt, stb = x2stg[pbk]
            T.op("scalar", lambda e, p=pst, d=stt: e.copy(out=d[:], in_=p[:]), reads=(psb,), writes=(stb,))
            T.dma("gpsimd", f"x2fw{pbk}", lambda e, s=stt, h=hl, o=oct_, t=ti:
                  e.dma_start(out=x2fm_scr[h, :, o * 8:(o + 1) * 8, t * 128:(t + 1) * 128],
                              in_=s[:].rearrange("p (k t) -> p k t", k=8)),
                  reads=(stb,), writes=(b_x2fm,))

    for half in range(2):
        for kc in range(NKC):
            wtt, wtb = wo_t[kc % 3]
            T.dma("sync", f"wof{kc % 3}", lambda e, k=kc, h=half, d=wtt:
                  e.dma_start(out=d[:], in_=woutT[k * 128:(k + 1) * 128, h * 1024:(h + 1) * 1024]), writes=(wtb,))
            T.op("scalar", lambda e, k=kc, s=wtt: e.copy(out=wo_h[:, k * 1024:(k + 1) * 1024], in_=s[:]),
                 reads=(wtb,), writes=(b_woh,))
            T.op("vector", lambda e, k=kc, s=wtt: e.tensor_tensor(out=wo_l[:, k * 1024:(k + 1) * 1024], in0=s[:],
                                                                  in1=wo_h[:, k * 1024:(k + 1) * 1024], op=ALU.subtract),
                 reads=(wtb, b_woh), writes=(b_wol,))
        def emit_at_prep(ti):
            bi = ti % 2
            a32t, a32b = at32[bi]
            atht, athb = ath[bi]
            atlt, atlb = atl[bi]
            for j in range(8):
                T.dma("sync", f"at{bi}_{j % 2}", lambda e, j2=j, b2=ti // 2, t2=(ti % 2) * 128, d=a32t:
                      e.dma_start(out=d[:, j2 * 256:(j2 + 1) * 256].rearrange("p (h t) -> p h t", h=2),
                                  in_=cc_out[b2][j2, :, :, t2:t2 + 128]),
                      reads=(b_ccout[ti // 2],), writes=(b_a32p[bi][j % 2],))
            a32rd = (b_a32p[bi][0], b_a32p[bi][1])
            T.op("scalar", lambda e, s=a32t, d=atht: e.copy(out=d[:], in_=s[:]), reads=a32rd, writes=(athb,))
            T.op("vector", lambda e, s=a32t, d1=atht, d2=atlt: e.tensor_tensor(out=d2[:], in0=s[:], in1=d1[:], op=ALU.subtract),
                 reads=a32rd + (athb,), writes=(atlb,))
            xmtt, xmtb = xmt[bi]
            T.dma("sync", f"xm{bi}", lambda e, t=ti, h=half, d=xmtt:
                  e.dma_start(out=d[:], in_=x_my[t * 128:(t + 1) * 128, h * 1024:(h + 1) * 1024]), writes=(xmtb,))

        emit_at_prep(0)
        for ti in range(NT):
            bi = ti % 2
            atht, athb = ath[bi]
            atlt, atlb = atl[bi]
            xmtt, xmtb = xmt[bi]
            if half == 1:
                ytt0, ytb0 = yt[bi]
                T.dma("sync", f"yld{bi}", lambda e, t=ti, d=ytt0: e.dma_start(out=d[:, 0:1024], in_=y_scr[t * 128:(t + 1) * 128, 0:1024]),
                      reads=(b_yscr,), writes=(ytb0,))
            pst, psb = psB[ti % 3]
            fns = []
            for ncc in range(2):
                for kc in range(NKC):
                    for p in range(3):
                        wt = wo_h if p in (0, 2) else wo_l
                        asrc = atht if p in (0, 1) else atlt
                        fns.append(lambda e, w=wt, a=asrc, k=kc, n=ncc, bk=pst, st=(kc == 0 and p == 0):
                                   e.matmul(bk[:, n * 512:(n + 1) * 512], a[:, k * 128:(k + 1) * 128],
                                            w[:, k * 1024 + n * 512:k * 1024 + (n + 1) * 512], start=st, stop=False))
                fns.append(lambda e, n=ncc, h=half, bk=pst: e.matmul(bk[:, n * 512:(n + 1) * 512], c_ones[0:1, 0:128],
                                                                     c_bouth[0:1, h * 1024 + n * 512:h * 1024 + (n + 1) * 512], start=False, stop=False))
                fns.append(lambda e, n=ncc, h=half, bk=pst: e.matmul(bk[:, n * 512:(n + 1) * 512], c_ones[0:1, 0:128],
                                                                     c_boutl[0:1, h * 1024 + n * 512:h * 1024 + (n + 1) * 512], start=False, stop=True))
            T.op("tensor", fns, reads=(athb, atlb, b_woh, b_wol, b_ones, b_bouth, b_boutl), writes=(psb,))
            if ti + 1 < NT:
                emit_at_prep(ti + 1)
            if half == 0:
                yst, ysb = ystg[bi]
                T.op("vector", lambda e, p=pst, x=xmtt, d=yst: e.scalar_tensor_tensor(out=d[:], in0=p[:], scalar=1.0, in1=x[:], op0=ALU.mult, op1=ALU.add),
                     reads=(psb, xmtb), writes=(ysb,))
                T.dma("gpsimd", f"yw{bi}", lambda e, t=ti, s=yst:
                      e.dma_start(out=y_scr[t * 128:(t + 1) * 128, 0:1024], in_=s[:]), reads=(ysb,), writes=(b_yscr,))
            else:
                ytt0, ytb0 = yt[bi]
                T.op("vector", lambda e, p=pst, x=xmtt, d=ytt0: e.scalar_tensor_tensor(out=d[:, 1024:2048], in0=p[:], scalar=1.0, in1=x[:], op0=ALU.mult, op1=ALU.add),
                     reads=(psb, xmtb, ytb0), writes=(ytb0,))
                emit_ln1_stats(ti)
                if ti >= 2:
                    emit_ln1_transposes(ti - 2)
                if ti == NT - 1:
                    emit_ln1_transposes(ti - 1)
                    emit_ln1_transposes(ti)

    T.barrier([b_woh, b_wol, b_stat] +
              [x[1] for x in wo_t + at32 + ath + atl + xmt + ystg + psB + yt + x2t + x2h + x2l + x2stg + ps_x2])
    pb_.close()

    pc = ExitStack()
    x2f, b_x2f = sb("x2f", [128, 2 * NKC * 512], BF16, pc)
    hbuf, b_h = sb("hbuf", [128, 2 * NHC * 512], BF16, pc)
    wg = [sb(f"wg{i}", [128, NKC * 128], BF16, pc) for i in range(2)]
    wu = [sb(f"wu{i}", [128, NKC * 128], BF16, pc) for i in range(2)]
    wd = [sb(f"wd{i}", [128, D], BF16, pc) for i in range(3)]
    gsil, b_gsil = sb("gsil", [128, 512], F32, pc)
    utmp, b_utmp = sb("utmp", [128, 512], F32, pc)
    h32, b_h32 = sb("h32", [128, 512], F32, pc)
    csx2, b_csx2 = sb("csx2", [1, 512], BF16, pc)
    csh, b_csh = sb("csh", [1, 512], BF16, pc)
    x2r, b_x2r = sb("x2r", [128, 2048], F32, pc)
    y2 = [sb(f"y2t{i}", [128, 2048], F32, pc) for i in range(2)]
    otst, b_ot = x2r, b_x2r
    st2, b_st2 = sb("st2", [128, 16], F32, pc)
    psC = [psum(f"psC{i}", [128, 512], F32, pc) for i in range(8)]
    b_out = Buf("out_my")

    wg_preloaded = set()

    def emit_x2f_loads(th):
        for hl in range(2):
            T.dma("sync", f"x2f{hl}", lambda e, h=hl, t=th:
                  e.dma_start(out=x2f[:, h * NKC * 512:(h + 1) * NKC * 512].rearrange("p (k t) -> p k t", k=NKC),
                              in_=x2fm_scr[h, :, :, t * 512:(t + 1) * 512]),
                  reads=(b_x2fm,), writes=(b_x2f,))

    emit_x2f_loads(0)
    for th in range(2):
        pst, psb = psC[0]
        fns = []
        for hl in range(2):
            for kc in range(NKC):
                fns.append(lambda e, h=hl, k=kc, bk=pst, st=(hl == 0 and kc == 0), sp=(hl == 1 and kc == NKC - 1):
                           e.matmul(bk[0:1, :], c_onec[:, 0:1], x2f[:, h * NKC * 512 + k * 512:h * NKC * 512 + (k + 1) * 512], start=st, stop=sp))
        T.op("tensor", fns, reads=(b_x2f, b_onec), writes=(psb,))
        T.op("scalar", lambda e, bk=pst: e.copy(out=csx2[0:1, :], in_=bk[0:1, :]), reads=(psb,), writes=(b_csx2,))

        for hc in range(NHC):
            bi = hc % 2
            wgt, wgb = wg[bi]
            wut, wub = wu[bi]
            if (th, hc) not in wg_preloaded:
                T.dma("gpsimd", f"wg{bi}", lambda e, h=hc, d=wgt: e.dma_start(out=d[:].rearrange("p (k t) -> p k t", k=NKC), in_=wgT_t[h, :, :, :]),
                      writes=(wgb,))
                T.dma("gpsimd", f"wu{bi}", lambda e, h=hc, d=wut: e.dma_start(out=d[:].rearrange("p (k t) -> p k t", k=NKC), in_=wuT_t[h, :, :, :]),
                      writes=(wub,))
            pg_t, pg_b = psC[2 + bi]
            pu_t, pu_b = psC[4 + bi]
            for (wt, wb_, pt, pb2) in ((wgt, wgb, pg_t, pg_b), (wut, wub, pu_t, pu_b)):
                fns = []
                for kc in range(NKC):
                    for hl in range(2):
                        fns.append(lambda e, w=wt, k=kc, h=hl, bk=pt, st=(kc == 0 and hl == 0):
                                   e.matmul(bk[:], w[:, k * 128:(k + 1) * 128],
                                            x2f[:, h * NKC * 512 + k * 512:h * NKC * 512 + (k + 1) * 512], start=st, stop=False))
                fns.append(lambda e, bk=pt: e.matmul(bk[:], c_n128[0:1, 0:128], csx2[0:1, :], start=False, stop=True))
                T.op("tensor", fns, reads=(wb_, b_x2f, b_n128, b_csx2), writes=(pb2,))
            T.op("scalar", lambda e, p=pg_t, h=hc: e.activation(out=gsil[:], in_=p[:], func=ACT.Silu, scale=c_sg[:, h:h + 1]),
                 reads=(pg_b, b_sg), writes=(b_gsil,))
            T.op("vector", lambda e, p=pu_t, h=hc: e.tensor_scalar_mul(utmp[:], p[:], c_su[:, h:h + 1]),
                 reads=(pu_b, b_su), writes=(b_utmp,))
            T.op("vector", lambda e: e.tensor_tensor(out=h32[:], in0=gsil[:], in1=utmp[:], op=ALU.mult),
                 reads=(b_gsil, b_utmp), writes=(b_h32,))
            T.op("scalar", lambda e, h=hc: e.copy(out=hbuf[:, h * 512:(h + 1) * 512], in_=h32[:]),
                 reads=(b_h32,), writes=(b_h,))
            T.op("vector", lambda e, h=hc: e.tensor_tensor(out=hbuf[:, NHC * 512 + h * 512:NHC * 512 + (h + 1) * 512],
                                                           in0=h32[:], in1=hbuf[:, h * 512:(h + 1) * 512], op=ALU.subtract),
                 reads=(b_h32,), writes=(b_h,))

        pst, psb = psC[1]
        fns = []
        for hl in range(2):
            for hcc in range(NHC):
                fns.append(lambda e, h=hl, k=hcc, bk=pst, st=(hl == 0 and hcc == 0), sp=(hl == 1 and hcc == NHC - 1):
                           e.matmul(bk[0:1, :], c_onec[:, 0:1], hbuf[:, h * NHC * 512 + k * 512:h * NHC * 512 + (k + 1) * 512], start=st, stop=sp))
        T.op("tensor", fns, reads=(b_h, b_onec), writes=(psb,))
        T.op("scalar", lambda e, bk=pst: e.copy(out=csh[0:1, :], in_=bk[0:1, :]), reads=(psb,), writes=(b_csh,))

        if th == 0:
            emit_x2f_loads(1)
            for hc2 in (0, 1):
                bi2 = hc2 % 2
                wgt2, wgb2 = wg[bi2]
                wut2, wub2 = wu[bi2]
                T.dma("gpsimd", f"wg{bi2}", lambda e, h=hc2, d=wgt2: e.dma_start(out=d[:].rearrange("p (k t) -> p k t", k=NKC), in_=wgT_t[h, :, :, :]),
                      writes=(wgb2,))
                T.dma("gpsimd", f"wu{bi2}", lambda e, h=hc2, d=wut2: e.dma_start(out=d[:].rearrange("p (k t) -> p k t", k=NKC), in_=wuT_t[h, :, :, :]),
                      writes=(wub2,))
                wg_preloaded.add((1, hc2))

        for tp in range(2):
            for kc in range(NHC):
                bi = kc % 3
                wdt, wdb = wd[bi]
                T.dma("gpsimd", f"wd{bi}", lambda e, k=kc, d=wdt: e.dma_start(out=d[:], in_=wdT[k * 128:(k + 1) * 128, :]), writes=(wdb,))
                fns = []
                for ti_in in range(2):
                    tloc = (tp * 2 + ti_in) * 128
                    for hl in range(2):
                        for ncc in range(4):
                            bk = ti_in * 4 + ncc
                            fns.append(lambda e, w=wdt, k2=kc, h=hl, t=tloc, n=ncc, p=psC[bk][0], st=(kc == 0 and hl == 0):
                                       e.matmul(p[:], hbuf[:, h * NHC * 512 + k2 * 512 + t:h * NHC * 512 + k2 * 512 + t + 128],
                                                w[:, n * 512:(n + 1) * 512], start=st, stop=False))
                T.op("tensor", fns, reads=(wdb, b_h), writes=tuple(psC[i][1] for i in range(8)))
            fns = []
            for ti_in in range(2):
                tloc = (tp * 2 + ti_in) * 128
                for ncc in range(4):
                    bk = ti_in * 4 + ncc
                    fns.append(lambda e, t=tloc, p=psC[bk][0]: e.matmul(p[:], csh[0:1, t:t + 128], c_n128[0:1, :], start=False, stop=True))
            T.op("tensor", fns, reads=(b_csh, b_n128), writes=tuple(psC[i][1] for i in range(8)))

            for ti_in in range(2):
                ti = th * 4 + tp * 2 + ti_in
                y2t, b_y2t = y2[ti_in]
                T.dma("sync", "x2r", lambda e, t=ti, d=x2r: e.dma_start(out=d[:], in_=x2_scr[t * 128:(t + 1) * 128, :]),
                      reads=(b_x2scr,), writes=(b_x2r,))
                for ncc in range(4):
                    bk = ti_in * 4 + ncc
                    pt, pb2 = psC[bk]
                    T.op("vector", lambda e, p=pt, n=ncc, d=y2t: e.tensor_tensor(out=d[:, n * 512:(n + 1) * 512], in0=p[:],
                                                                                 in1=c_sdb[:, n * 512:(n + 1) * 512], op=ALU.mult),
                         reads=(pb2, b_sdb), writes=(b_y2t,))
                T.op("vector", lambda e, d=y2t: e.tensor_tensor(out=d[:], in0=d[:], in1=x2r[:], op=ALU.add),
                     reads=(b_y2t, b_x2r), writes=(b_y2t,))
            for ti_in in range(2):
                ti = th * 4 + tp * 2 + ti_in
                y2t, b_y2t = y2[ti_in]
                emit_ln("vector", y2t, b_y2t, st2, b_st2, otst, b_ot)
                T.dma("sync", "outw", lambda e, t=ti, s=otst: e.dma_start(out=out_my[t * 128:(t + 1) * 128, :], in_=s[:]),
                      reads=(b_ot,), writes=(b_out,))

    h = T.sems["d_outw"]
    v = T.count["d_outw"]
    T.ops["sync"].append(lambda e, h=h, v=v: e.wait_ge(h, v))

    with nc.Block() as block:
        @block.sync
        def _(e):
            for f in T.ops["sync"]:
                f(e)

        @block.gpsimd
        def _(e):
            for f in T.ops["gpsimd"]:
                f(e)

        @block.tensor
        def _(e):
            for f in T.ops["tensor"]:
                f(e)

        @block.vector
        def _(e):
            for f in T.ops["vector"]:
                f(e)

        @block.scalar
        def _(e):
            for f in T.ops["scalar"]:
                f(e)

    pc.close()
    es.close()
    return nc


_NC_CACHE = []


def kernel(L0_attn_x, L0_attn_w_qkv, L0_attn_b_qkv, L0_attn_w_out, L0_attn_b_out,
           L0_ffn_wq_gate, L0_ffn_s_gate, L0_ffn_wq_up, L0_ffn_s_up,
           L0_ffn_wq_down, L0_ffn_s_down):
    x = np.ascontiguousarray(np.asarray(L0_attn_x, dtype=np.float32))
    w_qkv = np.asarray(L0_attn_w_qkv, dtype=np.float32)
    b_qkv = np.asarray(L0_attn_b_qkv, dtype=np.float32)
    w_out = np.asarray(L0_attn_w_out, dtype=np.float32)
    b_out = np.asarray(L0_attn_b_out, dtype=np.float32)
    wq_gate = np.asarray(L0_ffn_wq_gate, dtype=np.int32)
    wq_up = np.asarray(L0_ffn_wq_up, dtype=np.int32)
    wq_down = np.asarray(L0_ffn_wq_down, dtype=np.int32)
    s_gate = np.asarray(L0_ffn_s_gate, dtype=np.float32)
    s_up = np.asarray(L0_ffn_s_up, dtype=np.float32)
    s_down = np.asarray(L0_ffn_s_down, dtype=np.float32)

    woutT = np.ascontiguousarray(w_out.T)
    bo_h, bo_l = _hi_lo(b_out)
    wgT_t = np.ascontiguousarray(wq_gate.reshape(NHC, 128, NKC, 128).transpose(0, 3, 2, 1))
    wuT_t = np.ascontiguousarray(wq_up.reshape(NHC, 128, NKC, 128).transpose(0, 3, 2, 1))
    wdT = np.ascontiguousarray(wq_down.T)
    sg_pp = np.ascontiguousarray(s_gate.reshape(NHC, 128).T)
    su_pp = np.ascontiguousarray(s_up.reshape(NHC, 128).T)
    sdb = np.ascontiguousarray(np.broadcast_to(s_down, (128, D)))
    identf = np.eye(128, dtype=np.float32)
    identb = np.eye(128, dtype=ml_dtypes.bfloat16)
    ones_row = np.ones((1, 512), dtype=ml_dtypes.bfloat16)
    n128_row = np.full((1, 512), -128.0, dtype=ml_dtypes.bfloat16)
    ones_colb = np.ones((128, 1), dtype=ml_dtypes.bfloat16)

    slopes = _slopes16()
    lam = np.exp(-slopes)
    pos = np.arange(128, dtype=np.float64)
    diff = pos[:, None] - pos[None, :]

    in_maps = []
    for r in range(N_CORES):
        x_my = np.ascontiguousarray(x[:, r * 256:(r + 1) * 256, :].reshape(TOK, D))
        h0, h1 = 2 * r, 2 * r + 1
        rows = np.concatenate([np.arange(h0 * 128, (h1 + 1) * 128),
                               2048 + np.arange(h0 * 128, (h1 + 1) * 128),
                               4096 + np.arange(h0 * 128, (h1 + 1) * 128)])
        wqkvT = np.ascontiguousarray(w_qkv[rows, :].T)
        bq = b_qkv[rows]
        bqk_ = np.stack([bq[0:128], bq[128:256], bq[256:384], bq[384:512]], axis=1).astype(np.float32)
        bvh, bvl = _hi_lo(bq[512:768])
        attc = np.zeros((128, 6 * 128), dtype=np.float32)
        lamc = np.zeros((128, 2), dtype=np.float32)
        dkpp = np.zeros((128, 2), dtype=np.float32)
        for hl, hh in enumerate((h0, h1)):
            lh = lam[hh]
            attc[:, hl * 128:(hl + 1) * 128] = (lh ** (pos + 1.0))[None, :]
            attc[:, 256 + hl * 128:256 + (hl + 1) * 128] = (lh ** (127.0 - pos))[None, :]
            dmT = np.where(diff >= 0, lh ** np.maximum(diff, 0.0), 0.0).T
            attc[:, 512 + hl * 128:512 + (hl + 1) * 128] = dmT.astype(np.float32)
            lamc[:, hl] = lh ** 128.0
            dkpp[:, hl] = (lh ** (127.0 - pos)).astype(np.float32)
        in_maps.append(dict(
            x_my=x_my, wqkvT=wqkvT, bqk=bqk_,
            bv_h=bvh.reshape(1, 256), bv_l=bvl.reshape(1, 256),
            woutT=woutT, bout_h=bo_h.reshape(1, D), bout_l=bo_l.reshape(1, D),
            wgT_t=wgT_t, wuT_t=wuT_t, wdT=wdT,
            sg_pp=sg_pp, su_pp=su_pp, sdb=sdb,
            attc=attc, lamc=lamc, dk_pp=dkpp,
            identf=identf, identb=identb,
            ones_row=ones_row, n128_row=n128_row, ones_colb=ones_colb,
        ))

    if not _NC_CACHE:
        _NC_CACHE.append(build_nc())
    nc = _NC_CACHE[0]
    res = run_bass_kernel_spmd(nc, in_maps, core_ids=list(range(N_CORES)))
    out = np.empty((B, N, D), dtype=np.float32)
    for r in range(N_CORES):
        out[:, r * 256:(r + 1) * 256, :] = res.results[r]["out_my"].reshape(B, 256, D)
    return out


# version 19
# speedup vs baseline: 1.0996x; 1.0093x over previous
import math
from contextlib import ExitStack

import numpy as np
import ml_dtypes

import concourse.bass as bass
import concourse.mybir as mybir
from concourse.bass_utils import run_bass_kernel_spmd
from concourse.aot_env import allow_ici_all_to_all

F32 = mybir.dt.float32
BF16 = mybir.dt.bfloat16
I32 = mybir.dt.int32
ACT = mybir.ActivationFunctionType
ALU = mybir.AluOpType

N_CORES = 8
B, N, D = 4, 2048, 2048
HID = 5632
NKC = D // 128
NHC = HID // 128
TOK = 1024
NT = TOK // 128
EPS = 1e-5

ENGINES = ("sync", "gpsimd", "tensor", "vector", "scalar")


class Buf:
    __slots__ = ("name", "w", "readers")

    def __init__(self, name):
        self.name = name
        self.w = None
        self.readers = []


class Tracker:
    def __init__(self, nc, es, needed=None):
        self.nc = nc
        self.es = es
        self.needed = needed
        self.used = {}
        self.ops = {e: [] for e in ENGINES}
        self.sems = {}
        self.count = {}
        self.waited = {e: {} for e in ENGINES}
        self.eng_of_sem = {}
        for e in ENGINES:
            self._mk("p_" + e, e)

    def _rank(self, sem, val):
        import bisect
        lst = self.needed.get(sem)
        if lst is None:
            return val
        return bisect.bisect_right(lst, val)

    def _mk(self, name, eng=None):
        if name not in self.sems:
            self.sems[name] = self.es.enter_context(self.nc.semaphore(name))
            self.count[name] = 0
            self.eng_of_sem[name] = eng
        return name

    def wait(self, eng, tag):
        if tag is None:
            return
        sem, val = tag
        if val <= 0:
            return
        if self.eng_of_sem.get(sem) == eng and eng == "tensor":
            return
        if self.waited[eng].get(sem, 0) >= val:
            return
        self.waited[eng][sem] = val
        self.used.setdefault(sem, set()).add(val)
        h = self.sems[sem]
        v_emit = val if (self.needed is None or sem.startswith("d_")) else self._rank(sem, val)
        self.ops[eng].append(lambda e, h=h, v=v_emit: e.wait_ge(h, v))

    def _needed_set(self, sem):
        if not hasattr(self, "_nsets"):
            self._nsets = {}
        s = self._nsets.get(sem)
        if s is None:
            s = set(self.needed.get(sem, ()))
            self._nsets[sem] = s
        return s

    def _pre(self, eng, reads, writes):
        for b in reads:
            self.wait(eng, b.w)
        for b in writes:
            self.wait(eng, b.w)
            for t in b.readers:
                self.wait(eng, t)

    def op(self, eng, fns, reads=(), writes=(), self_sync=False):
        if not isinstance(fns, (list, tuple)):
            fns = [fns]
        self._pre(eng, reads, writes)
        sem = "p_" + eng
        self.count[sem] += 1
        v = self.count[sem]
        h = self.sems[sem]
        emit_inc = self.needed is None or v in self._needed_set(sem)
        if emit_inc:
            last = fns[-1]
            fns = list(fns[:-1]) + [lambda e, f=last, h=h: f(e).then_inc(h, 1)]
        t = (sem, v)
        for f in fns:
            self.ops[eng].append(f)
        for b in writes:
            b.w = t
            b.readers = []
        for b in reads:
            b.readers.append(t)
        return t

    def dma(self, eng, slot, fn, reads=(), writes=()):
        sem = self._mk("d_" + slot)
        self._pre(eng, reads, writes)
        self.count[sem] += 16
        v = self.count[sem]
        h = self.sems[sem]
        self.ops[eng].append(lambda e, f=fn, h=h: f(e).then_inc(h, 16))
        t = (sem, v)
        for b in writes:
            b.w = t
            b.readers = []
        for b in reads:
            b.readers.append(t)
        return t

    def barrier(self, bufs):
        tags = []
        for b in bufs:
            if b.w:
                tags.append(b.w)
            tags.extend(b.readers)
        for e in ENGINES:
            for t in tags:
                self.wait(e, t)


def _slopes16():
    start = 2.0 ** (-(2.0 ** (-(math.log2(16) - 3))))
    return np.array([start ** (i + 1) for i in range(16)], dtype=np.float64)


def _hi_lo(x):
    x = np.asarray(x, dtype=np.float32)
    hi = x.astype(ml_dtypes.bfloat16)
    lo = (x - hi.astype(np.float32)).astype(ml_dtypes.bfloat16)
    return np.ascontiguousarray(hi), np.ascontiguousarray(lo)


def build_nc(needed=None):
    nc = bass.Bass("TRN2", target_bir_lowering=False)

    dt_in = lambda n, s, d: nc.dram_tensor(n, s, d, kind="ExternalInput").ap()
    x_my = dt_in("x_my", [TOK, D], F32)
    wqkvT = dt_in("wqkvT", [D, 768], F32)
    bqk = dt_in("bqk", [128, 4], F32)
    bv_h = dt_in("bv_h", [1, 256], BF16)
    bv_l = dt_in("bv_l", [1, 256], BF16)
    woutT = dt_in("woutT", [D, D], F32)
    bout_h = dt_in("bout_h", [1, D], BF16)
    bout_l = dt_in("bout_l", [1, D], BF16)
    wgT_t = dt_in("wgT_t", [NHC, 128, NKC, 128], I32)
    wuT_t = dt_in("wuT_t", [NHC, 128, NKC, 128], I32)
    wdT = dt_in("wdT", [HID, D], I32)
    sg_pp = dt_in("sg_pp", [128, NHC], F32)
    su_pp = dt_in("su_pp", [128, NHC], F32)
    sdb = dt_in("sdb", [128, D], F32)
    attc = dt_in("attc", [128, 6 * 128], F32)
    lamc = dt_in("lamc", [128, 2], F32)
    dk_pp = dt_in("dk_pp", [128, 2], F32)
    identf = dt_in("identf", [128, 128], F32)
    identb = dt_in("identb", [128, 128], BF16)
    ones_row = dt_in("ones_row", [1, 512], BF16)
    n128_row = dt_in("n128_row", [1, 512], BF16)
    ones_colb = dt_in("ones_colb", [128, 1], BF16)
    out_my = nc.dram_tensor("out_my", [TOK, D], F32, kind="ExternalOutput").ap()

    ag_in = [nc.dram_tensor(f"ag_in{i}", [128, 2, NKC, 256], BF16).ap() for i in range(4)]
    ag_out = [nc.dram_tensor(f"ag_out{i}", [8, 128, 2, NKC, 256], BF16, addr_space="Shared").ap() for i in range(4)]
    cc_in = [nc.dram_tensor(f"cc_in{i}", [8, 128, 2, 256], F32).ap() for i in range(4)]
    cc_out = [nc.dram_tensor(f"cc_out{i}", [8, 128, 2, 256], F32).ap() for i in range(4)]
    y_scr = nc.dram_tensor("y_scr", [TOK, D], F32).ap()
    x2_scr = nc.dram_tensor("x2_scr", [TOK, D], F32).ap()
    x2fm_scr = nc.dram_tensor("x2fm_scr", [2, 128, NKC, TOK], BF16).ap()

    es = ExitStack()
    T = Tracker(nc, es, needed=needed)

    def sb(name, shape, dtype, stack):
        t = stack.enter_context(nc.sbuf_tensor(name, shape, dtype))
        return t, Buf(name)

    def psum(name, shape, dtype, stack):
        t = stack.enter_context(nc.psum_tensor(name, shape, dtype))
        return t, Buf(name)

    c_attc, b_attc = sb("c_attc", [128, 6 * 128], F32, es)
    c_lamc, b_lamc = sb("c_lamc", [128, 2], F32, es)
    c_dk, b_dk = sb("c_dk", [128, 2], F32, es)
    c_bqk, b_bqk = sb("c_bqk", [128, 4], F32, es)
    c_bvh, b_bvh = sb("c_bvh", [1, 256], BF16, es)
    c_bvl, b_bvl = sb("c_bvl", [1, 256], BF16, es)
    c_bouth, b_bouth = sb("c_bouth", [1, D], BF16, es)
    c_boutl, b_boutl = sb("c_boutl", [1, D], BF16, es)
    c_idf, b_idf = sb("c_idf", [128, 128], F32, es)
    c_idb, b_idb = sb("c_idb", [128, 128], BF16, es)
    c_ones, b_ones = sb("c_ones", [1, 512], BF16, es)
    c_n128, b_n128 = sb("c_n128", [1, 512], BF16, es)
    c_onec, b_onec = sb("c_onec", [128, 1], BF16, es)
    c_sg, b_sg = sb("c_sg", [128, NHC], F32, es)
    c_su, b_su = sb("c_su", [128, NHC], F32, es)
    c_sdb, b_sdb = sb("c_sdb", [128, D], F32, es)

    const_list = [
        (c_attc, b_attc, attc), (c_lamc, b_lamc, lamc), (c_dk, b_dk, dk_pp), (c_bqk, b_bqk, bqk),
        (c_bvh, b_bvh, bv_h), (c_bvl, b_bvl, bv_l),
        (c_bouth, b_bouth, bout_h), (c_boutl, b_boutl, bout_l),
        (c_idf, b_idf, identf), (c_idb, b_idb, identb),
        (c_ones, b_ones, ones_row), (c_n128, b_n128, n128_row),
        (c_onec, b_onec, ones_colb),
        (c_sg, b_sg, sg_pp), (c_su, b_su, su_pp), (c_sdb, b_sdb, sdb),
    ]
    for (t, bf, src) in const_list:
        T.dma("sync", "const", lambda e, t=t, s=src: e.dma_start(out=t[:], in_=s[:, :]), writes=(bf,))
    tot = ("d_const", T.count["d_const"])
    for (_, bf, _) in const_list:
        bf.w = tot

    wqsc = ExitStack()
    wqh, b_wqh = sb("wqh", [128, NKC * 768], BF16, wqsc)
    wql, b_wql = sb("wql", [128, NKC * 768], BF16, wqsc)
    wqtsc = ExitStack()
    wqt = [sb(f"wqt{i}", [128, 768], F32, wqtsc) for i in range(NKC)]
    for kc in range(NKC):
        wtt, wtb = wqt[kc]
        T.dma("sync", f"wqf{kc % 2}", lambda e, k=kc, d=wtt: e.dma_start(out=d[:], in_=wqkvT[k * 128:(k + 1) * 128, :]), writes=(wtb,))
    for kc in range(NKC):
        wqt[kc][1].w = ("d_wqf" + str(kc % 2), T.count["d_wqf" + str(kc % 2)])

    a0 = ExitStack()
    xf = [sb(f"xf{i}", [128, D], F32, a0) for i in range(2)]
    stgh = [sb(f"stgh{i}", [128, 512], BF16, a0) for i in range(4)]
    stgl = [sb(f"stgl{i}", [128, 512], BF16, a0) for i in range(4)]
    ps_tf = [psum(f"ps_tf{i}", [128, 512], F32, a0) for i in range(4)]
    b_agin = [Buf(f"ag_in{i}") for i in range(4)]
    b_agout = [Buf(f"ag_out{i}") for i in range(4)]
    ag_slot = [[Buf(f"agslot{q}_{i}") for i in range(8)] for q in range(4)]

    def emit_ag(hf):
        T.op("gpsimd", lambda e, h=hf: e.collective_compute(
            "AllGather", ALU.bypass, replica_groups=[list(range(N_CORES))],
            ins=[ag_in[h].tensor.ap().opt()], outs=[ag_out[h].tensor.ap().opt()]),
            reads=tuple(ag_slot[hf]), writes=(b_agout[hf],))

    for ti in range(NT):
        bi = ti % 2
        xft, xfb = xf[bi]
        T.dma("sync", f"xf{bi}", lambda e, t=ti, d=xft: e.dma_start(out=d[:], in_=x_my[t * 128:(t + 1) * 128, :]), writes=(xfb,))
        hf = ti // 2
        for q in range(4):
            sl = (ti * 4 + q) % 4
            pst, psb = ps_tf[sl]
            fns = []
            for k in range(4):
                kc = q * 4 + k
                fns.append(lambda e, s=xft, k2=kc, p=pst, kk=k:
                           e.transpose(p[:, kk * 128:(kk + 1) * 128], s[:, k2 * 128:(k2 + 1) * 128], c_idf[:]))
            T.op("tensor", fns, reads=(xfb, b_idf), writes=(psb,))
            ht, hb = stgh[sl]
            lt, lb = stgl[sl]
            T.op("scalar", lambda e, p=pst, d=ht: e.copy(out=d[:], in_=p[:]), reads=(psb,), writes=(hb,))
            T.op("vector", lambda e, p=pst, d1=ht, d2=lt: e.tensor_tensor(out=d2[:], in0=p[:], in1=d1[:], op=ALU.subtract),
                 reads=(psb, hb), writes=(lb,))
            T.dma("scalar", f"agwh{sl}", lambda e, s=ht, q2=q, t=ti % 2, f=hf:
                  e.dma_start(out=ag_in[f][:, 0, q2 * 4:(q2 + 1) * 4, t * 128:(t + 1) * 128],
                              in_=s[:].rearrange("p (k t) -> p k t", k=4)),
                  reads=(hb,), writes=(ag_slot[hf][sl],))
            T.dma("scalar", f"agwl{sl}", lambda e, s=lt, q2=q, t=ti % 2, f=hf:
                  e.dma_start(out=ag_in[f][:, 1, q2 * 4:(q2 + 1) * 4, t * 128:(t + 1) * 128],
                              in_=s[:].rearrange("p (k t) -> p k t", k=4)),
                  reads=(lb,), writes=(ag_slot[hf][4 + sl],))
        if ti % 2 == 1:
            emit_ag(ti // 2)

    T.barrier([x[1] for x in xf + stgh + stgl + ps_tf])
    a0.close()

    for kc in range(NKC):
        wtt, wtb = wqt[kc]
        T.op("scalar", lambda e, k=kc, s=wtt: e.copy(out=wqh[:, k * 768:(k + 1) * 768], in_=s[:]),
             reads=(wtb,), writes=(b_wqh,))
        T.op("vector", lambda e, k=kc, s=wtt: e.tensor_tensor(out=wql[:, k * 768:(k + 1) * 768], in0=s[:],
                                                              in1=wqh[:, k * 768:(k + 1) * 768], op=ALU.subtract),
             reads=(wtb, b_wqh), writes=(b_wql,))
    T.barrier([x[1] for x in wqt])
    wqtsc.close()
    am = ExitStack()

    xg = [sb(f"xg{i}", [128, 2 * NKC * 512], BF16, am) for i in range(2)]
    qkv_q, b_q = sb("qkv_q", [128, 2048], BF16, am)
    qkv_k, b_k = sb("qkv_k", [128, 2048], BF16, am)
    qkv_v, b_v = sb("qkv_v", [128, 2048], BF16, am)
    qk32 = [sb(f"qk32_{i}", [128, 512], F32, am) for i in range(2)]
    sbS, b_S = sb("sbS", [128, 256], F32, am)
    sbSh, b_Sh = sb("sbSh", [128, 256], BF16, am)
    sbSl, b_Sl = sb("sbSl", [128, 256], BF16, am)
    smf = [sb(f"smf{i}", [128, 256], F32, am) for i in range(2)]
    smh = [sb(f"smh{i}", [128, 256], BF16, am) for i in range(2)]
    sml = [sb(f"sml{i}", [128, 256], BF16, am) for i in range(2)]
    ktf = [sb(f"ktf{i}", [128, 256], F32, am) for i in range(2)]
    ktg = [sb(f"ktg{i}", [128, 256], F32, am) for i in range(2)]
    ktdh = [sb(f"ktdh{i}", [128, 256], BF16, am) for i in range(2)]
    ktdl = [sb(f"ktdl{i}", [128, 256], BF16, am) for i in range(2)]
    qdf = [sb(f"qdf{i}", [128, 256], F32, am) for i in range(2)]
    qdh = [sb(f"qdh{i}", [128, 256], BF16, am) for i in range(2)]
    qdl = [sb(f"qdl{i}", [128, 256], BF16, am) for i in range(2)]
    astg = [sb(f"astg{i}", [128, 256], F32, am) for i in range(2)]
    ps = [psum(f"psA{i}", [128, 512], F32, am) for i in range(6)]
    pkt = [psum(f"pkt{i}", [128, 512], BF16, am) for i in range(2)]
    b_ccin = [Buf(f"cc_in{i}") for i in range(4)]

    DQ0, DK0, DM0 = 0, 256, 512

    def chunk_views(c):
        pp = c % 2
        def qs(h, lo):
            o = 1024 * lo + h * 512 + c * 128
            return qkv_q[:, o:o + 128]
        def ks(h, lo):
            o = 1024 * lo + h * 512 + c * 128
            return qkv_k[:, o:o + 128]
        def vs(h, lo):
            o = 1024 * lo + c * 256 + h * 128
            return qkv_v[:, o:o + 128]
        return pp, qs, ks, vs

    def emit_chunk_front(b, g, c):
        pp, qs, ks, vs = chunk_views(c)
        psS_t, psS_b = ps[pp]
        pkt_t, pkt_b = pkt[pp]
        ktft, ktfb = ktf[pp]
        qdft, qdfb = qdf[pp]
        qdht, qdhb = qdh[pp]
        qdlt, qdlb = qdl[pp]
        smft, smfb = smf[pp]
        smht, smhb = smh[pp]
        smlt, smlb = sml[pp]
        ktdht, ktdhb = ktdh[pp]
        ktdlt, ktdlb = ktdl[pp]

        fns = []
        for h in range(2):
            fns += [
                lambda e, h2=h: e.matmul(psS_t[:, h2 * 128:(h2 + 1) * 128], ks(h2, 0), qs(h2, 0), start=True, stop=False),
                lambda e, h2=h: e.matmul(psS_t[:, h2 * 128:(h2 + 1) * 128], ks(h2, 0), qs(h2, 1), start=False, stop=False),
                lambda e, h2=h: e.matmul(psS_t[:, h2 * 128:(h2 + 1) * 128], ks(h2, 1), qs(h2, 0), start=False, stop=True),
            ]
        T.op("tensor", fns, reads=(b_k, b_q), writes=(psS_b,))
        T.op("tensor", [
            lambda e: e.transpose(pkt_t[:, 0:128], ks(0, 0), c_idb[:]),
            lambda e: e.transpose(pkt_t[:, 128:256], ks(1, 0), c_idb[:]),
            lambda e: e.transpose(pkt_t[:, 256:384], ks(0, 1), c_idb[:]),
            lambda e: e.transpose(pkt_t[:, 384:512], ks(1, 1), c_idb[:]),
        ], reads=(b_k, b_idb), writes=(pkt_b,))

        T.op("vector", [
            lambda e: e.tensor_tensor(out=qdft[:, 0:128], in0=qs(0, 0), in1=qs(0, 1), op=ALU.add),
            lambda e: e.tensor_tensor(out=qdft[:, 128:256], in0=qs(1, 0), in1=qs(1, 1), op=ALU.add),
        ], reads=(b_q,), writes=(qdfb,))
        T.op("vector", lambda e: e.tensor_tensor(out=qdft[:], in0=qdft[:], in1=c_attc[:, DQ0:DQ0 + 256], op=ALU.mult),
             reads=(qdfb, b_attc), writes=(qdfb,))
        T.op("vector", lambda e: e.tensor_copy(out=qdht[:], in_=qdft[:]), reads=(qdfb,), writes=(qdhb,))
        T.op("vector", lambda e: e.tensor_tensor(out=qdlt[:], in0=qdft[:], in1=qdht[:], op=ALU.subtract),
             reads=(qdfb, qdhb), writes=(qdlb,))

        T.op("vector", lambda e: e.tensor_tensor(out=smft[:], in0=psS_t[:, 0:256], in1=c_attc[:, DM0:DM0 + 256], op=ALU.mult),
             reads=(psS_b, b_attc), writes=(smfb,))
        T.op("vector", lambda e: e.tensor_copy(out=smht[:], in_=smft[:]), reads=(smfb,), writes=(smhb,))
        T.op("vector", lambda e: e.tensor_tensor(out=smlt[:], in0=smft[:], in1=smht[:], op=ALU.subtract),
             reads=(smfb, smhb), writes=(smlb,))

        ktgt, ktgb = ktg[pp]
        T.op("vector", [
            lambda e: e.tensor_scalar_mul(ktft[:, 0:128], pkt_t[:, 0:128], c_dk[:, 0:1]),
            lambda e: e.tensor_scalar_mul(ktft[:, 128:256], pkt_t[:, 128:256], c_dk[:, 1:2]),
        ], reads=(pkt_b, b_dk), writes=(ktfb,))
        T.op("vector", [
            lambda e: e.scalar_tensor_tensor(out=ktgt[:, 0:128], in0=pkt_t[:, 256:384], scalar=c_dk[:, 0:1], in1=ktft[:, 0:128], op0=ALU.mult, op1=ALU.add),
            lambda e: e.scalar_tensor_tensor(out=ktgt[:, 128:256], in0=pkt_t[:, 384:512], scalar=c_dk[:, 1:2], in1=ktft[:, 128:256], op0=ALU.mult, op1=ALU.add),
        ], reads=(pkt_b, b_dk, ktfb), writes=(ktgb,))
        T.op("vector", lambda e: e.tensor_copy(out=ktdht[:], in_=ktgt[:]), reads=(ktgb,), writes=(ktdhb,))
        T.op("vector", lambda e: e.tensor_tensor(out=ktdlt[:], in0=ktgt[:], in1=ktdht[:], op=ALU.subtract),
             reads=(ktgb, ktdhb), writes=(ktdlb,))

    def emit_chunk_back(b, g, c):
        gc = g * 4 + c
        pp, qs, ks, vs = chunk_views(c)
        psU_t, psU_b = ps[2 + pp]
        psV_t, psV_b = ps[4 + pp]
        qdht, qdhb = qdh[pp]
        qdlt, qdlb = qdl[pp]
        smht, smhb = smh[pp]
        smlt, smlb = sml[pp]
        ktdht, ktdhb = ktdh[pp]
        ktdlt, ktdlb = ktdl[pp]

        fns = []
        for h in range(2):
            fns += [
                lambda e, h2=h: e.matmul(psV_t[:, h2 * 128:(h2 + 1) * 128], vs(h2, 0), smht[:, h2 * 128:(h2 + 1) * 128], start=True, stop=False),
                lambda e, h2=h: e.matmul(psV_t[:, h2 * 128:(h2 + 1) * 128], vs(h2, 1), smht[:, h2 * 128:(h2 + 1) * 128], start=False, stop=False),
                lambda e, h2=h: e.matmul(psV_t[:, h2 * 128:(h2 + 1) * 128], vs(h2, 0), smlt[:, h2 * 128:(h2 + 1) * 128], start=False, stop=False),
                lambda e, h2=h: e.matmul(psV_t[:, h2 * 128:(h2 + 1) * 128], sbSh[:, h2 * 128:(h2 + 1) * 128], qdht[:, h2 * 128:(h2 + 1) * 128], start=False, stop=False),
                lambda e, h2=h: e.matmul(psV_t[:, h2 * 128:(h2 + 1) * 128], sbSl[:, h2 * 128:(h2 + 1) * 128], qdht[:, h2 * 128:(h2 + 1) * 128], start=False, stop=False),
                lambda e, h2=h: e.matmul(psV_t[:, h2 * 128:(h2 + 1) * 128], sbSh[:, h2 * 128:(h2 + 1) * 128], qdlt[:, h2 * 128:(h2 + 1) * 128], start=False, stop=True),
            ]
        T.op("tensor", fns, reads=(b_v, smhb, smlb, b_Sh, b_Sl, qdhb, qdlb), writes=(psV_b,))
        fns = []
        for h in range(2):
            fns += [
                lambda e, h2=h: e.matmul(psU_t[:, h2 * 128:(h2 + 1) * 128], ktdht[:, h2 * 128:(h2 + 1) * 128], vs(h2, 0), start=True, stop=False),
                lambda e, h2=h: e.matmul(psU_t[:, h2 * 128:(h2 + 1) * 128], ktdlt[:, h2 * 128:(h2 + 1) * 128], vs(h2, 0), start=False, stop=False),
                lambda e, h2=h: e.matmul(psU_t[:, h2 * 128:(h2 + 1) * 128], ktdht[:, h2 * 128:(h2 + 1) * 128], vs(h2, 1), start=False, stop=True),
            ]
        T.op("tensor", fns, reads=(ktdhb, ktdlb, b_v), writes=(psU_b,))

        T.op("vector", [
            lambda e: e.scalar_tensor_tensor(out=sbS[:, 0:128], in0=sbS[:, 0:128], scalar=c_lamc[:, 0:1], in1=psU_t[:, 0:128], op0=ALU.mult, op1=ALU.add),
            lambda e: e.scalar_tensor_tensor(out=sbS[:, 128:256], in0=sbS[:, 128:256], scalar=c_lamc[:, 1:2], in1=psU_t[:, 128:256], op0=ALU.mult, op1=ALU.add),
        ], reads=(psU_b, b_lamc), writes=(b_S,))
        T.op("vector", lambda e: e.tensor_copy(out=sbSh[:], in_=sbS[:]), reads=(b_S,), writes=(b_Sh,))
        T.op("vector", lambda e: e.tensor_tensor(out=sbSl[:], in0=sbS[:], in1=sbSh[:], op=ALU.subtract),
             reads=(b_S, b_Sh), writes=(b_Sl,))

        stt, stb = astg[pp]
        T.op("scalar", lambda e: e.copy(out=stt[:], in_=psV_t[:, 0:256]), reads=(psV_b,), writes=(stb,))
        j, toff = gc // 2, (gc % 2) * 128
        T.dma("gpsimd", f"ccw{pp}", lambda e, j2=j, t=toff, s=stt:
              e.dma_start(out=cc_in[b][j2, :, :, t:t + 128], in_=s[:].rearrange("p (h t) -> p h t", h=2)),
              reads=(stb,), writes=(b_ccin[b],))

    def emit_group(b, g):
        gi = b * 4 + g
        bi = gi % 2
        xgt, xgb = xg[bi]
        for hl in range(2):
            for si in range(2):
                seg = 2 * g + si
                T.dma("sync", f"xg{bi}_{hl}", lambda e, h=hl, sg=seg, s2=si, d=xgt, f=b:
                      e.dma_start(out=d[:, h * NKC * 512:(h + 1) * NKC * 512]
                                  .rearrange("p (k t) -> p k t", k=NKC)[:, :, s2 * 256:s2 * 256 + 256],
                                  in_=ag_out[f][sg, :, h, :, :]),
                      reads=(b_agout[b],), writes=(xgb,))

        for fc in range(4):
            pst, psb = ps[fc]
            fns = []
            for kc in range(NKC):
                for p in range(3):
                    wt = wqh if p in (0, 2) else wql
                    xs = 0 if p in (0, 1) else 1
                    fns.append(lambda e, w=wt, k=kc, f=fc, x=xs, d=xgt, bk=pst, st=(kc == 0 and p == 0), sp=(kc == NKC - 1 and p == 2):
                               e.matmul(bk[:], w[:, k * 768 + f * 128:k * 768 + (f + 1) * 128],
                                        d[:, x * NKC * 512 + k * 512:x * NKC * 512 + (k + 1) * 512], start=st, stop=sp))
            T.op("tensor", fns, reads=(xgb, b_wqh, b_wql), writes=(psb,))
        for sl in range(4):
            bank, half = 4 + sl // 2, (sl % 2) * 256
            pst, psb = ps[bank]
            fns = []
            for kc in range(NKC):
                for p in range(3):
                    wt = wqh if p in (0, 2) else wql
                    xs = 0 if p in (0, 1) else 1
                    fns.append(lambda e, w=wt, k=kc, s=sl, bk=pst, hf=half, x=xs, d=xgt, st=(kc == 0 and p == 0):
                               e.matmul(bk[:, hf:hf + 256], d[:, x * NKC * 512 + k * 512 + s * 128:x * NKC * 512 + k * 512 + (s + 1) * 128],
                                        w[:, k * 768 + 512:(k + 1) * 768], start=st, stop=False))
            fns.append(lambda e, bk=pst, hf=half: e.matmul(bk[:, hf:hf + 256], c_ones[0:1, 0:128], c_bvh[0:1, :], start=False, stop=False))
            fns.append(lambda e, bk=pst, hf=half: e.matmul(bk[:, hf:hf + 256], c_ones[0:1, 0:128], c_bvl[0:1, :], start=False, stop=True))
            T.op("tensor", fns, reads=(xgb, b_wqh, b_wql, b_ones, b_bvh, b_bvl), writes=(psb,))

        for fc in range(4):
            dstt, dstb = (qkv_q, b_q) if fc < 2 else (qkv_k, b_k)
            h = fc % 2
            pst, psb = ps[fc]
            s32t, s32b = qk32[fc % 2]
            T.op("vector", lambda e, p=pst, d=s32t, f=fc: e.tensor_scalar_add(d[:], p[:], c_bqk[:, f:f + 1]),
                 reads=(psb, b_bqk), writes=(s32b,))
            T.op("scalar", lambda e, s=s32t, d=dstt, h2=h: e.copy(out=d[:, h2 * 512:(h2 + 1) * 512], in_=s[:]),
                 reads=(s32b,), writes=(dstb,))
            T.op("vector", lambda e, s=s32t, d=dstt, h2=h: e.tensor_tensor(out=d[:, 1024 + h2 * 512:1024 + (h2 + 1) * 512], in0=s[:],
                                                                           in1=d[:, h2 * 512:(h2 + 1) * 512], op=ALU.subtract),
                 reads=(s32b, dstb), writes=(dstb,))
        for sl in range(4):
            bank, half = 4 + sl // 2, (sl % 2) * 256
            pst, psb = ps[bank]
            T.op("scalar", lambda e, p=pst, hf=half, s=sl: e.copy(out=qkv_v[:, s * 256:(s + 1) * 256], in_=p[:, hf:hf + 256]),
                 reads=(psb,), writes=(b_v,))
            T.op("vector", lambda e, p=pst, hf=half, s=sl: e.tensor_tensor(out=qkv_v[:, 1024 + s * 256:1024 + (s + 1) * 256],
                                                                           in0=p[:, hf:hf + 256], in1=qkv_v[:, s * 256:(s + 1) * 256], op=ALU.subtract),
                 reads=(psb,), writes=(b_v,))

        emit_chunk_front(b, g, 0)
        emit_chunk_front(b, g, 1)
        emit_chunk_back(b, g, 0)
        emit_chunk_front(b, g, 2)
        emit_chunk_back(b, g, 1)
        emit_chunk_front(b, g, 3)
        emit_chunk_back(b, g, 2)
        emit_chunk_back(b, g, 3)

    b_ccout = [Buf(f"cc_out{i}") for i in range(4)]

    def _a2a(b2):
        def f(e):
            with allow_ici_all_to_all():
                return e.collective_compute("AllToAll", ALU.bypass, replica_groups=[list(range(N_CORES))],
                                            ins=[cc_in[b2].tensor.ap().opt()], outs=[cc_out[b2].tensor.ap().opt()])
        return f

    for b in range(B):
        T.op("vector", lambda e: e.memset(sbS[:], 0.0), writes=(b_S,))
        T.op("vector", lambda e: e.memset(sbSh[:], 0.0), writes=(b_Sh,))
        T.op("vector", lambda e: e.memset(sbSl[:], 0.0), writes=(b_Sl,))
        for g in range(4):
            emit_group(b, g)
        T.op("gpsimd", _a2a(b), reads=(b_ccin[b],), writes=(b_ccout[b],))

    T.barrier([b_wqh, b_wql, b_q, b_k, b_v, b_S, b_Sh, b_Sl] +
              [x[1] for x in wqt + xg + qk32 + smf + smh + sml + ktf + ktg + ktdh + ktdl + qdf + qdh + qdl + astg + ps + pkt])
    am.close()
    wqsc.close()

    pb_ = ExitStack()
    wo_h, b_woh = sb("wo_h", [128, NKC * 1024], BF16, pb_)
    wo_l, b_wol = sb("wo_l", [128, NKC * 1024], BF16, pb_)
    wo_t = [sb(f"wo_t{i}", [128, 1024], F32, pb_) for i in range(3)]
    at32 = [sb(f"at32_{i}", [128, 2048], F32, pb_) for i in range(2)]
    b_a32p = [[Buf(f"a32_{i}_{p}") for p in range(2)] for i in range(2)]
    ath = [sb(f"ath{i}", [128, 2048], BF16, pb_) for i in range(2)]
    atl = [sb(f"atl{i}", [128, 2048], BF16, pb_) for i in range(2)]
    xmt = [sb(f"xmt{i}", [128, 1024], F32, pb_) for i in range(2)]
    ystg = [sb(f"ystg{i}", [128, 1024], F32, pb_) for i in range(2)]
    psB = [psum(f"psB{i}", [128, 1024], F32, pb_) for i in range(2)]
    b_yscr = Buf("y_scr")

    stat, b_stat = sb("stat", [128, 16], F32, pb_)
    yt = [sb(f"yt{i}", [128, 2048], F32, pb_) for i in range(2)]
    x2t = [sb(f"x2t{i}", [128, 2048], F32, pb_) for i in range(2)]
    x2h = [sb(f"x2h{i}", [128, 2048], BF16, pb_) for i in range(3)]
    x2l = [sb(f"x2l{i}", [128, 2048], BF16, pb_) for i in range(3)]
    x2stg = [sb(f"x2stg{i}", [128, 1024], BF16, pb_) for i in range(2)]
    ps_x2 = [psum(f"ps_x2{i}", [128, 1024], BF16, pb_) for i in range(2)]
    b_x2scr = Buf("x2_scr")
    b_x2fm = Buf("x2fm_scr")

    def emit_ln(engV, y_src, y_b, st_t, st_b, out_t, out_b):
        T.op("scalar", lambda e: e.activation(out=out_t[:], in_=y_src[:], func=ACT.Square, accum_out=st_t[:, 1:2]),
             reads=(y_b,), writes=(out_b, st_b))
        T.op("scalar", lambda e: e.activation(out=out_t[:], in_=y_src[:], func=ACT.Copy, accum_out=st_t[:, 0:1]),
             reads=(y_b,), writes=(out_b, st_b))
        T.op("vector", lambda e: e.tensor_scalar_mul(st_t[:, 2:3], st_t[:, 0:1], 1.0 / D), reads=(st_b,), writes=(st_b,), self_sync=True)
        T.op("vector", lambda e: e.tensor_scalar_mul(st_t[:, 3:4], st_t[:, 1:2], 1.0 / D), reads=(st_b,), writes=(st_b,), self_sync=True)
        T.op("vector", lambda e: e.tensor_tensor(out=st_t[:, 4:5], in0=st_t[:, 2:3], in1=st_t[:, 2:3], op=ALU.mult),
             reads=(st_b,), writes=(st_b,), self_sync=True)
        T.op("vector", lambda e: e.tensor_tensor(out=st_t[:, 5:6], in0=st_t[:, 3:4], in1=st_t[:, 4:5], op=ALU.subtract),
             reads=(st_b,), writes=(st_b,), self_sync=True)
        T.op("vector", lambda e: e.tensor_scalar_add(st_t[:, 6:7], st_t[:, 5:6], EPS), reads=(st_b,), writes=(st_b,), self_sync=True)
        T.op("scalar", lambda e: e.activation(out=st_t[:, 7:8], in_=st_t[:, 6:7], func=ACT.Sqrt), reads=(st_b,), writes=(st_b,))
        T.op("vector", lambda e: e.reciprocal(st_t[:, 8:9], st_t[:, 7:8]), reads=(st_b,), writes=(st_b,), self_sync=True)
        T.op("vector", lambda e: e.tensor_tensor(out=st_t[:, 9:10], in0=st_t[:, 2:3], in1=st_t[:, 8:9], op=ALU.mult),
             reads=(st_b,), writes=(st_b,), self_sync=True)
        T.op("vector", lambda e: e.tensor_scalar_mul(st_t[:, 10:11], st_t[:, 9:10], -1.0), reads=(st_b,), writes=(st_b,), self_sync=True)
        T.op("vector", lambda e: e.tensor_scalar(out=out_t[:], in0=y_src[:], scalar1=st_t[:, 8:9], scalar2=st_t[:, 10:11],
                                                 op0=ALU.mult, op1=ALU.add),
             reads=(y_b, st_b), writes=(out_b,))

    def emit_ln1_stats(ti):
        bi = ti % 2
        ytt, ytb = yt[bi]
        x2tt, x2tb = x2t[bi]
        emit_ln("vector", ytt, ytb, stat, b_stat, x2tt, x2tb)
        T.dma("gpsimd", f"x2w{bi}", lambda e, t=ti, s=x2tt: e.dma_start(out=x2_scr[t * 128:(t + 1) * 128, :], in_=s[:]),
              reads=(x2tb,), writes=(b_x2scr,))
        x2ht, x2hb = x2h[ti % 3]
        x2lt, x2lb = x2l[ti % 3]
        T.op("scalar", lambda e, s=x2tt, d=x2ht: e.copy(out=d[:], in_=s[:]), reads=(x2tb,), writes=(x2hb,))
        T.op("vector", lambda e, s=x2tt, d1=x2ht, d2=x2lt: e.tensor_tensor(out=d2[:], in0=s[:], in1=d1[:], op=ALU.subtract),
             reads=(x2tb, x2hb), writes=(x2lb,))

    def emit_ln1_transposes(ti):
        x2ht, x2hb = x2h[ti % 3]
        x2lt, x2lb = x2l[ti % 3]
        for grp in range(4):
            hl, oct_ = grp // 2, grp % 2
            pbk = (ti * 4 + grp) % 2
            pst, psb = ps_x2[pbk]
            srct, srcb = (x2ht, x2hb) if hl == 0 else (x2lt, x2lb)
            fns = []
            for k in range(8):
                kc = oct_ * 8 + k
                fns.append(lambda e, s=srct, k2=kc, p=pst, kk=k:
                           e.transpose(p[:, kk * 128:(kk + 1) * 128], s[:, k2 * 128:(k2 + 1) * 128], c_idb[:]))
            T.op("tensor", fns, reads=(srcb, b_idb), writes=(psb,))
            stt, stb = x2stg[pbk]
            T.op("scalar", lambda e, p=pst, d=stt: e.copy(out=d[:], in_=p[:]), reads=(psb,), writes=(stb,))
            T.dma("gpsimd", f"x2fw{pbk}", lambda e, s=stt, h=hl, o=oct_, t=ti:
                  e.dma_start(out=x2fm_scr[h, :, o * 8:(o + 1) * 8, t * 128:(t + 1) * 128],
                              in_=s[:].rearrange("p (k t) -> p k t", k=8)),
                  reads=(stb,), writes=(b_x2fm,))

    for half in range(2):
        for kc in range(NKC):
            wtt, wtb = wo_t[kc % 3]
            T.dma("sync", f"wof{kc % 3}", lambda e, k=kc, h=half, d=wtt:
                  e.dma_start(out=d[:], in_=woutT[k * 128:(k + 1) * 128, h * 1024:(h + 1) * 1024]), writes=(wtb,))
            T.op("scalar", lambda e, k=kc, s=wtt: e.copy(out=wo_h[:, k * 1024:(k + 1) * 1024], in_=s[:]),
                 reads=(wtb,), writes=(b_woh,))
            T.op("vector", lambda e, k=kc, s=wtt: e.tensor_tensor(out=wo_l[:, k * 1024:(k + 1) * 1024], in0=s[:],
                                                                  in1=wo_h[:, k * 1024:(k + 1) * 1024], op=ALU.subtract),
                 reads=(wtb, b_woh), writes=(b_wol,))
        def emit_at_prep(ti):
            bi = ti % 2
            a32t, a32b = at32[bi]
            atht, athb = ath[bi]
            atlt, atlb = atl[bi]
            for j in range(8):
                T.dma("sync", f"at{bi}_{j % 2}", lambda e, j2=j, b2=ti // 2, t2=(ti % 2) * 128, d=a32t:
                      e.dma_start(out=d[:, j2 * 256:(j2 + 1) * 256].rearrange("p (h t) -> p h t", h=2),
                                  in_=cc_out[b2][j2, :, :, t2:t2 + 128]),
                      reads=(b_ccout[ti // 2],), writes=(b_a32p[bi][j % 2],))
            a32rd = (b_a32p[bi][0], b_a32p[bi][1])
            T.op("scalar", lambda e, s=a32t, d=atht: e.copy(out=d[:], in_=s[:]), reads=a32rd, writes=(athb,))
            T.op("vector", lambda e, s=a32t, d1=atht, d2=atlt: e.tensor_tensor(out=d2[:], in0=s[:], in1=d1[:], op=ALU.subtract),
                 reads=a32rd + (athb,), writes=(atlb,))
            xmtt, xmtb = xmt[bi]
            T.dma("sync", f"xm{bi}", lambda e, t=ti, h=half, d=xmtt:
                  e.dma_start(out=d[:], in_=x_my[t * 128:(t + 1) * 128, h * 1024:(h + 1) * 1024]), writes=(xmtb,))

        emit_at_prep(0)
        for ti in range(NT):
            bi = ti % 2
            atht, athb = ath[bi]
            atlt, atlb = atl[bi]
            xmtt, xmtb = xmt[bi]
            if half == 1:
                ytt0, ytb0 = yt[bi]
                T.dma("sync", f"yld{bi}", lambda e, t=ti, d=ytt0: e.dma_start(out=d[:, 0:1024], in_=y_scr[t * 128:(t + 1) * 128, 0:1024]),
                      reads=(b_yscr,), writes=(ytb0,))
            pst, psb = psB[bi]
            fns = []
            for ncc in range(2):
                for kc in range(NKC):
                    for p in range(3):
                        wt = wo_h if p in (0, 2) else wo_l
                        asrc = atht if p in (0, 1) else atlt
                        fns.append(lambda e, w=wt, a=asrc, k=kc, n=ncc, bk=pst, st=(kc == 0 and p == 0):
                                   e.matmul(bk[:, n * 512:(n + 1) * 512], a[:, k * 128:(k + 1) * 128],
                                            w[:, k * 1024 + n * 512:k * 1024 + (n + 1) * 512], start=st, stop=False))
                fns.append(lambda e, n=ncc, h=half, bk=pst: e.matmul(bk[:, n * 512:(n + 1) * 512], c_ones[0:1, 0:128],
                                                                     c_bouth[0:1, h * 1024 + n * 512:h * 1024 + (n + 1) * 512], start=False, stop=False))
                fns.append(lambda e, n=ncc, h=half, bk=pst: e.matmul(bk[:, n * 512:(n + 1) * 512], c_ones[0:1, 0:128],
                                                                     c_boutl[0:1, h * 1024 + n * 512:h * 1024 + (n + 1) * 512], start=False, stop=True))
            T.op("tensor", fns, reads=(athb, atlb, b_woh, b_wol, b_ones, b_bouth, b_boutl), writes=(psb,))
            if ti + 1 < NT:
                emit_at_prep(ti + 1)
            if half == 0:
                yst, ysb = ystg[bi]
                T.op("vector", lambda e, p=pst, x=xmtt, d=yst: e.scalar_tensor_tensor(out=d[:], in0=p[:], scalar=1.0, in1=x[:], op0=ALU.mult, op1=ALU.add),
                     reads=(psb, xmtb), writes=(ysb,))
                T.dma("gpsimd", f"yw{bi}", lambda e, t=ti, s=yst:
                      e.dma_start(out=y_scr[t * 128:(t + 1) * 128, 0:1024], in_=s[:]), reads=(ysb,), writes=(b_yscr,))
            else:
                ytt0, ytb0 = yt[bi]
                T.op("vector", lambda e, p=pst, x=xmtt, d=ytt0: e.scalar_tensor_tensor(out=d[:, 1024:2048], in0=p[:], scalar=1.0, in1=x[:], op0=ALU.mult, op1=ALU.add),
                     reads=(psb, xmtb, ytb0), writes=(ytb0,))
                emit_ln1_stats(ti)
                if ti >= 2:
                    emit_ln1_transposes(ti - 2)
                if ti == NT - 1:
                    emit_ln1_transposes(ti - 1)
                    emit_ln1_transposes(ti)

    T.barrier([b_woh, b_wol, b_stat] +
              [x[1] for x in wo_t + at32 + ath + atl + xmt + ystg + psB + yt + x2t + x2h + x2l + x2stg + ps_x2])
    pb_.close()

    pc = ExitStack()
    x2f, b_x2f = sb("x2f", [128, 2 * NKC * 512], BF16, pc)
    hbuf, b_h = sb("hbuf", [128, 2 * NHC * 512], BF16, pc)
    wg = [sb(f"wg{i}", [128, NKC * 128], BF16, pc) for i in range(2)]
    wu = [sb(f"wu{i}", [128, NKC * 128], BF16, pc) for i in range(2)]
    wd = [sb(f"wd{i}", [128, D], BF16, pc) for i in range(3)]
    gsil, b_gsil = sb("gsil", [128, 512], F32, pc)
    utmp, b_utmp = sb("utmp", [128, 512], F32, pc)
    h32, b_h32 = sb("h32", [128, 512], F32, pc)
    csx2, b_csx2 = sb("csx2", [1, 512], BF16, pc)
    csh, b_csh = sb("csh", [1, 512], BF16, pc)
    x2r, b_x2r = sb("x2r", [128, 2048], F32, pc)
    y2 = [sb(f"y2t{i}", [128, 2048], F32, pc) for i in range(2)]
    otst, b_ot = x2r, b_x2r
    st2, b_st2 = sb("st2", [128, 16], F32, pc)
    psC = [psum(f"psC{i}", [128, 512], F32, pc) for i in range(8)]
    b_out = Buf("out_my")

    wg_preloaded = set()

    def emit_x2f_loads(th):
        for hl in range(2):
            T.dma("sync", f"x2f{hl}", lambda e, h=hl, t=th:
                  e.dma_start(out=x2f[:, h * NKC * 512:(h + 1) * NKC * 512].rearrange("p (k t) -> p k t", k=NKC),
                              in_=x2fm_scr[h, :, :, t * 512:(t + 1) * 512]),
                  reads=(b_x2fm,), writes=(b_x2f,))

    emit_x2f_loads(0)
    for th in range(2):
        pst, psb = psC[0]
        fns = []
        for hl in range(2):
            for kc in range(NKC):
                fns.append(lambda e, h=hl, k=kc, bk=pst, st=(hl == 0 and kc == 0), sp=(hl == 1 and kc == NKC - 1):
                           e.matmul(bk[0:1, :], c_onec[:, 0:1], x2f[:, h * NKC * 512 + k * 512:h * NKC * 512 + (k + 1) * 512], start=st, stop=sp))
        T.op("tensor", fns, reads=(b_x2f, b_onec), writes=(psb,))
        T.op("scalar", lambda e, bk=pst: e.copy(out=csx2[0:1, :], in_=bk[0:1, :]), reads=(psb,), writes=(b_csx2,))

        for hc in range(NHC):
            bi = hc % 2
            wgt, wgb = wg[bi]
            wut, wub = wu[bi]
            if (th, hc) not in wg_preloaded:
                T.dma("gpsimd", f"wg{bi}", lambda e, h=hc, d=wgt: e.dma_start(out=d[:].rearrange("p (k t) -> p k t", k=NKC), in_=wgT_t[h, :, :, :]),
                      writes=(wgb,))
                T.dma("gpsimd", f"wu{bi}", lambda e, h=hc, d=wut: e.dma_start(out=d[:].rearrange("p (k t) -> p k t", k=NKC), in_=wuT_t[h, :, :, :]),
                      writes=(wub,))
            pg_t, pg_b = psC[2 + bi]
            pu_t, pu_b = psC[4 + bi]
            for (wt, wb_, pt, pb2) in ((wgt, wgb, pg_t, pg_b), (wut, wub, pu_t, pu_b)):
                fns = []
                for kc in range(NKC):
                    for hl in range(2):
                        fns.append(lambda e, w=wt, k=kc, h=hl, bk=pt, st=(kc == 0 and hl == 0):
                                   e.matmul(bk[:], w[:, k * 128:(k + 1) * 128],
                                            x2f[:, h * NKC * 512 + k * 512:h * NKC * 512 + (k + 1) * 512], start=st, stop=False))
                fns.append(lambda e, bk=pt: e.matmul(bk[:], c_n128[0:1, 0:128], csx2[0:1, :], start=False, stop=True))
                T.op("tensor", fns, reads=(wb_, b_x2f, b_n128, b_csx2), writes=(pb2,))
            T.op("scalar", lambda e, p=pg_t, h=hc: e.activation(out=gsil[:], in_=p[:], func=ACT.Silu, scale=c_sg[:, h:h + 1]),
                 reads=(pg_b, b_sg), writes=(b_gsil,))
            T.op("vector", lambda e, p=pu_t, h=hc: e.tensor_scalar_mul(utmp[:], p[:], c_su[:, h:h + 1]),
                 reads=(pu_b, b_su), writes=(b_utmp,))
            T.op("vector", lambda e: e.tensor_tensor(out=h32[:], in0=gsil[:], in1=utmp[:], op=ALU.mult),
                 reads=(b_gsil, b_utmp), writes=(b_h32,))
            T.op("scalar", lambda e, h=hc: e.copy(out=hbuf[:, h * 512:(h + 1) * 512], in_=h32[:]),
                 reads=(b_h32,), writes=(b_h,))
            T.op("vector", lambda e, h=hc: e.tensor_tensor(out=hbuf[:, NHC * 512 + h * 512:NHC * 512 + (h + 1) * 512],
                                                           in0=h32[:], in1=hbuf[:, h * 512:(h + 1) * 512], op=ALU.subtract),
                 reads=(b_h32,), writes=(b_h,))

        pst, psb = psC[1]
        fns = []
        for hl in range(2):
            for hcc in range(NHC):
                fns.append(lambda e, h=hl, k=hcc, bk=pst, st=(hl == 0 and hcc == 0), sp=(hl == 1 and hcc == NHC - 1):
                           e.matmul(bk[0:1, :], c_onec[:, 0:1], hbuf[:, h * NHC * 512 + k * 512:h * NHC * 512 + (k + 1) * 512], start=st, stop=sp))
        T.op("tensor", fns, reads=(b_h, b_onec), writes=(psb,))
        T.op("scalar", lambda e, bk=pst: e.copy(out=csh[0:1, :], in_=bk[0:1, :]), reads=(psb,), writes=(b_csh,))

        if th == 0:
            emit_x2f_loads(1)
            for hc2 in (0, 1):
                bi2 = hc2 % 2
                wgt2, wgb2 = wg[bi2]
                wut2, wub2 = wu[bi2]
                T.dma("gpsimd", f"wg{bi2}", lambda e, h=hc2, d=wgt2: e.dma_start(out=d[:].rearrange("p (k t) -> p k t", k=NKC), in_=wgT_t[h, :, :, :]),
                      writes=(wgb2,))
                T.dma("gpsimd", f"wu{bi2}", lambda e, h=hc2, d=wut2: e.dma_start(out=d[:].rearrange("p (k t) -> p k t", k=NKC), in_=wuT_t[h, :, :, :]),
                      writes=(wub2,))
                wg_preloaded.add((1, hc2))

        for tp in range(2):
            for kc in range(NHC):
                bi = kc % 3
                wdt, wdb = wd[bi]
                T.dma("gpsimd", f"wd{bi}", lambda e, k=kc, d=wdt: e.dma_start(out=d[:], in_=wdT[k * 128:(k + 1) * 128, :]), writes=(wdb,))
                fns = []
                for ti_in in range(2):
                    tloc = (tp * 2 + ti_in) * 128
                    for hl in range(2):
                        for ncc in range(4):
                            bk = ti_in * 4 + ncc
                            fns.append(lambda e, w=wdt, k2=kc, h=hl, t=tloc, n=ncc, p=psC[bk][0], st=(kc == 0 and hl == 0):
                                       e.matmul(p[:], hbuf[:, h * NHC * 512 + k2 * 512 + t:h * NHC * 512 + k2 * 512 + t + 128],
                                                w[:, n * 512:(n + 1) * 512], start=st, stop=False))
                T.op("tensor", fns, reads=(wdb, b_h), writes=tuple(psC[i][1] for i in range(8)))
            fns = []
            for ti_in in range(2):
                tloc = (tp * 2 + ti_in) * 128
                for ncc in range(4):
                    bk = ti_in * 4 + ncc
                    fns.append(lambda e, t=tloc, p=psC[bk][0]: e.matmul(p[:], csh[0:1, t:t + 128], c_n128[0:1, :], start=False, stop=True))
            T.op("tensor", fns, reads=(b_csh, b_n128), writes=tuple(psC[i][1] for i in range(8)))

            for ti_in in range(2):
                ti = th * 4 + tp * 2 + ti_in
                y2t, b_y2t = y2[ti_in]
                T.dma("sync", "x2r", lambda e, t=ti, d=x2r: e.dma_start(out=d[:], in_=x2_scr[t * 128:(t + 1) * 128, :]),
                      reads=(b_x2scr,), writes=(b_x2r,))
                for ncc in range(4):
                    bk = ti_in * 4 + ncc
                    pt, pb2 = psC[bk]
                    T.op("vector", lambda e, p=pt, n=ncc, d=y2t: e.tensor_tensor(out=d[:, n * 512:(n + 1) * 512], in0=p[:],
                                                                                 in1=c_sdb[:, n * 512:(n + 1) * 512], op=ALU.mult),
                         reads=(pb2, b_sdb), writes=(b_y2t,))
                T.op("vector", lambda e, d=y2t: e.tensor_tensor(out=d[:], in0=d[:], in1=x2r[:], op=ALU.add),
                     reads=(b_y2t, b_x2r), writes=(b_y2t,))
            for ti_in in range(2):
                ti = th * 4 + tp * 2 + ti_in
                y2t, b_y2t = y2[ti_in]
                emit_ln("vector", y2t, b_y2t, st2, b_st2, otst, b_ot)
                T.dma("sync", "outw", lambda e, t=ti, s=otst: e.dma_start(out=out_my[t * 128:(t + 1) * 128, :], in_=s[:]),
                      reads=(b_ot,), writes=(b_out,))

    h = T.sems["d_outw"]
    v = T.count["d_outw"]
    T.ops["sync"].append(lambda e, h=h, v=v: e.wait_ge(h, v))

    with nc.Block() as block:
        @block.sync
        def _(e):
            for f in T.ops["sync"]:
                f(e)

        @block.gpsimd
        def _(e):
            for f in T.ops["gpsimd"]:
                f(e)

        @block.tensor
        def _(e):
            for f in T.ops["tensor"]:
                f(e)

        @block.vector
        def _(e):
            for f in T.ops["vector"]:
                f(e)

        @block.scalar
        def _(e):
            for f in T.ops["scalar"]:
                f(e)

    pc.close()
    es.close()
    return nc, {k: sorted(v) for k, v in T.used.items()}


_NC_CACHE = []


def kernel(L0_attn_x, L0_attn_w_qkv, L0_attn_b_qkv, L0_attn_w_out, L0_attn_b_out,
           L0_ffn_wq_gate, L0_ffn_s_gate, L0_ffn_wq_up, L0_ffn_s_up,
           L0_ffn_wq_down, L0_ffn_s_down):
    x = np.ascontiguousarray(np.asarray(L0_attn_x, dtype=np.float32))
    w_qkv = np.asarray(L0_attn_w_qkv, dtype=np.float32)
    b_qkv = np.asarray(L0_attn_b_qkv, dtype=np.float32)
    w_out = np.asarray(L0_attn_w_out, dtype=np.float32)
    b_out = np.asarray(L0_attn_b_out, dtype=np.float32)
    wq_gate = np.asarray(L0_ffn_wq_gate, dtype=np.int32)
    wq_up = np.asarray(L0_ffn_wq_up, dtype=np.int32)
    wq_down = np.asarray(L0_ffn_wq_down, dtype=np.int32)
    s_gate = np.asarray(L0_ffn_s_gate, dtype=np.float32)
    s_up = np.asarray(L0_ffn_s_up, dtype=np.float32)
    s_down = np.asarray(L0_ffn_s_down, dtype=np.float32)

    woutT = np.ascontiguousarray(w_out.T)
    bo_h, bo_l = _hi_lo(b_out)
    wgT_t = np.ascontiguousarray(wq_gate.reshape(NHC, 128, NKC, 128).transpose(0, 3, 2, 1))
    wuT_t = np.ascontiguousarray(wq_up.reshape(NHC, 128, NKC, 128).transpose(0, 3, 2, 1))
    wdT = np.ascontiguousarray(wq_down.T)
    sg_pp = np.ascontiguousarray(s_gate.reshape(NHC, 128).T)
    su_pp = np.ascontiguousarray(s_up.reshape(NHC, 128).T)
    sdb = np.ascontiguousarray(np.broadcast_to(s_down, (128, D)))
    identf = np.eye(128, dtype=np.float32)
    identb = np.eye(128, dtype=ml_dtypes.bfloat16)
    ones_row = np.ones((1, 512), dtype=ml_dtypes.bfloat16)
    n128_row = np.full((1, 512), -128.0, dtype=ml_dtypes.bfloat16)
    ones_colb = np.ones((128, 1), dtype=ml_dtypes.bfloat16)

    slopes = _slopes16()
    lam = np.exp(-slopes)
    pos = np.arange(128, dtype=np.float64)
    diff = pos[:, None] - pos[None, :]

    in_maps = []
    for r in range(N_CORES):
        x_my = np.ascontiguousarray(x[:, r * 256:(r + 1) * 256, :].reshape(TOK, D))
        h0, h1 = 2 * r, 2 * r + 1
        rows = np.concatenate([np.arange(h0 * 128, (h1 + 1) * 128),
                               2048 + np.arange(h0 * 128, (h1 + 1) * 128),
                               4096 + np.arange(h0 * 128, (h1 + 1) * 128)])
        wqkvT = np.ascontiguousarray(w_qkv[rows, :].T)
        bq = b_qkv[rows]
        bqk_ = np.stack([bq[0:128], bq[128:256], bq[256:384], bq[384:512]], axis=1).astype(np.float32)
        bvh, bvl = _hi_lo(bq[512:768])
        attc = np.zeros((128, 6 * 128), dtype=np.float32)
        lamc = np.zeros((128, 2), dtype=np.float32)
        dkpp = np.zeros((128, 2), dtype=np.float32)
        for hl, hh in enumerate((h0, h1)):
            lh = lam[hh]
            attc[:, hl * 128:(hl + 1) * 128] = (lh ** (pos + 1.0))[None, :]
            attc[:, 256 + hl * 128:256 + (hl + 1) * 128] = (lh ** (127.0 - pos))[None, :]
            dmT = np.where(diff >= 0, lh ** np.maximum(diff, 0.0), 0.0).T
            attc[:, 512 + hl * 128:512 + (hl + 1) * 128] = dmT.astype(np.float32)
            lamc[:, hl] = lh ** 128.0
            dkpp[:, hl] = (lh ** (127.0 - pos)).astype(np.float32)
        in_maps.append(dict(
            x_my=x_my, wqkvT=wqkvT, bqk=bqk_,
            bv_h=bvh.reshape(1, 256), bv_l=bvl.reshape(1, 256),
            woutT=woutT, bout_h=bo_h.reshape(1, D), bout_l=bo_l.reshape(1, D),
            wgT_t=wgT_t, wuT_t=wuT_t, wdT=wdT,
            sg_pp=sg_pp, su_pp=su_pp, sdb=sdb,
            attc=attc, lamc=lamc, dk_pp=dkpp,
            identf=identf, identb=identb,
            ones_row=ones_row, n128_row=n128_row, ones_colb=ones_colb,
        ))

    if not _NC_CACHE:
        _, used = build_nc()
        nc, _ = build_nc(needed=used)
        _NC_CACHE.append(nc)
    nc = _NC_CACHE[0]
    res = run_bass_kernel_spmd(nc, in_maps, core_ids=list(range(N_CORES)))
    out = np.empty((B, N, D), dtype=np.float32)
    for r in range(N_CORES):
        out[:, r * 256:(r + 1) * 256, :] = res.results[r]["out_my"].reshape(B, 256, D)
    return out


# version 20 (final)
# speedup vs baseline: 1.1015x; 1.0018x over previous
import math
from contextlib import ExitStack

import numpy as np
import ml_dtypes

import concourse.bass as bass
import concourse.mybir as mybir
from concourse.bass_utils import run_bass_kernel_spmd
from concourse.aot_env import allow_ici_all_to_all

F32 = mybir.dt.float32
BF16 = mybir.dt.bfloat16
I32 = mybir.dt.int32
ACT = mybir.ActivationFunctionType
ALU = mybir.AluOpType

N_CORES = 8
B, N, D = 4, 2048, 2048
HID = 5632
NKC = D // 128
NHC = HID // 128
TOK = 1024
NT = TOK // 128
EPS = 1e-5

ENGINES = ("sync", "gpsimd", "tensor", "vector", "scalar")


class Buf:
    __slots__ = ("name", "w", "readers")

    def __init__(self, name):
        self.name = name
        self.w = None
        self.readers = []


class Tracker:
    def __init__(self, nc, es):
        self.nc = nc
        self.es = es
        self.ops = {e: [] for e in ENGINES}
        self.sems = {}
        self.count = {}
        self.waited = {e: {} for e in ENGINES}
        self.eng_of_sem = {}
        for e in ENGINES:
            self._mk("p_" + e, e)

    def _mk(self, name, eng=None):
        if name not in self.sems:
            self.sems[name] = self.es.enter_context(self.nc.semaphore(name))
            self.count[name] = 0
            self.eng_of_sem[name] = eng
        return name

    def wait(self, eng, tag):
        if tag is None:
            return
        sem, val = tag
        if val <= 0:
            return
        if self.eng_of_sem.get(sem) == eng and eng == "tensor":
            return
        if self.waited[eng].get(sem, 0) >= val:
            return
        self.waited[eng][sem] = val
        h = self.sems[sem]
        self.ops[eng].append(lambda e, h=h, v=val: e.wait_ge(h, v))

    def _pre(self, eng, reads, writes):
        for b in reads:
            self.wait(eng, b.w)
        for b in writes:
            self.wait(eng, b.w)
            for t in b.readers:
                self.wait(eng, t)

    def op(self, eng, fns, reads=(), writes=(), self_sync=False):
        if not isinstance(fns, (list, tuple)):
            fns = [fns]
        self._pre(eng, reads, writes)
        sem = "p_" + eng
        self.count[sem] += 1
        v = self.count[sem]
        h = self.sems[sem]
        last = fns[-1]
        fns = list(fns[:-1]) + [lambda e, f=last, h=h: f(e).then_inc(h, 1)]
        t = (sem, v)
        for f in fns:
            self.ops[eng].append(f)
        for b in writes:
            b.w = t
            b.readers = []
        for b in reads:
            b.readers.append(t)
        return t

    def dma(self, eng, slot, fn, reads=(), writes=()):
        sem = self._mk("d_" + slot)
        self._pre(eng, reads, writes)
        self.count[sem] += 16
        v = self.count[sem]
        h = self.sems[sem]
        self.ops[eng].append(lambda e, f=fn, h=h: f(e).then_inc(h, 16))
        t = (sem, v)
        for b in writes:
            b.w = t
            b.readers = []
        for b in reads:
            b.readers.append(t)
        return t

    def barrier(self, bufs):
        tags = []
        for b in bufs:
            if b.w:
                tags.append(b.w)
            tags.extend(b.readers)
        for e in ENGINES:
            for t in tags:
                self.wait(e, t)


def _slopes16():
    start = 2.0 ** (-(2.0 ** (-(math.log2(16) - 3))))
    return np.array([start ** (i + 1) for i in range(16)], dtype=np.float64)


def _hi_lo(x):
    x = np.asarray(x, dtype=np.float32)
    hi = x.astype(ml_dtypes.bfloat16)
    lo = (x - hi.astype(np.float32)).astype(ml_dtypes.bfloat16)
    return np.ascontiguousarray(hi), np.ascontiguousarray(lo)


def build_nc():
    nc = bass.Bass("TRN2", target_bir_lowering=False)

    dt_in = lambda n, s, d: nc.dram_tensor(n, s, d, kind="ExternalInput").ap()
    x_my = dt_in("x_my", [TOK, D], F32)
    wqkvT = dt_in("wqkvT", [D, 768], F32)
    bqk = dt_in("bqk", [128, 4], F32)
    bq_rh = dt_in("bq_rh", [1, 512], BF16)
    bq_rl = dt_in("bq_rl", [1, 512], BF16)
    bv_h = dt_in("bv_h", [1, 256], BF16)
    bv_l = dt_in("bv_l", [1, 256], BF16)
    woutT = dt_in("woutT", [D, D], F32)
    bout_h = dt_in("bout_h", [1, D], BF16)
    bout_l = dt_in("bout_l", [1, D], BF16)
    wgT_t = dt_in("wgT_t", [NHC, 128, NKC, 128], I32)
    wuT_t = dt_in("wuT_t", [NHC, 128, NKC, 128], I32)
    wdT = dt_in("wdT", [HID, D], I32)
    sg_pp = dt_in("sg_pp", [128, NHC], F32)
    su_pp = dt_in("su_pp", [128, NHC], F32)
    sdb = dt_in("sdb", [128, D], F32)
    attc = dt_in("attc", [128, 6 * 128], F32)
    lamc = dt_in("lamc", [128, 2], F32)
    dk_pp = dt_in("dk_pp", [128, 2], F32)
    identf = dt_in("identf", [128, 128], F32)
    identb = dt_in("identb", [128, 128], BF16)
    ones_row = dt_in("ones_row", [1, 512], BF16)
    n128_row = dt_in("n128_row", [1, 512], BF16)
    ones_colb = dt_in("ones_colb", [128, 1], BF16)
    out_my = nc.dram_tensor("out_my", [TOK, D], F32, kind="ExternalOutput").ap()

    ag_in = [nc.dram_tensor(f"ag_in{i}", [128, 2, NKC, 256], BF16).ap() for i in range(4)]
    ag_out = [nc.dram_tensor(f"ag_out{i}", [8, 128, 2, NKC, 256], BF16, addr_space="Shared").ap() for i in range(4)]
    cc_in = [nc.dram_tensor(f"cc_in{i}", [8, 128, 2, 256], F32).ap() for i in range(4)]
    cc_out = [nc.dram_tensor(f"cc_out{i}", [8, 128, 2, 256], F32).ap() for i in range(4)]
    y_scr = nc.dram_tensor("y_scr", [TOK, D], F32).ap()
    x2_scr = nc.dram_tensor("x2_scr", [TOK, D], F32).ap()
    x2fm_scr = nc.dram_tensor("x2fm_scr", [2, 128, NKC, TOK], BF16).ap()

    es = ExitStack()
    T = Tracker(nc, es)

    def sb(name, shape, dtype, stack):
        t = stack.enter_context(nc.sbuf_tensor(name, shape, dtype))
        return t, Buf(name)

    def psum(name, shape, dtype, stack):
        t = stack.enter_context(nc.psum_tensor(name, shape, dtype))
        return t, Buf(name)

    c_attc, b_attc = sb("c_attc", [128, 6 * 128], F32, es)
    c_lamc, b_lamc = sb("c_lamc", [128, 2], F32, es)
    c_dk, b_dk = sb("c_dk", [128, 2], F32, es)
    c_bqk, b_bqk = sb("c_bqk", [128, 4], F32, es)
    c_bqrh, b_bqrh = sb("c_bqrh", [1, 512], BF16, es)
    c_bqrl, b_bqrl = sb("c_bqrl", [1, 512], BF16, es)
    c_bvh, b_bvh = sb("c_bvh", [1, 256], BF16, es)
    c_bvl, b_bvl = sb("c_bvl", [1, 256], BF16, es)
    c_bouth, b_bouth = sb("c_bouth", [1, D], BF16, es)
    c_boutl, b_boutl = sb("c_boutl", [1, D], BF16, es)
    c_idf, b_idf = sb("c_idf", [128, 128], F32, es)
    c_idb, b_idb = sb("c_idb", [128, 128], BF16, es)
    c_ones, b_ones = sb("c_ones", [1, 512], BF16, es)
    c_n128, b_n128 = sb("c_n128", [1, 512], BF16, es)
    c_onec, b_onec = sb("c_onec", [128, 1], BF16, es)
    c_sg, b_sg = sb("c_sg", [128, NHC], F32, es)
    c_su, b_su = sb("c_su", [128, NHC], F32, es)
    c_sdb, b_sdb = sb("c_sdb", [128, D], F32, es)

    const_list = [
        (c_attc, b_attc, attc), (c_lamc, b_lamc, lamc), (c_dk, b_dk, dk_pp), (c_bqk, b_bqk, bqk),
        (c_bvh, b_bvh, bv_h), (c_bvl, b_bvl, bv_l),
        (c_bqrh, b_bqrh, bq_rh), (c_bqrl, b_bqrl, bq_rl),
        (c_bouth, b_bouth, bout_h), (c_boutl, b_boutl, bout_l),
        (c_idf, b_idf, identf), (c_idb, b_idb, identb),
        (c_ones, b_ones, ones_row), (c_n128, b_n128, n128_row),
        (c_onec, b_onec, ones_colb),
        (c_sg, b_sg, sg_pp), (c_su, b_su, su_pp), (c_sdb, b_sdb, sdb),
    ]
    for (t, bf, src) in const_list:
        T.dma("sync", "const", lambda e, t=t, s=src: e.dma_start(out=t[:], in_=s[:, :]), writes=(bf,))
    tot = ("d_const", T.count["d_const"])
    for (_, bf, _) in const_list:
        bf.w = tot

    wqsc = ExitStack()
    wqh, b_wqh = sb("wqh", [128, NKC * 768], BF16, wqsc)
    wql, b_wql = sb("wql", [128, NKC * 768], BF16, wqsc)
    wqtsc = ExitStack()
    wqt = [sb(f"wqt{i}", [128, 768], F32, wqtsc) for i in range(NKC)]
    for kc in range(NKC):
        wtt, wtb = wqt[kc]
        T.dma("sync", f"wqf{kc % 2}", lambda e, k=kc, d=wtt: e.dma_start(out=d[:], in_=wqkvT[k * 128:(k + 1) * 128, :]), writes=(wtb,))
    for kc in range(NKC):
        wqt[kc][1].w = ("d_wqf" + str(kc % 2), T.count["d_wqf" + str(kc % 2)])

    a0 = ExitStack()
    xf = [sb(f"xf{i}", [128, D], F32, a0) for i in range(2)]
    stgh = [sb(f"stgh{i}", [128, 512], BF16, a0) for i in range(4)]
    stgl = [sb(f"stgl{i}", [128, 512], BF16, a0) for i in range(4)]
    ps_tf = [psum(f"ps_tf{i}", [128, 512], F32, a0) for i in range(4)]
    b_agin = [Buf(f"ag_in{i}") for i in range(4)]
    b_agout = [Buf(f"ag_out{i}") for i in range(4)]
    ag_slot = [[Buf(f"agslot{q}_{i}") for i in range(8)] for q in range(4)]

    def emit_ag(hf):
        T.op("gpsimd", lambda e, h=hf: e.collective_compute(
            "AllGather", ALU.bypass, replica_groups=[list(range(N_CORES))],
            ins=[ag_in[h].tensor.ap().opt()], outs=[ag_out[h].tensor.ap().opt()]),
            reads=tuple(ag_slot[hf]), writes=(b_agout[hf],))

    for ti in range(NT):
        bi = ti % 2
        xft, xfb = xf[bi]
        T.dma("sync", f"xf{bi}", lambda e, t=ti, d=xft: e.dma_start(out=d[:], in_=x_my[t * 128:(t + 1) * 128, :]), writes=(xfb,))
        hf = ti // 2
        for q in range(4):
            sl = (ti * 4 + q) % 4
            pst, psb = ps_tf[sl]
            fns = []
            for k in range(4):
                kc = q * 4 + k
                fns.append(lambda e, s=xft, k2=kc, p=pst, kk=k:
                           e.transpose(p[:, kk * 128:(kk + 1) * 128], s[:, k2 * 128:(k2 + 1) * 128], c_idf[:]))
            T.op("tensor", fns, reads=(xfb, b_idf), writes=(psb,))
            ht, hb = stgh[sl]
            lt, lb = stgl[sl]
            T.op("scalar", lambda e, p=pst, d=ht: e.copy(out=d[:], in_=p[:]), reads=(psb,), writes=(hb,))
            T.op("vector", lambda e, p=pst, d1=ht, d2=lt: e.tensor_tensor(out=d2[:], in0=p[:], in1=d1[:], op=ALU.subtract),
                 reads=(psb, hb), writes=(lb,))
            T.dma("scalar", f"agwh{sl}", lambda e, s=ht, q2=q, t=ti % 2, f=hf:
                  e.dma_start(out=ag_in[f][:, 0, q2 * 4:(q2 + 1) * 4, t * 128:(t + 1) * 128],
                              in_=s[:].rearrange("p (k t) -> p k t", k=4)),
                  reads=(hb,), writes=(ag_slot[hf][sl],))
            T.dma("scalar", f"agwl{sl}", lambda e, s=lt, q2=q, t=ti % 2, f=hf:
                  e.dma_start(out=ag_in[f][:, 1, q2 * 4:(q2 + 1) * 4, t * 128:(t + 1) * 128],
                              in_=s[:].rearrange("p (k t) -> p k t", k=4)),
                  reads=(lb,), writes=(ag_slot[hf][4 + sl],))
        if ti % 2 == 1:
            emit_ag(ti // 2)

    T.barrier([x[1] for x in xf + stgh + stgl + ps_tf])
    a0.close()

    for kc in range(NKC):
        wtt, wtb = wqt[kc]
        T.op("scalar", lambda e, k=kc, s=wtt: e.copy(out=wqh[:, k * 768:(k + 1) * 768], in_=s[:]),
             reads=(wtb,), writes=(b_wqh,))
        T.op("vector", lambda e, k=kc, s=wtt: e.tensor_tensor(out=wql[:, k * 768:(k + 1) * 768], in0=s[:],
                                                              in1=wqh[:, k * 768:(k + 1) * 768], op=ALU.subtract),
             reads=(wtb, b_wqh), writes=(b_wql,))
    T.barrier([x[1] for x in wqt])
    wqtsc.close()
    am = ExitStack()

    xg = [sb(f"xg{i}", [128, 2 * NKC * 512], BF16, am) for i in range(2)]
    qkv_q, b_q = sb("qkv_q", [128, 2048], BF16, am)
    qkv_k, b_k = sb("qkv_k", [128, 2048], BF16, am)
    qkv_v, b_v = sb("qkv_v", [128, 2048], BF16, am)
    sbS, b_S = sb("sbS", [128, 256], F32, am)
    sbSh, b_Sh = sb("sbSh", [128, 256], BF16, am)
    sbSl, b_Sl = sb("sbSl", [128, 256], BF16, am)
    smf = [sb(f"smf{i}", [128, 256], F32, am) for i in range(2)]
    smh = [sb(f"smh{i}", [128, 256], BF16, am) for i in range(2)]
    sml = [sb(f"sml{i}", [128, 256], BF16, am) for i in range(2)]
    ktf = [sb(f"ktf{i}", [128, 256], F32, am) for i in range(2)]
    ktg = [sb(f"ktg{i}", [128, 256], F32, am) for i in range(2)]
    ktdh = [sb(f"ktdh{i}", [128, 256], BF16, am) for i in range(2)]
    ktdl = [sb(f"ktdl{i}", [128, 256], BF16, am) for i in range(2)]
    qdf = [sb(f"qdf{i}", [128, 256], F32, am) for i in range(2)]
    qdh = [sb(f"qdh{i}", [128, 256], BF16, am) for i in range(2)]
    qdl = [sb(f"qdl{i}", [128, 256], BF16, am) for i in range(2)]
    astg = [sb(f"astg{i}", [128, 256], F32, am) for i in range(2)]
    ps = [psum(f"psA{i}", [128, 512], F32, am) for i in range(6)]
    pkt = [psum(f"pkt{i}", [128, 512], BF16, am) for i in range(2)]
    b_ccin = [Buf(f"cc_in{i}") for i in range(4)]

    DQ0, DK0, DM0 = 0, 256, 512

    def chunk_views(c):
        pp = c % 2
        def qs(h, lo):
            o = 1024 * lo + h * 512 + c * 128
            return qkv_q[:, o:o + 128]
        def ks(h, lo):
            o = 1024 * lo + h * 512 + c * 128
            return qkv_k[:, o:o + 128]
        def vs(h, lo):
            o = 1024 * lo + c * 256 + h * 128
            return qkv_v[:, o:o + 128]
        return pp, qs, ks, vs

    def emit_chunk_front(b, g, c):
        pp, qs, ks, vs = chunk_views(c)
        psS_t, psS_b = ps[pp]
        pkt_t, pkt_b = pkt[pp]
        ktft, ktfb = ktf[pp]
        qdft, qdfb = qdf[pp]
        qdht, qdhb = qdh[pp]
        qdlt, qdlb = qdl[pp]
        smft, smfb = smf[pp]
        smht, smhb = smh[pp]
        smlt, smlb = sml[pp]
        ktdht, ktdhb = ktdh[pp]
        ktdlt, ktdlb = ktdl[pp]

        fns = []
        for h in range(2):
            fns += [
                lambda e, h2=h: e.matmul(psS_t[:, h2 * 128:(h2 + 1) * 128], ks(h2, 0), qs(h2, 0), start=True, stop=False),
                lambda e, h2=h: e.matmul(psS_t[:, h2 * 128:(h2 + 1) * 128], ks(h2, 0), qs(h2, 1), start=False, stop=False),
                lambda e, h2=h: e.matmul(psS_t[:, h2 * 128:(h2 + 1) * 128], ks(h2, 1), qs(h2, 0), start=False, stop=True),
            ]
        T.op("tensor", fns, reads=(b_k, b_q), writes=(psS_b,))
        T.op("tensor", [
            lambda e: e.transpose(pkt_t[:, 0:128], ks(0, 0), c_idb[:]),
            lambda e: e.transpose(pkt_t[:, 128:256], ks(1, 0), c_idb[:]),
            lambda e: e.transpose(pkt_t[:, 256:384], ks(0, 1), c_idb[:]),
            lambda e: e.transpose(pkt_t[:, 384:512], ks(1, 1), c_idb[:]),
        ], reads=(b_k, b_idb), writes=(pkt_b,))

        T.op("vector", [
            lambda e: e.tensor_tensor(out=qdft[:, 0:128], in0=qs(0, 0), in1=qs(0, 1), op=ALU.add),
            lambda e: e.tensor_tensor(out=qdft[:, 128:256], in0=qs(1, 0), in1=qs(1, 1), op=ALU.add),
        ], reads=(b_q,), writes=(qdfb,))
        T.op("vector", lambda e: e.tensor_tensor(out=qdft[:], in0=qdft[:], in1=c_attc[:, DQ0:DQ0 + 256], op=ALU.mult),
             reads=(qdfb, b_attc), writes=(qdfb,))
        T.op("vector", lambda e: e.tensor_copy(out=qdht[:], in_=qdft[:]), reads=(qdfb,), writes=(qdhb,))
        T.op("vector", lambda e: e.tensor_tensor(out=qdlt[:], in0=qdft[:], in1=qdht[:], op=ALU.subtract),
             reads=(qdfb, qdhb), writes=(qdlb,))

        T.op("vector", lambda e: e.tensor_tensor(out=smft[:], in0=psS_t[:, 0:256], in1=c_attc[:, DM0:DM0 + 256], op=ALU.mult),
             reads=(psS_b, b_attc), writes=(smfb,))
        T.op("vector", lambda e: e.tensor_copy(out=smht[:], in_=smft[:]), reads=(smfb,), writes=(smhb,))
        T.op("vector", lambda e: e.tensor_tensor(out=smlt[:], in0=smft[:], in1=smht[:], op=ALU.subtract),
             reads=(smfb, smhb), writes=(smlb,))

        ktgt, ktgb = ktg[pp]
        T.op("vector", [
            lambda e: e.tensor_scalar_mul(ktft[:, 0:128], pkt_t[:, 0:128], c_dk[:, 0:1]),
            lambda e: e.tensor_scalar_mul(ktft[:, 128:256], pkt_t[:, 128:256], c_dk[:, 1:2]),
        ], reads=(pkt_b, b_dk), writes=(ktfb,))
        T.op("vector", [
            lambda e: e.scalar_tensor_tensor(out=ktgt[:, 0:128], in0=pkt_t[:, 256:384], scalar=c_dk[:, 0:1], in1=ktft[:, 0:128], op0=ALU.mult, op1=ALU.add),
            lambda e: e.scalar_tensor_tensor(out=ktgt[:, 128:256], in0=pkt_t[:, 384:512], scalar=c_dk[:, 1:2], in1=ktft[:, 128:256], op0=ALU.mult, op1=ALU.add),
        ], reads=(pkt_b, b_dk, ktfb), writes=(ktgb,))
        T.op("vector", lambda e: e.tensor_copy(out=ktdht[:], in_=ktgt[:]), reads=(ktgb,), writes=(ktdhb,))
        T.op("vector", lambda e: e.tensor_tensor(out=ktdlt[:], in0=ktgt[:], in1=ktdht[:], op=ALU.subtract),
             reads=(ktgb, ktdhb), writes=(ktdlb,))

    def emit_chunk_back(b, g, c):
        gc = g * 4 + c
        pp, qs, ks, vs = chunk_views(c)
        psU_t, psU_b = ps[2 + pp]
        psV_t, psV_b = ps[4 + pp]
        qdht, qdhb = qdh[pp]
        qdlt, qdlb = qdl[pp]
        smht, smhb = smh[pp]
        smlt, smlb = sml[pp]
        ktdht, ktdhb = ktdh[pp]
        ktdlt, ktdlb = ktdl[pp]

        fns = []
        for h in range(2):
            fns += [
                lambda e, h2=h: e.matmul(psV_t[:, h2 * 128:(h2 + 1) * 128], vs(h2, 0), smht[:, h2 * 128:(h2 + 1) * 128], start=True, stop=False),
                lambda e, h2=h: e.matmul(psV_t[:, h2 * 128:(h2 + 1) * 128], vs(h2, 1), smht[:, h2 * 128:(h2 + 1) * 128], start=False, stop=False),
                lambda e, h2=h: e.matmul(psV_t[:, h2 * 128:(h2 + 1) * 128], vs(h2, 0), smlt[:, h2 * 128:(h2 + 1) * 128], start=False, stop=False),
                lambda e, h2=h: e.matmul(psV_t[:, h2 * 128:(h2 + 1) * 128], sbSh[:, h2 * 128:(h2 + 1) * 128], qdht[:, h2 * 128:(h2 + 1) * 128], start=False, stop=False),
                lambda e, h2=h: e.matmul(psV_t[:, h2 * 128:(h2 + 1) * 128], sbSl[:, h2 * 128:(h2 + 1) * 128], qdht[:, h2 * 128:(h2 + 1) * 128], start=False, stop=False),
                lambda e, h2=h: e.matmul(psV_t[:, h2 * 128:(h2 + 1) * 128], sbSh[:, h2 * 128:(h2 + 1) * 128], qdlt[:, h2 * 128:(h2 + 1) * 128], start=False, stop=True),
            ]
        T.op("tensor", fns, reads=(b_v, smhb, smlb, b_Sh, b_Sl, qdhb, qdlb), writes=(psV_b,))
        fns = []
        for h in range(2):
            fns += [
                lambda e, h2=h: e.matmul(psU_t[:, h2 * 128:(h2 + 1) * 128], ktdht[:, h2 * 128:(h2 + 1) * 128], vs(h2, 0), start=True, stop=False),
                lambda e, h2=h: e.matmul(psU_t[:, h2 * 128:(h2 + 1) * 128], ktdlt[:, h2 * 128:(h2 + 1) * 128], vs(h2, 0), start=False, stop=False),
                lambda e, h2=h: e.matmul(psU_t[:, h2 * 128:(h2 + 1) * 128], ktdht[:, h2 * 128:(h2 + 1) * 128], vs(h2, 1), start=False, stop=True),
            ]
        T.op("tensor", fns, reads=(ktdhb, ktdlb, b_v), writes=(psU_b,))

        T.op("vector", [
            lambda e: e.scalar_tensor_tensor(out=sbS[:, 0:128], in0=sbS[:, 0:128], scalar=c_lamc[:, 0:1], in1=psU_t[:, 0:128], op0=ALU.mult, op1=ALU.add),
            lambda e: e.scalar_tensor_tensor(out=sbS[:, 128:256], in0=sbS[:, 128:256], scalar=c_lamc[:, 1:2], in1=psU_t[:, 128:256], op0=ALU.mult, op1=ALU.add),
        ], reads=(psU_b, b_lamc), writes=(b_S,))
        T.op("vector", lambda e: e.tensor_copy(out=sbSh[:], in_=sbS[:]), reads=(b_S,), writes=(b_Sh,))
        T.op("vector", lambda e: e.tensor_tensor(out=sbSl[:], in0=sbS[:], in1=sbSh[:], op=ALU.subtract),
             reads=(b_S, b_Sh), writes=(b_Sl,))

        stt, stb = astg[pp]
        T.op("scalar", lambda e: e.copy(out=stt[:], in_=psV_t[:, 0:256]), reads=(psV_b,), writes=(stb,))
        j, toff = gc // 2, (gc % 2) * 128
        T.dma("gpsimd", f"ccw{pp}", lambda e, j2=j, t=toff, s=stt:
              e.dma_start(out=cc_in[b][j2, :, :, t:t + 128], in_=s[:].rearrange("p (h t) -> p h t", h=2)),
              reads=(stb,), writes=(b_ccin[b],))

    def emit_group(b, g):
        gi = b * 4 + g
        bi = gi % 2
        xgt, xgb = xg[bi]
        for hl in range(2):
            for si in range(2):
                seg = 2 * g + si
                T.dma("sync", f"xg{bi}_{hl}", lambda e, h=hl, sg=seg, s2=si, d=xgt, f=b:
                      e.dma_start(out=d[:, h * NKC * 512:(h + 1) * NKC * 512]
                                  .rearrange("p (k t) -> p k t", k=NKC)[:, :, s2 * 256:s2 * 256 + 256],
                                  in_=ag_out[f][sg, :, h, :, :]),
                      reads=(b_agout[b],), writes=(xgb,))

        for fc in range(4):
            pst, psb = ps[fc]
            fns = []
            for kc in range(NKC):
                for p in range(3):
                    wt = wqh if p in (0, 2) else wql
                    xs = 0 if p in (0, 1) else 1
                    fns.append(lambda e, w=wt, k=kc, f=fc, x=xs, d=xgt, bk=pst, st=(kc == 0 and p == 0):
                               e.matmul(bk[:], w[:, k * 768 + f * 128:k * 768 + (f + 1) * 128],
                                        d[:, x * NKC * 512 + k * 512:x * NKC * 512 + (k + 1) * 512], start=st, stop=False))
            fns.append(lambda e, f=fc, bk=pst: e.matmul(bk[:], c_bqrh[0:1, f * 128:(f + 1) * 128], c_ones[0:1, :], start=False, stop=False))
            fns.append(lambda e, f=fc, bk=pst: e.matmul(bk[:], c_bqrl[0:1, f * 128:(f + 1) * 128], c_ones[0:1, :], start=False, stop=True))
            T.op("tensor", fns, reads=(xgb, b_wqh, b_wql, b_bqrh, b_bqrl, b_ones), writes=(psb,))
        for sl in range(4):
            bank, half = 4 + sl // 2, (sl % 2) * 256
            pst, psb = ps[bank]
            fns = []
            for kc in range(NKC):
                for p in range(3):
                    wt = wqh if p in (0, 2) else wql
                    xs = 0 if p in (0, 1) else 1
                    fns.append(lambda e, w=wt, k=kc, s=sl, bk=pst, hf=half, x=xs, d=xgt, st=(kc == 0 and p == 0):
                               e.matmul(bk[:, hf:hf + 256], d[:, x * NKC * 512 + k * 512 + s * 128:x * NKC * 512 + k * 512 + (s + 1) * 128],
                                        w[:, k * 768 + 512:(k + 1) * 768], start=st, stop=False))
            fns.append(lambda e, bk=pst, hf=half: e.matmul(bk[:, hf:hf + 256], c_ones[0:1, 0:128], c_bvh[0:1, :], start=False, stop=False))
            fns.append(lambda e, bk=pst, hf=half: e.matmul(bk[:, hf:hf + 256], c_ones[0:1, 0:128], c_bvl[0:1, :], start=False, stop=True))
            T.op("tensor", fns, reads=(xgb, b_wqh, b_wql, b_ones, b_bvh, b_bvl), writes=(psb,))

        for fc in range(4):
            dstt, dstb = (qkv_q, b_q) if fc < 2 else (qkv_k, b_k)
            h = fc % 2
            pst, psb = ps[fc]
            T.op("scalar", lambda e, p=pst, d=dstt, h2=h: e.copy(out=d[:, h2 * 512:(h2 + 1) * 512], in_=p[:]),
                 reads=(psb,), writes=(dstb,))
            T.op("vector", lambda e, p=pst, d=dstt, h2=h: e.tensor_tensor(out=d[:, 1024 + h2 * 512:1024 + (h2 + 1) * 512], in0=p[:],
                                                                          in1=d[:, h2 * 512:(h2 + 1) * 512], op=ALU.subtract),
                 reads=(psb, dstb), writes=(dstb,))
        for sl in range(4):
            bank, half = 4 + sl // 2, (sl % 2) * 256
            pst, psb = ps[bank]
            T.op("scalar", lambda e, p=pst, hf=half, s=sl: e.copy(out=qkv_v[:, s * 256:(s + 1) * 256], in_=p[:, hf:hf + 256]),
                 reads=(psb,), writes=(b_v,))
            T.op("vector", lambda e, p=pst, hf=half, s=sl: e.tensor_tensor(out=qkv_v[:, 1024 + s * 256:1024 + (s + 1) * 256],
                                                                           in0=p[:, hf:hf + 256], in1=qkv_v[:, s * 256:(s + 1) * 256], op=ALU.subtract),
                 reads=(psb,), writes=(b_v,))

        emit_chunk_front(b, g, 0)
        emit_chunk_front(b, g, 1)
        emit_chunk_back(b, g, 0)
        emit_chunk_front(b, g, 2)
        emit_chunk_back(b, g, 1)
        emit_chunk_front(b, g, 3)
        emit_chunk_back(b, g, 2)
        emit_chunk_back(b, g, 3)

    b_ccout = [Buf(f"cc_out{i}") for i in range(4)]

    def _a2a(b2):
        def f(e):
            with allow_ici_all_to_all():
                return e.collective_compute("AllToAll", ALU.bypass, replica_groups=[list(range(N_CORES))],
                                            ins=[cc_in[b2].tensor.ap().opt()], outs=[cc_out[b2].tensor.ap().opt()])
        return f

    for b in range(B):
        T.op("vector", lambda e: e.memset(sbS[:], 0.0), writes=(b_S,))
        T.op("vector", lambda e: e.memset(sbSh[:], 0.0), writes=(b_Sh,))
        T.op("vector", lambda e: e.memset(sbSl[:], 0.0), writes=(b_Sl,))
        for g in range(4):
            emit_group(b, g)
        T.op("gpsimd", _a2a(b), reads=(b_ccin[b],), writes=(b_ccout[b],))

    T.barrier([b_wqh, b_wql, b_q, b_k, b_v, b_S, b_Sh, b_Sl] +
              [x[1] for x in wqt + xg + smf + smh + sml + ktf + ktg + ktdh + ktdl + qdf + qdh + qdl + astg + ps + pkt])
    am.close()
    wqsc.close()

    pb_ = ExitStack()
    wo_h, b_woh = sb("wo_h", [128, NKC * 1024], BF16, pb_)
    wo_l, b_wol = sb("wo_l", [128, NKC * 1024], BF16, pb_)
    wo_t = [sb(f"wo_t{i}", [128, 1024], F32, pb_) for i in range(3)]
    at32 = [sb(f"at32_{i}", [128, 2048], F32, pb_) for i in range(2)]
    b_a32p = [[Buf(f"a32_{i}_{p}") for p in range(2)] for i in range(2)]
    ath = [sb(f"ath{i}", [128, 2048], BF16, pb_) for i in range(2)]
    atl = [sb(f"atl{i}", [128, 2048], BF16, pb_) for i in range(2)]
    xmt = [sb(f"xmt{i}", [128, 1024], F32, pb_) for i in range(2)]
    psB = [psum(f"psB{i}", [128, 1024], F32, pb_) for i in range(2)]
    b_yscr = Buf("y_scr")

    stat, b_stat = sb("stat", [128, 16], F32, pb_)
    yt = [sb(f"yt{i}", [128, 2048], F32, pb_) for i in range(2)]
    x2t = [sb(f"x2t{i}", [128, 2048], F32, pb_) for i in range(2)]
    x2h = [sb(f"x2h{i}", [128, 2048], BF16, pb_) for i in range(3)]
    x2l = [sb(f"x2l{i}", [128, 2048], BF16, pb_) for i in range(3)]
    x2stg = [sb(f"x2stg{i}", [128, 1024], BF16, pb_) for i in range(2)]
    ps_x2 = [psum(f"ps_x2{i}", [128, 1024], BF16, pb_) for i in range(2)]
    b_x2scr = Buf("x2_scr")
    b_x2fm = Buf("x2fm_scr")

    def emit_ln(engV, y_src, y_b, st_t, st_b, out_t, out_b):
        T.op("scalar", lambda e: e.activation(out=out_t[:], in_=y_src[:], func=ACT.Square, accum_out=st_t[:, 1:2]),
             reads=(y_b,), writes=(out_b, st_b))
        T.op("scalar", lambda e: e.activation(out=out_t[:], in_=y_src[:], func=ACT.Copy, accum_out=st_t[:, 0:1]),
             reads=(y_b,), writes=(out_b, st_b))
        T.op("vector", lambda e: e.tensor_scalar_mul(st_t[:, 2:3], st_t[:, 0:1], 1.0 / D), reads=(st_b,), writes=(st_b,), self_sync=True)
        T.op("vector", lambda e: e.tensor_scalar_mul(st_t[:, 3:4], st_t[:, 1:2], 1.0 / D), reads=(st_b,), writes=(st_b,), self_sync=True)
        T.op("vector", lambda e: e.tensor_tensor(out=st_t[:, 4:5], in0=st_t[:, 2:3], in1=st_t[:, 2:3], op=ALU.mult),
             reads=(st_b,), writes=(st_b,), self_sync=True)
        T.op("vector", lambda e: e.tensor_tensor(out=st_t[:, 5:6], in0=st_t[:, 3:4], in1=st_t[:, 4:5], op=ALU.subtract),
             reads=(st_b,), writes=(st_b,), self_sync=True)
        T.op("vector", lambda e: e.tensor_scalar_add(st_t[:, 6:7], st_t[:, 5:6], EPS), reads=(st_b,), writes=(st_b,), self_sync=True)
        T.op("scalar", lambda e: e.activation(out=st_t[:, 7:8], in_=st_t[:, 6:7], func=ACT.Sqrt), reads=(st_b,), writes=(st_b,))
        T.op("vector", lambda e: e.reciprocal(st_t[:, 8:9], st_t[:, 7:8]), reads=(st_b,), writes=(st_b,), self_sync=True)
        T.op("vector", lambda e: e.tensor_tensor(out=st_t[:, 9:10], in0=st_t[:, 2:3], in1=st_t[:, 8:9], op=ALU.mult),
             reads=(st_b,), writes=(st_b,), self_sync=True)
        T.op("vector", lambda e: e.tensor_scalar_mul(st_t[:, 10:11], st_t[:, 9:10], -1.0), reads=(st_b,), writes=(st_b,), self_sync=True)
        T.op("vector", lambda e: e.tensor_scalar(out=out_t[:], in0=y_src[:], scalar1=st_t[:, 8:9], scalar2=st_t[:, 10:11],
                                                 op0=ALU.mult, op1=ALU.add),
             reads=(y_b, st_b), writes=(out_b,))

    def emit_ln1_stats(ti):
        bi = ti % 2
        ytt, ytb = yt[bi]
        x2tt, x2tb = x2t[bi]
        emit_ln("vector", ytt, ytb, stat, b_stat, x2tt, x2tb)
        T.dma("gpsimd", f"x2w{bi}", lambda e, t=ti, s=x2tt: e.dma_start(out=x2_scr[t * 128:(t + 1) * 128, :], in_=s[:]),
              reads=(x2tb,), writes=(b_x2scr,))
        x2ht, x2hb = x2h[ti % 3]
        x2lt, x2lb = x2l[ti % 3]
        T.op("scalar", lambda e, s=x2tt, d=x2ht: e.copy(out=d[:], in_=s[:]), reads=(x2tb,), writes=(x2hb,))
        T.op("vector", lambda e, s=x2tt, d1=x2ht, d2=x2lt: e.tensor_tensor(out=d2[:], in0=s[:], in1=d1[:], op=ALU.subtract),
             reads=(x2tb, x2hb), writes=(x2lb,))

    def emit_ln1_transposes(ti):
        x2ht, x2hb = x2h[ti % 3]
        x2lt, x2lb = x2l[ti % 3]
        for grp in range(4):
            hl, oct_ = grp // 2, grp % 2
            pbk = (ti * 4 + grp) % 2
            pst, psb = ps_x2[pbk]
            srct, srcb = (x2ht, x2hb) if hl == 0 else (x2lt, x2lb)
            fns = []
            for k in range(8):
                kc = oct_ * 8 + k
                fns.append(lambda e, s=srct, k2=kc, p=pst, kk=k:
                           e.transpose(p[:, kk * 128:(kk + 1) * 128], s[:, k2 * 128:(k2 + 1) * 128], c_idb[:]))
            T.op("tensor", fns, reads=(srcb, b_idb), writes=(psb,))
            stt, stb = x2stg[pbk]
            T.op("scalar", lambda e, p=pst, d=stt: e.copy(out=d[:], in_=p[:]), reads=(psb,), writes=(stb,))
            T.dma("gpsimd", f"x2fw{pbk}", lambda e, s=stt, h=hl, o=oct_, t=ti:
                  e.dma_start(out=x2fm_scr[h, :, o * 8:(o + 1) * 8, t * 128:(t + 1) * 128],
                              in_=s[:].rearrange("p (k t) -> p k t", k=8)),
                  reads=(stb,), writes=(b_x2fm,))

    for half in range(2):
        for kc in range(NKC):
            wtt, wtb = wo_t[kc % 3]
            T.dma("sync", f"wof{kc % 3}", lambda e, k=kc, h=half, d=wtt:
                  e.dma_start(out=d[:], in_=woutT[k * 128:(k + 1) * 128, h * 1024:(h + 1) * 1024]), writes=(wtb,))
            T.op("scalar", lambda e, k=kc, s=wtt: e.copy(out=wo_h[:, k * 1024:(k + 1) * 1024], in_=s[:]),
                 reads=(wtb,), writes=(b_woh,))
            T.op("vector", lambda e, k=kc, s=wtt: e.tensor_tensor(out=wo_l[:, k * 1024:(k + 1) * 1024], in0=s[:],
                                                                  in1=wo_h[:, k * 1024:(k + 1) * 1024], op=ALU.subtract),
                 reads=(wtb, b_woh), writes=(b_wol,))
        def emit_at_prep(ti):
            bi = ti % 2
            a32t, a32b = at32[bi]
            atht, athb = ath[bi]
            atlt, atlb = atl[bi]
            for j in range(8):
                T.dma("sync", f"at{bi}_{j % 2}", lambda e, j2=j, b2=ti // 2, t2=(ti % 2) * 128, d=a32t:
                      e.dma_start(out=d[:, j2 * 256:(j2 + 1) * 256].rearrange("p (h t) -> p h t", h=2),
                                  in_=cc_out[b2][j2, :, :, t2:t2 + 128]),
                      reads=(b_ccout[ti // 2],), writes=(b_a32p[bi][j % 2],))
            a32rd = (b_a32p[bi][0], b_a32p[bi][1])
            T.op("scalar", lambda e, s=a32t, d=atht: e.copy(out=d[:], in_=s[:]), reads=a32rd, writes=(athb,))
            T.op("vector", lambda e, s=a32t, d1=atht, d2=atlt: e.tensor_tensor(out=d2[:], in0=s[:], in1=d1[:], op=ALU.subtract),
                 reads=a32rd + (athb,), writes=(atlb,))
            xmtt, xmtb = xmt[bi]
            T.dma("sync", f"xm{bi}", lambda e, t=ti, h=half, d=xmtt:
                  e.dma_start(out=d[:], in_=x_my[t * 128:(t + 1) * 128, h * 1024:(h + 1) * 1024]), writes=(xmtb,))

        emit_at_prep(0)
        for ti in range(NT):
            bi = ti % 2
            atht, athb = ath[bi]
            atlt, atlb = atl[bi]
            xmtt, xmtb = xmt[bi]
            if half == 1:
                ytt0, ytb0 = yt[bi]
                T.dma("sync", f"yld{bi}", lambda e, t=ti, d=ytt0: e.dma_start(out=d[:, 0:1024], in_=y_scr[t * 128:(t + 1) * 128, 0:1024]),
                      reads=(b_yscr,), writes=(ytb0,))
            pst, psb = psB[bi]
            fns = []
            for ncc in range(2):
                for kc in range(NKC):
                    for p in range(3):
                        wt = wo_h if p in (0, 2) else wo_l
                        asrc = atht if p in (0, 1) else atlt
                        fns.append(lambda e, w=wt, a=asrc, k=kc, n=ncc, bk=pst, st=(kc == 0 and p == 0):
                                   e.matmul(bk[:, n * 512:(n + 1) * 512], a[:, k * 128:(k + 1) * 128],
                                            w[:, k * 1024 + n * 512:k * 1024 + (n + 1) * 512], start=st, stop=False))
                fns.append(lambda e, n=ncc, h=half, bk=pst: e.matmul(bk[:, n * 512:(n + 1) * 512], c_ones[0:1, 0:128],
                                                                     c_bouth[0:1, h * 1024 + n * 512:h * 1024 + (n + 1) * 512], start=False, stop=False))
                fns.append(lambda e, n=ncc, h=half, bk=pst: e.matmul(bk[:, n * 512:(n + 1) * 512], c_ones[0:1, 0:128],
                                                                     c_boutl[0:1, h * 1024 + n * 512:h * 1024 + (n + 1) * 512], start=False, stop=True))
            T.op("tensor", fns, reads=(athb, atlb, b_woh, b_wol, b_ones, b_bouth, b_boutl), writes=(psb,))
            if ti + 1 < NT:
                emit_at_prep(ti + 1)
            if half == 0:
                yst, ysb = yt[bi]
                T.op("vector", lambda e, p=pst, x=xmtt, d=yst: e.scalar_tensor_tensor(out=d[:, 0:1024], in0=p[:], scalar=1.0, in1=x[:], op0=ALU.mult, op1=ALU.add),
                     reads=(psb, xmtb), writes=(ysb,))
                T.dma("gpsimd", f"yw{bi}", lambda e, t=ti, s=yst:
                      e.dma_start(out=y_scr[t * 128:(t + 1) * 128, 0:1024], in_=s[:, 0:1024]), reads=(ysb,), writes=(b_yscr,))
            else:
                ytt0, ytb0 = yt[bi]
                T.op("vector", lambda e, p=pst, x=xmtt, d=ytt0: e.scalar_tensor_tensor(out=d[:, 1024:2048], in0=p[:], scalar=1.0, in1=x[:], op0=ALU.mult, op1=ALU.add),
                     reads=(psb, xmtb, ytb0), writes=(ytb0,))
                emit_ln1_stats(ti)
                if ti >= 2:
                    emit_ln1_transposes(ti - 2)
                if ti == NT - 1:
                    emit_ln1_transposes(ti - 1)
                    emit_ln1_transposes(ti)

    T.barrier([b_woh, b_wol, b_stat] +
              [x[1] for x in wo_t + at32 + ath + atl + xmt + psB + yt + x2t + x2h + x2l + x2stg + ps_x2])
    pb_.close()

    pc = ExitStack()
    x2f, b_x2f = sb("x2f", [128, 2 * NKC * 512], BF16, pc)
    hbuf, b_h = sb("hbuf", [128, 2 * NHC * 512], BF16, pc)
    wg = [sb(f"wg{i}", [128, NKC * 128], BF16, pc) for i in range(2)]
    wu = [sb(f"wu{i}", [128, NKC * 128], BF16, pc) for i in range(2)]
    wd = [sb(f"wd{i}", [128, D], BF16, pc) for i in range(3)]
    gsil, b_gsil = sb("gsil", [128, 512], F32, pc)
    utmp, b_utmp = sb("utmp", [128, 512], F32, pc)
    h32, b_h32 = sb("h32", [128, 512], F32, pc)
    csx2, b_csx2 = sb("csx2", [1, 512], BF16, pc)
    csh, b_csh = sb("csh", [1, 512], BF16, pc)
    x2r, b_x2r = sb("x2r", [128, 2048], F32, pc)
    y2 = [sb(f"y2t{i}", [128, 2048], F32, pc) for i in range(2)]
    otst, b_ot = x2r, b_x2r
    st2, b_st2 = sb("st2", [128, 16], F32, pc)
    psC = [psum(f"psC{i}", [128, 512], F32, pc) for i in range(8)]
    b_out = Buf("out_my")

    wg_preloaded = set()

    def emit_x2f_loads(th):
        for hl in range(2):
            T.dma("sync", f"x2f{hl}", lambda e, h=hl, t=th:
                  e.dma_start(out=x2f[:, h * NKC * 512:(h + 1) * NKC * 512].rearrange("p (k t) -> p k t", k=NKC),
                              in_=x2fm_scr[h, :, :, t * 512:(t + 1) * 512]),
                  reads=(b_x2fm,), writes=(b_x2f,))

    emit_x2f_loads(0)
    for th in range(2):
        pst, psb = psC[0]
        fns = []
        for hl in range(2):
            for kc in range(NKC):
                fns.append(lambda e, h=hl, k=kc, bk=pst, st=(hl == 0 and kc == 0), sp=(hl == 1 and kc == NKC - 1):
                           e.matmul(bk[0:1, :], c_onec[:, 0:1], x2f[:, h * NKC * 512 + k * 512:h * NKC * 512 + (k + 1) * 512], start=st, stop=sp))
        T.op("tensor", fns, reads=(b_x2f, b_onec), writes=(psb,))
        T.op("scalar", lambda e, bk=pst: e.copy(out=csx2[0:1, :], in_=bk[0:1, :]), reads=(psb,), writes=(b_csx2,))

        for hc in range(NHC):
            bi = hc % 2
            wgt, wgb = wg[bi]
            wut, wub = wu[bi]
            if (th, hc) not in wg_preloaded:
                T.dma("gpsimd", f"wg{bi}", lambda e, h=hc, d=wgt: e.dma_start(out=d[:].rearrange("p (k t) -> p k t", k=NKC), in_=wgT_t[h, :, :, :]),
                      writes=(wgb,))
                T.dma("gpsimd", f"wu{bi}", lambda e, h=hc, d=wut: e.dma_start(out=d[:].rearrange("p (k t) -> p k t", k=NKC), in_=wuT_t[h, :, :, :]),
                      writes=(wub,))
            pg_t, pg_b = psC[2 + bi]
            pu_t, pu_b = psC[4 + bi]
            for (wt, wb_, pt, pb2) in ((wgt, wgb, pg_t, pg_b), (wut, wub, pu_t, pu_b)):
                fns = []
                for kc in range(NKC):
                    for hl in range(2):
                        fns.append(lambda e, w=wt, k=kc, h=hl, bk=pt, st=(kc == 0 and hl == 0):
                                   e.matmul(bk[:], w[:, k * 128:(k + 1) * 128],
                                            x2f[:, h * NKC * 512 + k * 512:h * NKC * 512 + (k + 1) * 512], start=st, stop=False))
                fns.append(lambda e, bk=pt: e.matmul(bk[:], c_n128[0:1, 0:128], csx2[0:1, :], start=False, stop=True))
                T.op("tensor", fns, reads=(wb_, b_x2f, b_n128, b_csx2), writes=(pb2,))
            T.op("scalar", lambda e, p=pg_t, h=hc: e.activation(out=gsil[:], in_=p[:], func=ACT.Silu, scale=c_sg[:, h:h + 1]),
                 reads=(pg_b, b_sg), writes=(b_gsil,))
            T.op("vector", lambda e, p=pu_t, h=hc: e.tensor_scalar_mul(utmp[:], p[:], c_su[:, h:h + 1]),
                 reads=(pu_b, b_su), writes=(b_utmp,))
            T.op("vector", lambda e: e.tensor_tensor(out=h32[:], in0=gsil[:], in1=utmp[:], op=ALU.mult),
                 reads=(b_gsil, b_utmp), writes=(b_h32,))
            T.op("scalar", lambda e, h=hc: e.copy(out=hbuf[:, h * 512:(h + 1) * 512], in_=h32[:]),
                 reads=(b_h32,), writes=(b_h,))
            T.op("vector", lambda e, h=hc: e.tensor_tensor(out=hbuf[:, NHC * 512 + h * 512:NHC * 512 + (h + 1) * 512],
                                                           in0=h32[:], in1=hbuf[:, h * 512:(h + 1) * 512], op=ALU.subtract),
                 reads=(b_h32,), writes=(b_h,))

        pst, psb = psC[1]
        fns = []
        for hl in range(2):
            for hcc in range(NHC):
                fns.append(lambda e, h=hl, k=hcc, bk=pst, st=(hl == 0 and hcc == 0), sp=(hl == 1 and hcc == NHC - 1):
                           e.matmul(bk[0:1, :], c_onec[:, 0:1], hbuf[:, h * NHC * 512 + k * 512:h * NHC * 512 + (k + 1) * 512], start=st, stop=sp))
        T.op("tensor", fns, reads=(b_h, b_onec), writes=(psb,))
        T.op("scalar", lambda e, bk=pst: e.copy(out=csh[0:1, :], in_=bk[0:1, :]), reads=(psb,), writes=(b_csh,))

        if th == 0:
            emit_x2f_loads(1)
            for hc2 in (0, 1):
                bi2 = hc2 % 2
                wgt2, wgb2 = wg[bi2]
                wut2, wub2 = wu[bi2]
                T.dma("gpsimd", f"wg{bi2}", lambda e, h=hc2, d=wgt2: e.dma_start(out=d[:].rearrange("p (k t) -> p k t", k=NKC), in_=wgT_t[h, :, :, :]),
                      writes=(wgb2,))
                T.dma("gpsimd", f"wu{bi2}", lambda e, h=hc2, d=wut2: e.dma_start(out=d[:].rearrange("p (k t) -> p k t", k=NKC), in_=wuT_t[h, :, :, :]),
                      writes=(wub2,))
                wg_preloaded.add((1, hc2))

        for tp in range(2):
            for kc in range(NHC):
                bi = kc % 3
                wdt, wdb = wd[bi]
                T.dma("gpsimd", f"wd{bi}", lambda e, k=kc, d=wdt: e.dma_start(out=d[:], in_=wdT[k * 128:(k + 1) * 128, :]), writes=(wdb,))
                fns = []
                for ti_in in range(2):
                    tloc = (tp * 2 + ti_in) * 128
                    for hl in range(2):
                        for ncc in range(4):
                            bk = ti_in * 4 + ncc
                            fns.append(lambda e, w=wdt, k2=kc, h=hl, t=tloc, n=ncc, p=psC[bk][0], st=(kc == 0 and hl == 0):
                                       e.matmul(p[:], hbuf[:, h * NHC * 512 + k2 * 512 + t:h * NHC * 512 + k2 * 512 + t + 128],
                                                w[:, n * 512:(n + 1) * 512], start=st, stop=False))
                T.op("tensor", fns, reads=(wdb, b_h), writes=tuple(psC[i][1] for i in range(8)))
            fns = []
            for ti_in in range(2):
                tloc = (tp * 2 + ti_in) * 128
                for ncc in range(4):
                    bk = ti_in * 4 + ncc
                    fns.append(lambda e, t=tloc, p=psC[bk][0]: e.matmul(p[:], csh[0:1, t:t + 128], c_n128[0:1, :], start=False, stop=True))
            T.op("tensor", fns, reads=(b_csh, b_n128), writes=tuple(psC[i][1] for i in range(8)))

            for ti_in in range(2):
                ti = th * 4 + tp * 2 + ti_in
                y2t, b_y2t = y2[ti_in]
                T.dma("sync", "x2r", lambda e, t=ti, d=x2r: e.dma_start(out=d[:], in_=x2_scr[t * 128:(t + 1) * 128, :]),
                      reads=(b_x2scr,), writes=(b_x2r,))
                for ncc in range(4):
                    bk = ti_in * 4 + ncc
                    pt, pb2 = psC[bk]
                    T.op("vector", lambda e, p=pt, n=ncc, d=y2t: e.tensor_tensor(out=d[:, n * 512:(n + 1) * 512], in0=p[:],
                                                                                 in1=c_sdb[:, n * 512:(n + 1) * 512], op=ALU.mult),
                         reads=(pb2, b_sdb), writes=(b_y2t,))
                T.op("vector", lambda e, d=y2t: e.tensor_tensor(out=d[:], in0=d[:], in1=x2r[:], op=ALU.add),
                     reads=(b_y2t, b_x2r), writes=(b_y2t,))
            for ti_in in range(2):
                ti = th * 4 + tp * 2 + ti_in
                y2t, b_y2t = y2[ti_in]
                emit_ln("vector", y2t, b_y2t, st2, b_st2, otst, b_ot)
                T.dma("sync", "outw", lambda e, t=ti, s=otst: e.dma_start(out=out_my[t * 128:(t + 1) * 128, :], in_=s[:]),
                      reads=(b_ot,), writes=(b_out,))

    h = T.sems["d_outw"]
    v = T.count["d_outw"]
    T.ops["sync"].append(lambda e, h=h, v=v: e.wait_ge(h, v))

    with nc.Block() as block:
        @block.sync
        def _(e):
            for f in T.ops["sync"]:
                f(e)

        @block.gpsimd
        def _(e):
            for f in T.ops["gpsimd"]:
                f(e)

        @block.tensor
        def _(e):
            for f in T.ops["tensor"]:
                f(e)

        @block.vector
        def _(e):
            for f in T.ops["vector"]:
                f(e)

        @block.scalar
        def _(e):
            for f in T.ops["scalar"]:
                f(e)

    pc.close()
    es.close()
    return nc


_NC_CACHE = []


def kernel(L0_attn_x, L0_attn_w_qkv, L0_attn_b_qkv, L0_attn_w_out, L0_attn_b_out,
           L0_ffn_wq_gate, L0_ffn_s_gate, L0_ffn_wq_up, L0_ffn_s_up,
           L0_ffn_wq_down, L0_ffn_s_down):
    x = np.ascontiguousarray(np.asarray(L0_attn_x, dtype=np.float32))
    w_qkv = np.asarray(L0_attn_w_qkv, dtype=np.float32)
    b_qkv = np.asarray(L0_attn_b_qkv, dtype=np.float32)
    w_out = np.asarray(L0_attn_w_out, dtype=np.float32)
    b_out = np.asarray(L0_attn_b_out, dtype=np.float32)
    wq_gate = np.asarray(L0_ffn_wq_gate, dtype=np.int32)
    wq_up = np.asarray(L0_ffn_wq_up, dtype=np.int32)
    wq_down = np.asarray(L0_ffn_wq_down, dtype=np.int32)
    s_gate = np.asarray(L0_ffn_s_gate, dtype=np.float32)
    s_up = np.asarray(L0_ffn_s_up, dtype=np.float32)
    s_down = np.asarray(L0_ffn_s_down, dtype=np.float32)

    woutT = np.ascontiguousarray(w_out.T)
    bo_h, bo_l = _hi_lo(b_out)
    wgT_t = np.ascontiguousarray(wq_gate.reshape(NHC, 128, NKC, 128).transpose(0, 3, 2, 1))
    wuT_t = np.ascontiguousarray(wq_up.reshape(NHC, 128, NKC, 128).transpose(0, 3, 2, 1))
    wdT = np.ascontiguousarray(wq_down.T)
    sg_pp = np.ascontiguousarray(s_gate.reshape(NHC, 128).T)
    su_pp = np.ascontiguousarray(s_up.reshape(NHC, 128).T)
    sdb = np.ascontiguousarray(np.broadcast_to(s_down, (128, D)))
    identf = np.eye(128, dtype=np.float32)
    identb = np.eye(128, dtype=ml_dtypes.bfloat16)
    ones_row = np.ones((1, 512), dtype=ml_dtypes.bfloat16)
    n128_row = np.full((1, 512), -128.0, dtype=ml_dtypes.bfloat16)
    ones_colb = np.ones((128, 1), dtype=ml_dtypes.bfloat16)

    slopes = _slopes16()
    lam = np.exp(-slopes)
    pos = np.arange(128, dtype=np.float64)
    diff = pos[:, None] - pos[None, :]

    in_maps = []
    for r in range(N_CORES):
        x_my = np.ascontiguousarray(x[:, r * 256:(r + 1) * 256, :].reshape(TOK, D))
        h0, h1 = 2 * r, 2 * r + 1
        rows = np.concatenate([np.arange(h0 * 128, (h1 + 1) * 128),
                               2048 + np.arange(h0 * 128, (h1 + 1) * 128),
                               4096 + np.arange(h0 * 128, (h1 + 1) * 128)])
        wqkvT = np.ascontiguousarray(w_qkv[rows, :].T)
        bq = b_qkv[rows]
        bqk_ = np.stack([bq[0:128], bq[128:256], bq[256:384], bq[384:512]], axis=1).astype(np.float32)
        bqrh, bqrl = _hi_lo(bq[0:512])
        bvh, bvl = _hi_lo(bq[512:768])
        attc = np.zeros((128, 6 * 128), dtype=np.float32)
        lamc = np.zeros((128, 2), dtype=np.float32)
        dkpp = np.zeros((128, 2), dtype=np.float32)
        for hl, hh in enumerate((h0, h1)):
            lh = lam[hh]
            attc[:, hl * 128:(hl + 1) * 128] = (lh ** (pos + 1.0))[None, :]
            attc[:, 256 + hl * 128:256 + (hl + 1) * 128] = (lh ** (127.0 - pos))[None, :]
            dmT = np.where(diff >= 0, lh ** np.maximum(diff, 0.0), 0.0).T
            attc[:, 512 + hl * 128:512 + (hl + 1) * 128] = dmT.astype(np.float32)
            lamc[:, hl] = lh ** 128.0
            dkpp[:, hl] = (lh ** (127.0 - pos)).astype(np.float32)
        in_maps.append(dict(
            x_my=x_my, wqkvT=wqkvT, bqk=bqk_,
            bq_rh=bqrh.reshape(1, 512), bq_rl=bqrl.reshape(1, 512),
            bv_h=bvh.reshape(1, 256), bv_l=bvl.reshape(1, 256),
            woutT=woutT, bout_h=bo_h.reshape(1, D), bout_l=bo_l.reshape(1, D),
            wgT_t=wgT_t, wuT_t=wuT_t, wdT=wdT,
            sg_pp=sg_pp, su_pp=su_pp, sdb=sdb,
            attc=attc, lamc=lamc, dk_pp=dkpp,
            identf=identf, identb=identb,
            ones_row=ones_row, n128_row=n128_row, ones_colb=ones_colb,
        ))

    if not _NC_CACHE:
        _NC_CACHE.append(build_nc())
    nc = _NC_CACHE[0]
    res = run_bass_kernel_spmd(nc, in_maps, core_ids=list(range(N_CORES)))
    out = np.empty((B, N, D), dtype=np.float32)
    for r in range(N_CORES):
        out[:, r * 256:(r + 1) * 256, :] = res.results[r]["out_my"].reshape(B, 256, D)
    return out
